# version 34
# baseline (speedup 1.0000x reference)
import sys
if "/opt/trn_rl_repo" not in sys.path:
    sys.path.insert(0, "/opt/trn_rl_repo")
from contextlib import ExitStack

import numpy as np
import concourse.bass as bass
import concourse.mybir as mybir
from concourse.bass_utils import run_bass_kernel_spmd

F32 = mybir.dt.float32
BF16 = mybir.dt.bfloat16
AF = mybir.ActivationFunctionType
ALU = mybir.AluOpType

D = 1024
INNER = 2048
HID = 4096
DT = D // 128
IT = INNER // 128
HT = HID // 128
SUB = 512
EPS = 1e-5

ENG_SEMS = ["tE", "sE", "vE", "pE"]


class Build:
    def __init__(self, nc, prog):
        self.nc = nc
        self.prog = prog
        self.cnt = {}
        self.sems = {}
        for s in ENG_SEMS:
            self._mk(s)

    def _mk(self, name):
        self.sems[name] = self.prog.enter_context(self.nc.semaphore(name))
        self.cnt[name] = 0

    def stream(self, name):
        if name not in self.sems:
            self._mk(name)
        return name


class Phase:
    def __init__(self, B: Build):
        self.B = B
        self.streams = {e: [] for e in ["sync", "gpsimd", "tensor", "scalar", "vector"]}
        self.dma_streams = set()

    def wait(self, eng, sem, cnt):
        if cnt is None or cnt <= 0:
            return
        h = self.B.sems[sem]
        self.streams[eng].append(lambda e, h=h, cnt=cnt: e.wait_ge(h, cnt))

    def op(self, eng, fn, inc=None, inc_n=1):
        if inc is None:
            self.streams[eng].append(fn)
            return None
        self.B.cnt[inc] += inc_n
        h = self.B.sems[inc]
        def wrapped(e, fn=fn, h=h, n=inc_n):
            fn(e).then_inc(h, n)
        self.streams[eng].append(wrapped)
        return self.B.cnt[inc]

    def dma(self, eng, dst, src, stream, gate=True):
        s = self.B.stream(stream)
        self.dma_streams.add(s)
        if gate and self.B.cnt[s] > 0:
            self.wait(eng, s, self.B.cnt[s])
        return self.op(eng, lambda e, d=dst, sr=src: e.dma_start(d, sr), inc=s, inc_n=16)

    def run(self):
        for s in sorted(self.dma_streams):
            self.wait("sync", s, self.B.cnt[s])
        with self.B.nc.Block() as blk:
            for eng in ["sync", "gpsimd", "tensor", "scalar", "vector"]:
                ops = self.streams[eng]
                if not ops:
                    continue
                def body(e, ops=ops):
                    for f in ops:
                        f(e)
                getattr(blk, eng)(body)


def build_nc(n_b=4, n_c=4):
    K = n_c * SUB
    NSUB = n_b * n_c
    nc = bass.Bass("TRN2", target_bir_lowering=False)

    d_x = nc.dram_tensor("xT", [D, n_b, K], F32, kind="ExternalInput")
    d_wu = nc.dram_tensor("w_in_uT", [D, INNER], F32, kind="ExternalInput")
    d_wz = nc.dram_tensor("w_in_zT", [D, INNER], F32, kind="ExternalInput")
    d_wdt = nc.dram_tensor("w_dtT", [INNER, INNER], F32, kind="ExternalInput")
    d_wout = nc.dram_tensor("w_outT", [INNER, D], F32, kind="ExternalInput")
    d_w1 = [nc.dram_tensor(f"w1_{e}", [D, HID], F32, kind="ExternalInput") for e in range(2)]
    d_w2 = [nc.dram_tensor(f"w2_{e}", [HID, D], F32, kind="ExternalInput") for e in range(2)]
    d_sw1 = nc.dram_tensor("sw1", [D, HID], F32, kind="ExternalInput")
    d_sw2 = nc.dram_tensor("sw2", [HID, D], F32, kind="ExternalInput")
    d_wn = nc.dram_tensor("wn_c", [128, DT], F32, kind="ExternalInput")
    d_bdt = nc.dram_tensor("bdt_c", [128, IT], F32, kind="ExternalInput")
    d_b1 = [nc.dram_tensor(f"b1_{e}c", [128, HT], F32, kind="ExternalInput") for e in range(2)]
    d_sb1 = nc.dram_tensor("sb1_c", [128, HT], F32, kind="ExternalInput")
    d_b2 = [nc.dram_tensor(f"b2_{e}c", [128, DT], F32, kind="ExternalInput") for e in range(2)]
    d_sb2 = nc.dram_tensor("sb2_c", [128, DT], F32, kind="ExternalInput")
    d_g0 = nc.dram_tensor("g0_c", [128, DT], F32, kind="ExternalInput")
    d_g1 = nc.dram_tensor("g1_c", [128, DT], F32, kind="ExternalInput")
    d_out = nc.dram_tensor("outT", [D, n_b, K], F32, kind="ExternalOutput")

    d_usp = nc.dram_tensor("u_sp", [NSUB, 128, IT * SUB], BF16)
    d_zsp = nc.dram_tensor("z_sp", [NSUB, 128, IT * SUB], BF16)
    d_xlnsp = nc.dram_tensor("xln_sp", [NSUB, 128, DT * SUB], F32)
    d_szsp = nc.dram_tensor("sz_sp", [NSUB, 128, IT * SUB], BF16)
    d_xmsp = nc.dram_tensor("xm_sp", [NSUB, 128, DT * SUB], F32)
    d_xfsp = nc.dram_tensor("xf_sp", [NSUB, 128, DT * SUB], BF16)
    d_msp = nc.dram_tensor("m_sp", [NSUB, 1, SUB], BF16)
    d_acc0 = nc.dram_tensor("acc0_sp", [NSUB, 128, DT * SUB], F32)
    d_acc1 = nc.dram_tensor("acc1_sp", [NSUB, 128, DT * SUB], F32)
    d_wdt_b = nc.dram_tensor("wdt_b", [INNER, INNER], BF16)
    d_wout_b = nc.dram_tensor("wout_b", [INNER, D], BF16)
    d_w1_b = [nc.dram_tensor(f"w1b_{e}", [D, HID], BF16) for e in range(2)]
    d_w2_b = [nc.dram_tensor(f"w2b_{e}", [HID, D], BF16) for e in range(2)]
    d_sw1_b = nc.dram_tensor("sw1_b", [D, HID], BF16)
    d_sw2_b = nc.dram_tensor("sw2_b", [HID, D], BF16)
    cache_ready = {}
    late_cache = []

    prog = ExitStack()
    B = Build(nc, prog)

    c_wn = prog.enter_context(nc.sbuf_tensor("c_wn", [128, DT], F32))
    c_bdt = prog.enter_context(nc.sbuf_tensor("c_bdt", [128, IT], F32))
    c_nbdt = prog.enter_context(nc.sbuf_tensor("c_nbdt", [128, IT], F32))
    c_b1 = [prog.enter_context(nc.sbuf_tensor(f"c_b1_{e}", [128, HT], F32)) for e in range(2)]
    c_sb1 = prog.enter_context(nc.sbuf_tensor("c_sb1", [128, HT], F32))
    c_b2 = [prog.enter_context(nc.sbuf_tensor(f"c_b2_{e}", [128, DT], F32)) for e in range(2)]
    c_sb2 = prog.enter_context(nc.sbuf_tensor("c_sb2", [128, DT], F32))
    c_g0 = prog.enter_context(nc.sbuf_tensor("c_g0", [128, DT], F32))
    c_g1 = prog.enter_context(nc.sbuf_tensor("c_g1", [128, DT], F32))
    c_dwf = prog.enter_context(nc.sbuf_tensor("c_dwf", [128, DT], F32))
    c_dwhi = prog.enter_context(nc.sbuf_tensor("c_dwhi", [128, DT], BF16))
    c_dwhi32 = prog.enter_context(nc.sbuf_tensor("c_dwhi32", [128, DT], F32))
    c_dwlo = prog.enter_context(nc.sbuf_tensor("c_dwlo", [128, DT], BF16))
    c_dwsum = prog.enter_context(nc.sbuf_tensor("c_dwsum", [128, 1], F32))
    c_dwsumb = prog.enter_context(nc.sbuf_tensor("c_dwsumb", [128, 1], BF16))
    r_sdw = prog.enter_context(nc.sbuf_tensor("r_sdw", [1, 1], F32))
    ones_col = prog.enter_context(nc.sbuf_tensor("ones_col", [128, 1], BF16))
    ones_colf = prog.enter_context(nc.sbuf_tensor("ones_colf", [128, 1], F32))
    ones_rowf = prog.enter_context(nc.sbuf_tensor("ones_rowf", [1, 128], F32))
    ones_rowb = prog.enter_context(nc.sbuf_tensor("ones_rowb", [1, 128], BF16))
    c_eps = prog.enter_context(nc.sbuf_tensor("c_eps", [1, 1], F32))
    c_eps2 = prog.enter_context(nc.sbuf_tensor("c_eps2", [1, 1], F32))
    carry = prog.enter_context(nc.sbuf_tensor("carry", [128, n_b * IT], F32))

    ph = Phase(B)
    for dst, src in [(c_wn, d_wn), (c_bdt, d_bdt), (c_b1[0], d_b1[0]), (c_b1[1], d_b1[1]),
                     (c_sb1, d_sb1), (c_b2[0], d_b2[0]), (c_b2[1], d_b2[1]), (c_sb2, d_sb2),
                     (c_g0, d_g0), (c_g1, d_g1)]:
        ph.dma("sync", dst[:, :], src.ap(), "CL", gate=False)
    nconst = B.cnt["CL"]
    ph.op("gpsimd", lambda e: e.memset(ones_col[:, :], 1.0))
    ph.op("gpsimd", lambda e: e.memset(ones_colf[:, :], 1.0))
    ph.op("gpsimd", lambda e: e.memset(ones_rowf[:, :], 1.0))
    ph.op("gpsimd", lambda e: e.memset(ones_rowb[:, :], 1.0))
    ph.op("gpsimd", lambda e: e.memset(c_eps[:, :], EPS))
    ph.op("gpsimd", lambda e: e.memset(c_eps2[:, :], EPS * EPS))
    ph.op("gpsimd", lambda e: e.memset(carry[:, :], 0.0))
    ph.wait("vector", "CL", nconst)
    ph.op("vector", lambda e: e.tensor_scalar_mul(c_nbdt[:, :], c_bdt[:, :], -1.0))
    v1 = ph.op("vector", lambda e: e.tensor_sub(c_dwf[:, :], c_g0[:, :], c_g1[:, :]), inc="vE")
    ph.wait("scalar", "vE", v1)
    s0 = ph.op("scalar", lambda e: e.copy(c_dwhi[:, :], c_dwf[:, :]), inc="sE")
    ph.wait("scalar", "sE", s0)
    s1 = ph.op("scalar", lambda e: e.copy(c_dwhi32[:, :], c_dwhi[:, :]), inc="sE")
    ph.wait("vector", "sE", s1)
    v2 = ph.op("vector", lambda e: e.tensor_sub(c_dwlo[:, :], c_dwf[:, :], c_dwhi32[:, :]), inc="vE")
    ph.wait("vector", "vE", v2)
    v3 = ph.op("vector", lambda e: e.reduce_sum(c_dwsum[:, :], c_dwf[:, :], axis=mybir.AxisListType.X), inc="vE")
    ph.wait("scalar", "vE", v3)
    ph.op("scalar", lambda e: e.copy(c_dwsumb[:, :], c_dwsum[:, :]), inc="sE")
    ph.run()

    with nc.psum_tensor("p_sdw", [1, 1], F32) as p_sdw:
        ph = Phase(B)
        t1 = ph.op("tensor", lambda e: e.matmul(p_sdw[:, :], ones_col[:, :], c_dwsumb[:, :], start=True, stop=True), inc="tE")
        ph.wait("scalar", "tE", t1)
        ph.op("scalar", lambda e: e.copy(r_sdw[:, :], p_sdw[:, :]), inc="sE")
        ph.run()

    with ExitStack() as st:
        wu = st.enter_context(nc.sbuf_tensor("wu", [128, DT * INNER], BF16))
        wz = st.enter_context(nc.sbuf_tensor("wz", [128, DT * INNER], BF16))
        xs = [st.enter_context(nc.sbuf_tensor(f"xs{i}", [128, DT * SUB], F32)) for i in range(2)]
        xbf = st.enter_context(nc.sbuf_tensor("xbf", [128, DT * SUB], BF16))
        x2bf = st.enter_context(nc.sbuf_tensor("x2bf", [128, DT * SUB], BF16))
        ln1 = st.enter_context(nc.sbuf_tensor("ln1", [128, DT * SUB], F32))
        hb = [st.enter_context(nc.sbuf_tensor(f"hb{i}", [128, DT * SUB], BF16)) for i in range(2)]
        ub = st.enter_context(nc.sbuf_tensor("ub", [128, IT * SUB], BF16))
        zb = st.enter_context(nc.sbuf_tensor("zb", [128, 8 * SUB], BF16))
        xlnb = st.enter_context(nc.sbuf_tensor("xlnb", [128, DT * SUB], F32))
        rows = {nm: st.enter_context(nc.sbuf_tensor(f"ra_{nm}", [1, SUB], F32))
                for nm in ["m", "msq", "t0", "v", "sd1", "sd12"]}
        for nm in ["r1", "r12"]:
            rows[nm] = st.enter_context(nc.sbuf_tensor(f"ra_{nm}", [1, SUB], F32))
        for nm in ["r1b", "mr1", "rr"]:
            rows[nm] = st.enter_context(nc.sbuf_tensor(f"ra_{nm}", [1, SUB], BF16))
        pstat = st.enter_context(nc.psum_tensor("pstat", [128, SUB], F32))
        pbc = [st.enter_context(nc.psum_tensor(f"pbc{i}", [128, SUB], F32)) for i in range(3)]
        pA = [st.enter_context(nc.psum_tensor(f"pA{i}", [128, SUB], F32)) for i in range(4)]

        ph = Phase(B)
        qw_base = B.cnt.get("qW", 0)

        x_loaded, x_loaded2, cast_done, sum_done, sumsq_done = {}, {}, {}, {}, {}
        rows_done, bc_done, h_done, xln_done, xln_stored = {}, {}, {}, {}, {}
        evac_cnt, grp_cnt, bank_hist = {}, {}, {}
        ustoreA, ustoreB, zstoreA, zstoreB = {}, {}, {}, {}

        def x_src(i):
            c, b = divmod(i, n_b)
            return bass.AP(d_x, b * K + c * SUB, [[n_b * K, 128], [128 * n_b * K, DT], [1, SUB]])

        def slab_dst(buf, nt=DT):
            return bass.AP(buf, 0, [[nt * SUB, 128], [SUB, nt], [1, SUB]])

        wq_cnt = {}

        def em_weights():
            n = 0
            for wt, dr in [(wu, d_wu), (wz, d_wz)]:
                for hh in range(2):
                    srcap = bass.AP(dr, hh * 4 * 128 * INNER, [[INNER, 128], [128 * INNER, 4], [1, INNER]])
                    dstap = bass.AP(wt, hh * 4 * INNER, [[DT * INNER, 128], [INNER, 4], [1, INNER]])
                    wq_cnt[n] = ph.dma("gpsimd", dstap, srcap, "qW0")
                    n += 1

        def cache_chunks(pairs):
            out = []
            for nm, dsrc, ddst, r, cN in pairs:
                nch = max(1, (r * cN * 4) // (2 << 20))
                rch = r // nch
                for hh in range(nch):
                    sap = bass.AP(dsrc, hh * rch * cN, [[cN, rch], [1, cN]])
                    dap = bass.AP(ddst, hh * rch * cN, [[cN, rch], [1, cN]])
                    out.append((nm, dap, sap))
            return out

        chunks = cache_chunks([
            ("wdt", d_wdt, d_wdt_b, INNER, INNER),
            ("wout", d_wout, d_wout_b, INNER, D),
            ("w1_0", d_w1[0], d_w1_b[0], D, HID),
            ("w2_0", d_w2[0], d_w2_b[0], HID, D)])
        late_cache.extend(cache_chunks([
            ("w1_1", d_w1[1], d_w1_b[1], D, HID),
            ("w2_1", d_w2[1], d_w2_b[1], HID, D),
            ("sw1", d_sw1, d_sw1_b, D, HID),
            ("sw2", d_sw2, d_sw2_b, HID, D)]))

        def em_cache_some(k, gate_cnt=None):
            if gate_cnt is not None and chunks:
                ph.wait("gpsimd", "tE", gate_cnt)
            for _ in range(k):
                if not chunks:
                    return
                nm, dap, sap = chunks.pop(0)
                cache_ready[nm] = ph.dma("gpsimd", dap, sap, "qWC")

        def em_xload(i, eng="scalar"):
            if i >= NSUB:
                return
            if i >= 2:
                ph.wait(eng, "vE", xln_done[i - 2])
                x_loaded[i] = ph.dma(eng, slab_dst(xs[i % 2]), x_src(i), f"XL{i % 2}")
                return
            c, b = divmod(i, n_b)
            for h, (heng, hq) in enumerate([(eng, f"XL{i % 2}"), ("sync", f"XS{i % 2}")]):
                src = bass.AP(d_x, b * K + c * SUB + h * 4 * 128 * n_b * K,
                              [[n_b * K, 128], [128 * n_b * K, 4], [1, SUB]])
                dst = bass.AP(xs[i % 2], h * 4 * SUB, [[DT * SUB, 128], [SUB, 4], [1, SUB]])
                cv = ph.dma(heng, dst, src, hq)
                (x_loaded if h == 0 else x_loaded2)[i] = cv

        def em_casts(i):
            if i >= NSUB:
                return
            ph.wait("vector", f"XL{i % 2}", x_loaded[i])
            if i < 2:
                ph.wait("vector", f"XS{i % 2}", x_loaded2[i])
            if i >= 1:
                ph.wait("vector", "tE", sumsq_done[i - 1])
                ph.wait("vector", "tE", sum_done[i - 1])
            ph.op("vector", lambda e, x=xs[i % 2]: e.tensor_mul(x2bf[:, :], x[:, :], x[:, :]))
            cast_done[i] = ph.op("vector", lambda e, x=xs[i % 2]: e.tensor_copy(xbf[:, :], x[:, :]), inc="vE")

        def em_stats(i):
            ph.wait("tensor", "vE", cast_done[i])
            if i >= 1:
                ph.wait("tensor", "vE", rows_done[i - 1])
            for t in range(DT):
                sum_done[i] = ph.op("tensor", lambda e, t=t: e.matmul(
                    pstat[0:1, :], ones_col[:, :], xbf[:, t * SUB:(t + 1) * SUB],
                    start=(t == 0), stop=(t == DT - 1)), inc="tE" if t == DT - 1 else None)
            for t in range(DT):
                sumsq_done[i] = ph.op("tensor", lambda e, t=t: e.matmul(
                    pstat[32:33, :], ones_col[:, :], x2bf[:, t * SUB:(t + 1) * SUB],
                    start=(t == 0), stop=(t == DT - 1)), inc="tE" if t == DT - 1 else None)

        def em_rows_a(i):
            ph.wait("vector", "tE", sumsq_done[i])
            ph.op("vector", lambda e: e.tensor_single_scalar(rows["m"][:, :], pstat[0:1, :], 1.0 / D, ALU.mult))
            c1 = ph.op("vector", lambda e: e.tensor_single_scalar(rows["msq"][:, :], pstat[32:33, :], 1.0 / D, ALU.mult), inc="vE")
            ph.wait("vector", "vE", c1)
            c2 = ph.op("vector", lambda e: e.tensor_mul(rows["t0"][:, :], rows["m"][:, :], rows["m"][:, :]), inc="vE")
            ph.wait("vector", "vE", c2)
            rows_done[("v", i)] = ph.op("vector", lambda e: e.tensor_sub(rows["v"][:, :], rows["msq"][:, :], rows["t0"][:, :]), inc="vE")

        def em_rows_b(i):
            ph.wait("scalar", "vE", rows_done[("v", i)])
            ph.op("scalar", lambda e: e.activation(rows["sd1"][:, :], rows["v"][:, :], AF.Ln, bias=c_eps[:, :], scale=1.0))
            l2 = ph.op("scalar", lambda e: e.activation(rows["sd12"][:, :], rows["v"][:, :], AF.Ln, bias=c_eps2[:, :], scale=1.0 + EPS), inc="sE")
            ph.wait("scalar", "sE", l2)
            ph.op("scalar", lambda e: e.activation(rows["r1"][:, :], rows["sd1"][:, :], AF.Exp, scale=-0.5))
            rows_done[("sq", i)] = ph.op("scalar", lambda e: e.activation(rows["r12"][:, :], rows["sd12"][:, :], AF.Exp, scale=-0.5), inc="sE")

        def em_rows_c(i):
            ph.wait("vector", "sE", rows_done[("sq", i)])
            ph.op("vector", lambda e: e.tensor_copy(rows["r1b"][:, :], rows["r1"][:, :]))
            ph.op("vector", lambda e: e.tensor_copy(rows["mr1"][:, :], rows["m"][:, :]))
            rows_done[i] = ph.op("vector", lambda e: e.tensor_copy(rows["rr"][:, :], rows["r12"][:, :]), inc="vE")

        def em_bc(i):
            ph.wait("tensor", "vE", rows_done[i])
            if i >= 1:
                ph.wait("tensor", "vE", h_done[i - 1])
            ph.op("tensor", lambda e: e.matmul(pbc[0][:, :], ones_rowb[:, :], rows["mr1"][:, :], start=True, stop=True))
            ph.op("tensor", lambda e: e.matmul(pbc[1][:, :], ones_rowb[:, :], rows["r1b"][:, :], start=True, stop=True))
            bc_done[i] = ph.op("tensor", lambda e: e.matmul(pbc[2][:, :], ones_rowb[:, :], rows["rr"][:, :], start=True, stop=True), inc="tE")

        def em_lnh_h(i):
            xcur, hcur = xs[i % 2], hb[i % 2]
            ph.wait("vector", "tE", bc_done[i])
            p1 = None
            for t in range(DT):
                sl = slice(t * SUB, (t + 1) * SUB)
                p1 = ph.op("vector", lambda e, x=xcur, sl=sl: e.tensor_sub(ln1[:, sl], x[:, sl], pbc[0][:, :]),
                           inc="vE" if t == DT - 1 else None)
            ph.wait("vector", "vE", p1)
            if i >= 2:
                ph.wait("vector", "tE", grp_cnt[(i - 2, "z", IT - 1)])
            for t in range(DT):
                sl = slice(t * SUB, (t + 1) * SUB)
                h_done[i] = ph.op("vector", lambda e, h=hcur, sl=sl, t=t: e.scalar_tensor_tensor(
                    h[:, sl], ln1[:, sl], c_wn[:, t:t + 1], pbc[2][:, :], ALU.mult, ALU.mult),
                    inc="vE" if t == DT - 1 else None)

        def em_lnh_rest(i):
            xcur = xs[i % 2]
            ph.wait("vector", "vE", h_done[i])
            p2 = None
            for t in range(DT):
                sl = slice(t * SUB, (t + 1) * SUB)
                p2 = ph.op("vector", lambda e, sl=sl: e.tensor_mul(ln1[:, sl], ln1[:, sl], pbc[1][:, :]),
                           inc="vE" if t == DT - 1 else None)
            ph.wait("vector", "vE", p2)
            if i >= 1:
                ph.wait("vector", "XLNS", xln_stored[i - 1])
            for t in range(DT):
                sl = slice(t * SUB, (t + 1) * SUB)
                xln_done[i] = ph.op("vector", lambda e, x=xcur, sl=sl: e.tensor_add(xlnb[:, sl], x[:, sl], ln1[:, sl]),
                                    inc="vE" if t == DT - 1 else None)

        def em_uvz(i, tag, jlist):
            wt = wu if tag == "u" else wz
            dstbuf = ub if tag == "u" else zb
            hcur = hb[i % 2]
            ph.wait("tensor", "vE", h_done[i])
            for j in jlist:
                bank = (0 if tag == "u" else IT) + j
                bank = bank % 4
                pg = pA[bank]
                prev = bank_hist.get(bank)
                if prev is not None:
                    ph.wait("tensor", "sE", evac_cnt[prev])
                g = None
                for k in range(DT):
                    if i == 0:
                        n = (0 if tag == "u" else 2) + k // 4
                        ph.wait("tensor", "qW0", wq_cnt[n])
                    g = ph.op("tensor", lambda e, wt=wt, j=j, k=k, pg=pg, h=hcur: e.matmul(
                        pg[:, :], wt[:, k * INNER + j * 128: k * INNER + j * 128 + 128],
                        h[:, k * SUB:(k + 1) * SUB], start=(k == 0), stop=(k == DT - 1)),
                        inc="tE" if k == DT - 1 else None)
                grp_cnt[(i, tag, j)] = g
                ph.wait("scalar", "tE", g)
                if tag == "u":
                    if j == 0 and i >= 1:
                        ph.wait("scalar", "USA", ustoreA[i - 1])
                    if j == 8 and i >= 1:
                        ph.wait("scalar", "USB", ustoreB[i - 1])
                    col = j * SUB
                else:
                    if j == 0 and i >= 1:
                        ph.wait("scalar", "ZSB", zstoreB[i - 1])
                    if j == 8:
                        ph.wait("scalar", "ZSA", zstoreA[i])
                    col = (j % 8) * SUB
                evac_cnt[(i, tag, j)] = ph.op("scalar", lambda e, d=dstbuf, col=col, pg=pg: e.activation(
                    d[:, col:col + SUB], pg[:, :], AF.Silu), inc="sE")
                bank_hist[bank] = (i, tag, j)
                if tag == "z" and j == 7:
                    ph.wait("scalar", "sE", evac_cnt[(i, "z", 7)])
                    apz = bass.AP(d_zsp, i * 128 * IT * SUB, [[IT * SUB, 128], [1, 8 * SUB]])
                    zstoreA[i] = ph.dma("scalar", apz, zb[:, :], "ZSA")

        def em_store_uz(i, tag, half):
            if tag == "z" and half == "A":
                return
            jend = 7 if half == "A" else 15
            ph.wait("sync", "sE", evac_cnt[(i, tag, jend)])
            cofs = 0 if half == "A" else 8 * SUB
            if tag == "u":
                srcb = ub[:, cofs:cofs + 8 * SUB]
                ap = bass.AP(d_usp, i * 128 * IT * SUB + cofs, [[IT * SUB, 128], [1, 8 * SUB]])
            else:
                srcb = zb[:, :]
                ap = bass.AP(d_zsp, i * 128 * IT * SUB + cofs, [[IT * SUB, 128], [1, 8 * SUB]])
            cntv = ph.dma("sync", ap, srcb, f"{tag.upper()}S{half}")
            {("u", "A"): ustoreA, ("u", "B"): ustoreB,
             ("z", "B"): zstoreB}[(tag, half)][i] = cntv

        em_xload(0); em_xload(1)
        em_weights()
        em_casts(0)
        em_stats(0)
        em_rows_a(0); em_rows_b(0); em_rows_c(0)
        em_bc(0)
        em_lnh_h(0)
        em_casts(1)
        em_lnh_rest(0)
        ph.wait("sync", "vE", xln_done[0])
        xln_stored[0] = ph.dma("sync", d_xlnsp.ap()[0, :, :], xlnb[:, :], "XLNS")

        for w in range(NSUB):
            em_xload(w + 2)
            nxt = w + 1 < NSUB
            if nxt and w > 0:
                em_stats(w + 1)
                em_rows_a(w + 1)
            em_uvz(w, "u", range(0, 4))
            if nxt and w == 0:
                em_stats(1)
                em_rows_a(1)
            if nxt:
                em_rows_b(w + 1)
            em_uvz(w, "u", range(4, 8))
            em_store_uz(w, "u", "A")
            if nxt:
                em_rows_c(w + 1)
            em_casts(w + 2)
            em_uvz(w, "u", range(8, 12))
            em_uvz(w, "u", range(12, IT))
            if nxt:
                em_bc(w + 1)
            em_store_uz(w, "u", "B")
            if nxt:
                em_lnh_h(w + 1)
                em_lnh_rest(w + 1)
            em_uvz(w, "z", range(0, 8))
            em_store_uz(w, "z", "A")
            em_uvz(w, "z", range(8, IT))
            em_store_uz(w, "z", "B")
            if nxt:
                ph.wait("sync", "vE", xln_done[w + 1])
                xln_stored[w + 1] = ph.dma("sync", d_xlnsp.ap()[w + 1, :, :], xlnb[:, :], "XLNS")
            if w >= 2:
                em_cache_some(2, gate_cnt=grp_cnt[(w, "u", 7)])

        em_cache_some(len(chunks))
        ph.run()

    wout_ctx = nc.sbuf_tensor("wout", [128, IT * D], BF16)
    wout = wout_ctx.__enter__()
    wout_cnt = {}
    szin0_ctx = nc.sbuf_tensor("szin0", [128, IT * SUB], BF16)
    szin0 = szin0_ctx.__enter__()
    p1c_pre = {}
    with ExitStack() as st:
        wdt = st.enter_context(nc.sbuf_tensor("wdt", [128, IT * INNER], BF16))
        uin = [st.enter_context(nc.sbuf_tensor(f"uin{i}", [128, IT * SUB], BF16)) for i in range(2)]
        zin = [st.enter_context(nc.sbuf_tensor(f"zin{i}", [128, IT * SUB], BF16)) for i in range(2)]
        szb = st.enter_context(nc.sbuf_tensor("szb", [128, 8 * SUB], BF16))
        lamt = [st.enter_context(nc.sbuf_tensor(f"lamt{i}", [128, SUB], F32)) for i in range(2)]
        lampt = [st.enter_context(nc.sbuf_tensor(f"lampt{i}", [128, SUB], F32)) for i in range(2)]
        bbt = [st.enter_context(nc.sbuf_tensor(f"bbt{i}", [128, SUB], F32)) for i in range(2)]
        sst = [st.enter_context(nc.sbuf_tensor(f"sst{i}", [128, SUB], F32)) for i in range(2)]
        pA = [st.enter_context(nc.psum_tensor(f"pBA{i}", [128, SUB], F32)) for i in range(8)]

        ph = Phase(B)
        wq_cnt = {}
        ph.wait("gpsimd", "qWC", cache_ready["wdt"])
        for g in range(8):
            srcw = bass.AP(d_wdt_b, g * 2 * 128 * INNER, [[INNER, 128], [128 * INNER, 2], [1, INNER]])
            dstw = bass.AP(wdt, g * 2 * INNER, [[IT * INNER, 128], [INNER, 2], [1, INNER]])
            wq_cnt[g] = ph.dma("gpsimd", dstw, srcw, f"qW0{'b' * (g % 2)}")
        ph.wait("gpsimd", "qWC", cache_ready["wout"])
        for hh in range(2):
            srcap = bass.AP(d_wout_b, hh * 8 * 128 * D, [[D, 128], [128 * D, 8], [1, D]])
            dstap = bass.AP(wout, hh * 8 * D, [[IT * D, 128], [D, 8], [1, D]])
            wout_cnt[hh] = ph.dma("gpsimd", dstap, srcap, f"qW{4 + hh}")

        u_loaded, z_loaded, grp, evac = {}, {}, {}, {}
        b_done, scan_done, sz_done, szA_stored, szB_stored = {}, {}, {}, {}, {}

        half = IT * SUB // 2
        u_loaded[0] = ph.dma("sync", uin[0][:, 0:half], d_usp.ap()[0, :, 0:half], "UL0")
        u_half2 = ph.dma("scalar", uin[0][:, half:IT * SUB], d_usp.ap()[0, :, half:IT * SUB], "ULS")
        z_loaded[0] = ph.dma("sync", zin[0][:, :], d_zsp.ap()[0, :, :], "ZL0")
        if NSUB > 1:
            u_loaded[1] = ph.dma("sync", uin[1][:, :], d_usp.ap()[1, :, :], "UL1")
            z_loaded[1] = ph.dma("sync", zin[1][:, :], d_zsp.ap()[1, :, :], "ZL1")

        def em_lam_tail(i, j, pg, ucur, zcur, b):
            gj = i * IT + j
            g = grp[(i, j)]

            lj = j % 2
            ph.wait("scalar", "tE", g)
            if gj >= 2:
                ph.wait("scalar", "vE", scan_done[gj - 2])
                ph.wait("scalar", "pE", b_done[gj - 2])
            ph.op("scalar", lambda e, pg=pg, j=j, lj=lj: e.activation(
                lamt[lj][:, :], pg[:, :], AF.Sigmoid, bias=c_bdt[:, j:j + 1], scale=1.0))
            ev = ph.op("scalar", lambda e, pg=pg, j=j, lj=lj: e.activation(
                lampt[lj][:, :], pg[:, :], AF.Sigmoid, bias=c_nbdt[:, j:j + 1], scale=-1.0), inc="sE")
            evac[(i, j)] = ev

            ph.wait("gpsimd", "sE", ev)
            if gj >= 2:
                ph.wait("gpsimd", "vE", scan_done[gj - 2])
            b_done[gj] = ph.op("gpsimd", lambda e, lj=lj, u=ucur, j=j: e.tensor_tensor(
                bbt[lj][:, :], lampt[lj][:, :], u[:, j * SUB:(j + 1) * SUB], ALU.mult), inc="pE")

            ccol = b * IT + j
            ph.wait("vector", "pE", b_done[gj])
            if gj >= 2:
                ph.wait("vector", "vE", sz_done[gj - 2])
            sc = ph.op("vector", lambda e, lj=lj, ccol=ccol: e.tensor_tensor_scan(
                sst[lj][:, :], lamt[lj][:, :], bbt[lj][:, :], carry[:, ccol:ccol + 1],
                ALU.mult, ALU.add), inc="vE")
            scan_done[gj] = sc
            ph.wait("vector", "vE", sc)
            ph.op("vector", lambda e, lj=lj, ccol=ccol: e.tensor_copy(
                carry[:, ccol:ccol + 1], sst[lj][:, SUB - 1:SUB]), inc="vE")
            if j == 0 and i >= 1:
                ph.wait("vector", "SZSB", szB_stored[i - 1])
            if j == 8:
                ph.wait("vector", "SZSA", szA_stored[i])
            sz_done[gj] = ph.op("vector", lambda e, lj=lj, z=zcur, j=j: e.tensor_mul(
                szb[:, (j % 8) * SUB:(j % 8 + 1) * SUB], sst[lj][:, :], z[:, j * SUB:(j + 1) * SUB]), inc="vE")
            if j == 7:
                ph.wait("sync", "vE", sz_done[gj])
                apz = bass.AP(d_szsp, i * 128 * IT * SUB, [[IT * SUB, 128], [1, 8 * SUB]])
                szA_stored[i] = ph.dma("sync", apz, szb[:, :], "SZSA")

        for i in range(NSUB):
            c, b = divmod(i, n_b)
            ucur, zcur = uin[i % 2], zin[i % 2]
            ph.wait("tensor", f"UL{i % 2}", u_loaded[i])
            ph.wait("gpsimd", f"UL{i % 2}", u_loaded[i])
            if i == 0:
                ph.wait("tensor", "ULS", u_half2)
                ph.wait("gpsimd", "ULS", u_half2)
            ph.wait("vector", f"ZL{i % 2}", z_loaded[i])
            if i == 0:
                for g8 in range(8):
                    ph.wait("tensor", f"qW0{'b' * (g8 % 2)}", wq_cnt[g8])
                    for k in (2 * g8, 2 * g8 + 1):
                        for j in range(8):
                            gg = ph.op("tensor", lambda e, j=j, k=k, u=ucur: e.matmul(
                                pA[j][:, :], wdt[:, k * INNER + j * 128: k * INNER + j * 128 + 128],
                                u[:, k * SUB:(k + 1) * SUB], start=(k == 0), stop=(k == IT - 1)),
                                inc="tE" if k == IT - 1 else None)
                            if k == IT - 1:
                                grp[(0, j)] = gg
                for j in range(8):
                    em_lam_tail(0, j, pA[j], ucur, zcur, b)
                jstart = 8
            else:
                jstart = 0
            for j in range(jstart, IT):
                gj = i * IT + j
                pg = pA[j % 2]
                key = (i, j - 2) if j >= 2 else (i - 1, IT - 2 + j)
                if key in evac:
                    ph.wait("tensor", "sE", evac[key])
                g = None
                for k in range(IT):
                    g = ph.op("tensor", lambda e, j=j, k=k, pg=pg, u=ucur: e.matmul(
                        pg[:, :], wdt[:, k * INNER + j * 128: k * INNER + j * 128 + 128],
                        u[:, k * SUB:(k + 1) * SUB], start=(k == 0), stop=(k == IT - 1)),
                        inc="tE" if k == IT - 1 else None)
                grp[(i, j)] = g
                em_lam_tail(i, j, pg, ucur, zcur, b)

            ph.wait("sync", "vE", sz_done[i * IT + IT - 1])
            apz = bass.AP(d_szsp, i * 128 * IT * SUB + 8 * SUB, [[IT * SUB, 128], [1, 8 * SUB]])
            szB_stored[i] = ph.dma("sync", apz, szb[:, :], "SZSB")
            if i + 2 < NSUB:
                ph.wait("sync", "pE", b_done[i * IT + IT - 1])
                u_loaded[i + 2] = ph.dma("sync", uin[i % 2][:, :], d_usp.ap()[i + 2, :, :], f"UL{i % 2}")
                z_loaded[i + 2] = ph.dma("sync", zin[i % 2][:, :], d_zsp.ap()[i + 2, :, :], f"ZL{i % 2}")

        ph.wait("sync", "SZSB", szB_stored[0])
        p1c_pre["sz"] = ph.dma("sync", szin0[:, :], d_szsp.ap()[0, :, :], "SZL0")
        ph.run()

    with ExitStack() as st:
        szin = [szin0, st.enter_context(nc.sbuf_tensor("szin1", [128, IT * SUB], BF16))]
        xlnin = [st.enter_context(nc.sbuf_tensor(f"xlnin{i}", [128, DT * SUB], F32)) for i in range(2)]
        xmb = [st.enter_context(nc.sbuf_tensor(f"xmb{i}", [128, DT * SUB], F32)) for i in range(3)]
        xmbf = st.enter_context(nc.sbuf_tensor("xmbf", [128, DT * SUB], BF16))
        xm2bf = st.enter_context(nc.sbuf_tensor("xm2bf", [128, DT * SUB], BF16))
        xmlo = st.enter_context(nc.sbuf_tensor("xmlo", [128, DT * SUB], BF16))
        xfb = st.enter_context(nc.sbuf_tensor("xfb", [128, DT * SUB], BF16))
        t1b = st.enter_context(nc.sbuf_tensor("t1b", [128, DT * SUB], BF16))
        rows = {nm: st.enter_context(nc.sbuf_tensor(f"rc_{nm}", [1, SUB], F32))
                for nm in ["m", "msq", "t0", "v"]}
        rows["sd1"] = rows["t0"]
        rows2 = []
        for p in range(2):
            d = {nm: st.enter_context(nc.sbuf_tensor(f"rc2_{nm}{p}", [1, SUB], BF16))
                 for nm in ["r2b", "mr2", "mrow"]}
            rows2.append(d)
        for nm in ["pg", "r2", "thr"]:
            rows[nm] = st.enter_context(nc.sbuf_tensor(f"rc_{nm}", [1, SUB], F32))
        pstat = st.enter_context(nc.psum_tensor("pstatc", [128, SUB], F32))
        pgate = st.enter_context(nc.psum_tensor("pgate", [1, SUB], F32))
        pbc = [st.enter_context(nc.psum_tensor(f"pbcc{i}", [128, SUB], F32)) for i in range(4)]
        pA = [st.enter_context(nc.psum_tensor(f"pCA{i}", [128, SUB], F32)) for i in range(2)]

        ph = Phase(B)
        wq_cnt = wout_cnt

        sz_loaded, xln_loaded, grp, ev = {}, {}, {}, {}
        stats_done, gate_done, lo_done, rows_done = {}, {}, {}, {}
        bc_done, xf_done, mrow_done, pg_read, cast_done = {}, {}, {}, {}, {}
        xm_stored, xf_stored, m_stored, cast_tile = {}, {}, {}, {}

        def em_loads(i):
            if i >= NSUB:
                return
            if i >= 2:
                ph.wait("sync", "tE", grp[(i - 2, DT - 1)])
            sz_loaded[i] = ph.dma("sync", szin[i % 2][:, :], d_szsp.ap()[i, :, :], f"SZL{i % 2}")
            if i >= 2:
                ph.wait("sync", "vE", ev[(i - 2, DT - 1)])
            xln_loaded[i] = ph.dma("sync", xlnin[i % 2][:, :], d_xlnsp.ap()[i, :, :], f"XLL{i % 2}")

        def em_y(i, olist):
            szc, xlnc, xmc = szin[i % 2], xlnin[i % 2], xmb[i % 3]
            if olist[0] == 0:
                ph.wait("tensor", f"SZL{i % 2}", sz_loaded[i])
            for o in olist:
                pg = pA[o % 2]
                key = (i, o - 2) if o >= 2 else (i - 1, DT - 2 + o)
                if key in ev:
                    ph.wait("tensor", "vE", ev[key])
                g = None
                for k in range(IT):
                    if i == 0 and o == 0:
                        ph.wait("tensor", f"qW{4 + k // 8}", wq_cnt[k // 8])
                    g = ph.op("tensor", lambda e, o=o, k=k, pg=pg, s=szc: e.matmul(
                        pg[:, :], wout[:, k * D + o * 128: k * D + o * 128 + 128],
                        s[:, k * SUB:(k + 1) * SUB], start=(k == 0), stop=(k == IT - 1)),
                        inc="tE" if k == IT - 1 else None)
                grp[(i, o)] = g
                ph.wait("vector", "tE", g)
                if o == 0:
                    ph.wait("vector", f"XLL{i % 2}", xln_loaded[i])
                    if i >= 3:
                        ph.wait("vector", "XMS", xm_stored[i - 3])
                        ph.wait("vector", "sE", cast_done[i - 3])
                        ph.wait("vector", "vE", xf_done[i - 3])
                sl = slice(o * SUB, (o + 1) * SUB)
                ev[(i, o)] = ph.op("vector", lambda e, pg=pg, sl=sl, xl=xlnc, xm=xmc: e.tensor_add(
                    xm[:, sl], pg[:, :], xl[:, sl]), inc="vE")

        def em_casts(i):
            xmc = xmb[i % 3]
            for o in range(DT):
                sl = slice(o * SUB, (o + 1) * SUB)
                ph.wait("scalar", "vE", ev[(i, o)])
                if o == 0 and i >= 1:
                    ph.wait("scalar", "tE", gate_done[i - 1])
                ph.op("scalar", lambda e, sl=sl, xm=xmc: e.copy(xmbf[:, sl], xm[:, sl]))
                cd = ph.op("scalar", lambda e, sl=sl, xm=xmc: e.activation(xm2bf[:, sl], xm[:, sl], AF.Square), inc="sE")
                cast_tile[(i, o)] = cd
                ph.wait("gpsimd", "sE", cd)
                if o == 0 and i >= 1:
                    ph.wait("gpsimd", "tE", gate_done[i - 1])
                lo_done[i] = ph.op("gpsimd", lambda e, sl=sl, xm=xmc: e.tensor_sub(
                    xmlo[:, sl], xm[:, sl], xmbf[:, sl]), inc="pE")
            cast_done[i] = cast_tile[(i, DT - 1)]

        def em_statsgate(i):
            if i >= 1:
                ph.wait("tensor", "vE", rows_done[i - 1])
            for t in range(DT):
                ph.wait("tensor", "sE", cast_tile[(i, t)])
                ph.op("tensor", lambda e, t=t: e.matmul(
                    pstat[0:1, :], ones_col[:, :], xmbf[:, t * SUB:(t + 1) * SUB],
                    start=(t == 0), stop=(t == DT - 1)))
            for t in range(DT):
                stats_done[i] = ph.op("tensor", lambda e, t=t: e.matmul(
                    pstat[32:33, :], ones_col[:, :], xm2bf[:, t * SUB:(t + 1) * SUB],
                    start=(t == 0), stop=(t == DT - 1)), inc="tE" if t == DT - 1 else None)
            ph.wait("tensor", "pE", lo_done[i])
            if i >= 1:
                ph.wait("tensor", "sE", pg_read[i - 1])
            ngate = 0
            for lhs, rhs in [(c_dwhi, xmbf), (c_dwhi, xmlo), (c_dwlo, xmbf)]:
                for t in range(DT):
                    ngate += 1
                    gate_done[i] = ph.op("tensor", lambda e, lhs=lhs, rhs=rhs, t=t,
                                         stt=(ngate == 1), spp=(ngate == 24): e.matmul(
                        pgate[:, :], lhs[:, t:t + 1], rhs[:, t * SUB:(t + 1) * SUB],
                        start=stt, stop=spp), inc="tE" if ngate == 24 else None)

        def em_rows(i):
            rp = rows2[i % 2]
            ph.wait("vector", "tE", stats_done[i])
            ph.op("vector", lambda e: e.tensor_single_scalar(rows["m"][:, :], pstat[0:1, :], 1.0 / D, ALU.mult))
            c1 = ph.op("vector", lambda e: e.tensor_single_scalar(rows["msq"][:, :], pstat[32:33, :], 1.0 / D, ALU.mult), inc="vE")
            ph.wait("vector", "vE", c1)
            c2 = ph.op("vector", lambda e: e.tensor_mul(rows["t0"][:, :], rows["m"][:, :], rows["m"][:, :]), inc="vE")
            ph.wait("vector", "vE", c2)
            v_v = ph.op("vector", lambda e: e.tensor_sub(rows["v"][:, :], rows["msq"][:, :], rows["t0"][:, :]), inc="vE")
            ph.wait("scalar", "vE", v_v)
            s_l = ph.op("scalar", lambda e: e.activation(rows["sd1"][:, :], rows["v"][:, :], AF.Ln, bias=c_eps[:, :], scale=1.0), inc="sE")
            ph.wait("scalar", "sE", s_l)
            s_sq = ph.op("scalar", lambda e: e.activation(rows["r2"][:, :], rows["sd1"][:, :], AF.Exp, scale=-0.5), inc="sE")
            ph.wait("vector", "sE", s_sq)
            ph.op("vector", lambda e, rp=rp: e.tensor_copy(rp["r2b"][:, :], rows["r2"][:, :]))
            ph.op("vector", lambda e, rp=rp: e.tensor_mul(rp["mr2"][:, :], rows["m"][:, :], rows["r2"][:, :]))
            thr_d = ph.op("vector", lambda e: e.tensor_scalar(rows["thr"][:, :], rows["m"][:, :], r_sdw[:, :], None, ALU.mult), inc="vE")
            rows_done[i] = thr_d
            ph.wait("scalar", "tE", gate_done[i])
            if i >= 1:
                ph.wait("scalar", "vE", mrow_done[i - 1])
            pg_read[i] = ph.op("scalar", lambda e: e.copy(rows["pg"][:, :], pgate[:, :]), inc="sE")
            ph.wait("vector", "sE", pg_read[i])
            ph.wait("vector", "vE", thr_d)
            if i >= 2:
                ph.wait("vector", "MS", m_stored[i - 2])
            mrow_done[i] = ph.op("vector", lambda e, rp=rp: e.tensor_tensor(
                rp["mrow"][:, :], rows["pg"][:, :], rows["thr"][:, :], ALU.is_ge), inc="vE")

        def em_bc(i):
            rp = rows2[i % 2]
            pb = pbc[2 * (i % 2):2 * (i % 2) + 2]
            ph.wait("tensor", "vE", rows_done[i])
            if i >= 2:
                ph.wait("tensor", "vE", xf_done[i - 2])
            ph.op("tensor", lambda e, rp=rp, pb=pb: e.matmul(pb[0][:, :], ones_rowb[:, :], rp["r2b"][:, :], start=True, stop=True))
            bc_done[i] = ph.op("tensor", lambda e, rp=rp, pb=pb: e.matmul(pb[1][:, :], ones_rowb[:, :], rp["mr2"][:, :], start=True, stop=True), inc="tE")

        def em_xf_t1b(i):
            xmc = xmb[i % 3]
            pb = pbc[2 * (i % 2):2 * (i % 2) + 2]
            ph.wait("vector", "tE", bc_done[i])
            if i >= 1:
                ph.wait("vector", "vE", xf_done[i - 1])
            for t in range(DT):
                sl = slice(t * SUB, (t + 1) * SUB)
                xf_done[("t1b", i)] = ph.op("vector", lambda e, sl=sl, xm=xmc, pb=pb: e.tensor_mul(
                    t1b[:, sl], xm[:, sl], pb[0][:, :]), inc="vE" if t == DT - 1 else None)

        def em_xf_sub(i):
            pb = pbc[2 * (i % 2):2 * (i % 2) + 2]
            ph.wait("vector", "vE", xf_done[("t1b", i)])
            if i >= 1:
                ph.wait("vector", "XFS", xf_stored[i - 1])
            for t in range(DT):
                sl = slice(t * SUB, (t + 1) * SUB)
                xf_done[i] = ph.op("vector", lambda e, sl=sl, pb=pb: e.tensor_sub(xfb[:, sl], t1b[:, sl], pb[1][:, :]),
                                   inc="vE" if t == DT - 1 else None)

        def em_stores(i):
            rp = rows2[i % 2]
            xmc = xmb[i % 3]
            ph.wait("sync", "vE", xf_done[i])
            ph.wait("sync", "vE", mrow_done[i])
            xm_stored[i] = ph.dma("sync", d_xmsp.ap()[i, :, :], xmc[:, :], "XMS")
            xf_stored[i] = ph.dma("sync", d_xfsp.ap()[i, :, :], xfb[:, :], "XFS")
            m_stored[i] = ph.dma("sync", d_msp.ap()[i, :, :], rp["mrow"][:, :], "MS")

        sz_loaded[0] = p1c_pre["sz"]
        xln_loaded[0] = ph.dma("sync", xlnin[0][:, :], d_xlnsp.ap()[0, :, :], "XLL0")
        em_loads(1)
        for w in range(NSUB + 2):
            a, b, c = w, w - 1, w - 2
            if 0 <= c < NSUB:
                em_bc(c)
            if 0 <= b < NSUB:
                em_statsgate(b)
                em_rows(b)
            if a < NSUB:
                em_y(a, list(range(0, 4)))
            if 0 <= c < NSUB:
                em_xf_t1b(c)
            if a < NSUB:
                em_y(a, list(range(4, DT)))
            if 0 <= c < NSUB:
                em_xf_sub(c)
            if a < NSUB:
                em_casts(a)
            if 0 <= c < NSUB:
                em_stores(c)
            if a < NSUB:
                em_loads(a + 2)

        ph.run()

    szin0_ctx.__exit__(None, None, None)
    wout_ctx.__exit__(None, None, None)

    p2w = {}

    def ffn_branch(which, next_key=None, next_dW1=None):
        with ExitStack() as st:
            if "wA" not in p2w:
                p2w["wA"] = prog.enter_context(nc.sbuf_tensor("wA_p2", [128, DT * HID], BF16))
                p2w["wB"] = prog.enter_context(nc.sbuf_tensor("wB_p2", [128, HT * D], BF16))
                p2w["xf0"] = prog.enter_context(nc.sbuf_tensor("xf0_p2", [128, DT * SUB], BF16))
                p2w["mr0"] = prog.enter_context(nc.sbuf_tensor("mr0_p2", [1, SUB], BF16))
            wA, wB = p2w["wA"], p2w["wB"]
            hd = st.enter_context(nc.sbuf_tensor(f"hd_{which}", [128, HT * SUB], BF16))
            xfin = [p2w["xf0"],
                    st.enter_context(nc.sbuf_tensor(f"xfin_{which}1", [128, DT * SUB], BF16))]
            xr = st.enter_context(nc.sbuf_tensor(f"xr_{which}", [128, DT * SUB], BF16)) if which != "sh" else None
            outst = [st.enter_context(nc.sbuf_tensor(f"outst_{which}{i}", [128, SUB], F32)) for i in range(2)]
            accin = [st.enter_context(nc.sbuf_tensor(f"acc_{which}{i}", [128, SUB], F32)) for i in range(2)] if which != "e0" else None
            xmin = [st.enter_context(nc.sbuf_tensor(f"xmin_{which}{i}", [128, SUB], F32)) for i in range(2)] if which == "sh" else None
            axm = [st.enter_context(nc.sbuf_tensor(f"axm_{which}{i}", [128, SUB], F32)) for i in range(2)] if which == "sh" else None
            mr = [p2w["mr0"],
                  st.enter_context(nc.sbuf_tensor(f"mr_{which}1", [1, SUB], BF16))] if which != "sh" else None
            thrr = st.enter_context(nc.sbuf_tensor(f"thr_{which}", [1, SUB], BF16)) if which == "e1" else None
            pA = [st.enter_context(nc.psum_tensor(f"pm1_{which}{i}", [128, SUB], F32)) for i in range(2)]
            pB = [st.enter_context(nc.psum_tensor(f"pm2_{which}{i}", [128, SUB], F32)) for i in range(2)]
            pM = st.enter_context(nc.psum_tensor(f"pM_{which}", [128, SUB], F32)) if which != "sh" else None

            if which == "e0":
                dW1, dW2, bias1, bias2 = d_w1_b[0], d_w2_b[0], c_b1[0], c_b2[0]
                ck1, ck2 = "w1_0", "w2_0"
            elif which == "e1":
                dW1, dW2, bias1, bias2 = d_w1_b[1], d_w2_b[1], c_b1[1], c_b2[1]
                ck1, ck2 = "w1_1", "w2_1"
            else:
                dW1, dW2, bias1, bias2 = d_sw1_b, d_sw2_b, c_sb1, c_sb2
                ck1, ck2 = "sw1", "sw2"

            ph = Phase(B)
            wqb = {}
            while late_cache and late_cache[0][0] in (ck1, ck2):
                nm, dap, sap = late_cache.pop(0)
                cache_ready[nm] = ph.dma("gpsimd", dap, sap, "qWC")
            if "wa_next" in p2w:
                wqa = p2w.pop("wa_next")
                wqa_jmajor = False
            else:
                wqa_jmajor = True
                wqa = {}
                ph.wait("gpsimd", "qWC", cache_ready[ck1])
                for jg in range(8):
                    srcap = bass.AP(dW1, jg * 512, [[HID, 128], [128 * HID, DT], [1, 512]])
                    dstap = bass.AP(wA, jg * 512, [[DT * HID, 128], [HID, DT], [1, 512]])
                    wqa[jg] = ph.dma("gpsimd", dstap, srcap, "qWA")
            xf_loaded, xf_loaded2, m_loaded = {}, {}, {}
            if wqa_jmajor:
                ph.wait("scalar", "qWC", cache_ready[ck2])
                hh = DT * SUB // 2
                xf_loaded2[0] = ph.dma("scalar", xfin[0][:, hh:DT * SUB],
                                       d_xfsp.ap()[0, :, hh:DT * SUB], "PXFS")
            else:
                ph.wait("gpsimd", "qWC", cache_ready[ck2])
            for q in range(4):
                srcap = bass.AP(dW2, q * 8 * 128 * D, [[D, 128], [128 * D, 8], [1, D]])
                dstap = bass.AP(wB, q * 8 * D, [[HT * D, 128], [D, 8], [1, D]])
                if wqa_jmajor:
                    wqb[q] = ph.dma("scalar", dstap, srcap, f"qWB{q}", gate=False)
                else:
                    wqb[q] = ph.dma("gpsimd", dstap, srcap, f"qW{q}")

            mb_done, thr_done, xr_done, hd_ev, ax_done = {}, {}, {}, {}, {}
            m1_grp, m2_grp, out_done, out_stored = {}, {}, {}, {}

            def em_xfload(i):
                if i >= NSUB:
                    return
                if i == 0 and "xf0_cnt" in p2w:
                    xf_loaded[0] = p2w.pop("xf0_cnt")
                    if which != "sh":
                        m_loaded[0] = p2w.pop("m0_cnt")
                    return
                if i == 0 and wqa_jmajor:
                    m_loaded[0] = ph.dma("sync", mr[0][:, :], d_msp.ap()[0, :, :], "PM0")
                    hh = DT * SUB // 2
                    xf_loaded[0] = ph.dma("sync", xfin[0][:, 0:hh], d_xfsp.ap()[0, :, 0:hh], "PXF0")
                    return
                if i >= 2:
                    if which == "sh":
                        ph.wait("sync", "tE", m1_grp[(i - 2, HT - 1)])
                    else:
                        ph.wait("sync", "vE", xr_done[i - 2])
                xf_loaded[i] = ph.dma("sync", xfin[i % 2][:, :], d_xfsp.ap()[i, :, :], f"PXF{i % 2}")
                if which != "sh":
                    m_loaded[i] = ph.dma("sync", mr[i % 2][:, :], d_msp.ap()[i, :, :], f"PM{i % 2}")

            def em_mask(i):
                mrow = mr[i % 2]
                if which == "e1":
                    ph.wait("vector", f"PM{i % 2}", m_loaded[i])
                    if i >= 1:
                        ph.wait("vector", "tE", mb_done[i - 1])
                    thr_done[i] = ph.op("vector", lambda e, r=mrow: e.tensor_scalar(
                        thrr[:, :], r[:, :], -1.0, 1.0, ALU.mult, ALU.add), inc="vE")
                    brow = thrr
                    ph.wait("tensor", "vE", thr_done[i])
                else:
                    brow = mrow
                    ph.wait("tensor", f"PM{i % 2}", m_loaded[i])
                if i >= 1:
                    ph.wait("tensor", "vE", xr_done[i - 1])
                mb_done[i] = ph.op("tensor", lambda e, r=brow: e.matmul(
                    pM[:, :], ones_rowb[:, :], r[:, :], start=True, stop=True), inc="tE")
                ph.wait("vector", "tE", mb_done[i])
                ph.wait("vector", f"PXF{i % 2}", xf_loaded[i])
                if i in xf_loaded2:
                    ph.wait("vector", "PXFS", xf_loaded2[i])
                if i >= 1:
                    ph.wait("vector", "tE", m1_grp[(i - 1, HT - 1)])
                xfc = xfin[i % 2]
                for t in range(DT):
                    sl = slice(t * SUB, (t + 1) * SUB)
                    xr_done[i] = ph.op("vector", lambda e, x=xfc, sl=sl: e.tensor_mul(
                        xr[:, sl], x[:, sl], pM[:, :]), inc="vE" if t == DT - 1 else None)

            def em_m1(i):
                rhs = xfin[i % 2] if which == "sh" else xr
                if which == "sh":
                    ph.wait("tensor", f"PXF{i % 2}", xf_loaded[i])
                else:
                    ph.wait("tensor", "vE", xr_done[i])
                for j in range(HT):
                    pg = pA[j % 2]
                    if i == 0 and wqa_jmajor and j % 4 == 0:
                        ph.wait("tensor", "qWA", wqa[j // 4])
                    key = (i, j - 2) if j >= 2 else (i - 1, HT - 2 + j)
                    if key in hd_ev:
                        ph.wait("tensor", "sE", hd_ev[key])
                    g = None
                    for k in range(DT):
                        if i == 0 and j == 0 and not wqa_jmajor:
                            ph.wait("tensor", f"qW{k // 2}", wqa[k // 2])
                        g = ph.op("tensor", lambda e, j=j, k=k, pg=pg, r=rhs: e.matmul(
                            pg[:, :], wA[:, k * HID + j * 128: k * HID + j * 128 + 128],
                            r[:, k * SUB:(k + 1) * SUB], start=(k == 0), stop=(k == DT - 1)),
                            inc="tE" if k == DT - 1 else None)
                    m1_grp[(i, j)] = g
                    ph.wait("scalar", "tE", g)
                    if i >= 1 and j < 2:
                        ph.wait("scalar", "tE", m2_grp[(i - 1, DT - 1)])
                    hd_ev[(i, j)] = ph.op("scalar", lambda e, j=j, pg=pg, b1=bias1: e.activation(
                        hd[:, j * SUB:(j + 1) * SUB], pg[:, :], AF.Relu, bias=b1[:, j:j + 1]), inc="sE")

            def em_m2out(i):
                c, b = divmod(i, n_b)
                for o in range(DT):
                    pg = pB[o % 2]
                    oc = o % 2
                    go = i * DT + o
                    key = (i, o - 2) if o >= 2 else (i - 1, DT - 2 + o)
                    if key in out_done:
                        ph.wait("tensor", "vE", out_done[key])
                    g = None
                    for k in range(HT):
                        if i == 0 and o == 0:
                            ph.wait("tensor", f"qWB{k // 8}" if wqa_jmajor else f"qW{k // 8}", wqb[k // 8])
                        if o == 0 and k == 0:
                            ph.wait("tensor", "sE", hd_ev[(i, HT - 2)])
                        if o == 0 and k == HT - 1:
                            ph.wait("tensor", "sE", hd_ev[(i, HT - 1)])
                        g = ph.op("tensor", lambda e, o=o, k=k, pg=pg: e.matmul(
                            pg[:, :], wB[:, k * D + o * 128: k * D + o * 128 + 128],
                            hd[:, k * SUB:(k + 1) * SUB], start=(k == 0), stop=(k == HT - 1)),
                            inc="tE" if k == HT - 1 else None)
                    m2_grp[(i, o)] = g

                    la = lx = None
                    if which != "e0":
                        d_prev = d_acc0 if which == "e1" else d_acc1
                        la = ph.dma("sync", accin[oc][:, :], bass.AP(
                            d_prev, i * 128 * DT * SUB + o * SUB, [[DT * SUB, 128], [1, SUB]]), f"PAC{oc}")
                    if which == "sh":
                        if key in ax_done:
                            ph.wait("scalar", "pE", ax_done[key])
                        lx = ph.dma("scalar", xmin[oc][:, :], bass.AP(
                            d_xmsp, i * 128 * DT * SUB + o * SUB, [[DT * SUB, 128], [1, SUB]]), f"PXM{oc}")

                    ph.wait("vector", "tE", g)
                    if go >= 2:
                        ph.wait("vector", f"POS{oc}", out_stored[go - 2])
                    if which == "e0":
                        out_done[(i, o)] = ph.op("vector", lambda e, pg=pg, o=o, oc=oc, b2=bias2: e.tensor_scalar(
                            outst[oc][:, :], pg[:, :], b2[:, o:o + 1], None, ALU.add), inc="vE")
                    elif which == "e1":
                        ph.wait("vector", f"PAC{oc}", la)
                        out_done[(i, o)] = ph.op("vector", lambda e, pg=pg, o=o, oc=oc, b2=bias2: e.scalar_tensor_tensor(
                            outst[oc][:, :], pg[:, :], b2[:, o:o + 1], accin[oc][:, :], ALU.add, ALU.add), inc="vE")
                    else:
                        ph.wait("gpsimd", f"PAC{oc}", la)
                        ph.wait("gpsimd", f"PXM{oc}", lx)
                        if key in out_done:
                            ph.wait("gpsimd", "vE", out_done[key])
                        ax_done[(i, o)] = ph.op("gpsimd", lambda e, oc=oc: e.tensor_add(
                            axm[oc][:, :], accin[oc][:, :], xmin[oc][:, :]), inc="pE")
                        ph.wait("vector", "pE", ax_done[(i, o)])
                        out_done[(i, o)] = ph.op("vector", lambda e, pg=pg, o=o, oc=oc, b2=bias2: e.scalar_tensor_tensor(
                            outst[oc][:, :], pg[:, :], b2[:, o:o + 1], axm[oc][:, :], ALU.add, ALU.add), inc="vE")

                    ph.wait("sync", "vE", out_done[(i, o)])
                    if which == "e0":
                        dstap = bass.AP(d_acc0, i * 128 * DT * SUB + o * SUB, [[DT * SUB, 128], [1, SUB]])
                    elif which == "e1":
                        dstap = bass.AP(d_acc1, i * 128 * DT * SUB + o * SUB, [[DT * SUB, 128], [1, SUB]])
                    else:
                        dstap = bass.AP(d_out, (o * 128) * n_b * K + b * K + c * SUB, [[n_b * K, 128], [1, SUB]])
                    out_stored[go] = ph.dma("sync", dstap, outst[oc][:, :], f"POS{oc}")

            em_xfload(0)
            em_xfload(1)
            if which != "sh":
                em_mask(0)
            for w in range(NSUB):
                em_m1(w)
                if which != "sh" and late_cache:
                    ph.wait("gpsimd", "tE", m1_grp[(w, 0)])
                    for _ in range(2):
                        if late_cache:
                            nm, dap, sap = late_cache.pop(0)
                            cache_ready[nm] = ph.dma("gpsimd", dap, sap, "qWC")
                if which != "sh" and w + 1 < NSUB:
                    em_mask(w + 1)
                em_m2out(w)
                em_xfload(w + 2)

            if which != "sh":
                ph.wait("sync", "vE", xr_done[NSUB - 2])
                p2w["xf0_cnt"] = ph.dma("sync", xfin[0][:, :], d_xfsp.ap()[0, :, :], "PXF0")
                if which == "e0":
                    ph.wait("sync", "tE", mb_done[NSUB - 2])
                    p2w["m0_cnt"] = ph.dma("sync", mr[0][:, :], d_msp.ap()[0, :, :], "PM0")

            if next_dW1 is not None:
                ph.wait("gpsimd", "tE", m1_grp[(NSUB - 1, HT - 1)])
                ph.wait("gpsimd", "qWC", cache_ready[next_key])
                nxt = {}
                for q in range(4):
                    srcap = bass.AP(next_dW1, q * 2 * 128 * HID, [[HID, 128], [128 * HID, 2], [1, HID]])
                    dstap = bass.AP(wA, q * 2 * HID, [[DT * HID, 128], [HID, 2], [1, HID]])
                    nxt[q] = ph.dma("gpsimd", dstap, srcap, f"qW{q}")
                p2w["wa_next"] = nxt
            for s in ["POS0", "POS1"]:
                if s in B.cnt:
                    ph.wait("sync", s, B.cnt[s])
            ph.run()

    ffn_branch("e0", next_key="w1_1", next_dW1=d_w1_b[1])
    ffn_branch("e1", next_key="sw1", next_dW1=d_sw1_b)
    ffn_branch("sh")

    ph = Phase(B)
    for s in ["POS0", "POS1", "qW"]:
        if s in B.cnt:
            ph.wait("sync", s, B.cnt[s])
    ph.run()

    prog.close()
    return nc


def _cols(v, nt):
    return np.ascontiguousarray(np.asarray(v, dtype=np.float32).reshape(nt, 128).T)


def kernel(**inp):
    x = np.asarray(inp["L0_attn_x"], dtype=np.float32)
    W_in = np.asarray(inp["L0_attn_W_in"], dtype=np.float32)
    W_dt = np.asarray(inp["L0_attn_W_dt"], dtype=np.float32)
    W_out = np.asarray(inp["L0_attn_W_out"], dtype=np.float32)
    gate_w = np.asarray(inp["L0_ffn_gate_w"], dtype=np.float32)
    w1 = np.asarray(inp["L0_ffn_w1"], dtype=np.float32)
    w2 = np.asarray(inp["L0_ffn_w2"], dtype=np.float32)
    b1 = np.asarray(inp["L0_ffn_b1"], dtype=np.float32)
    b2 = np.asarray(inp["L0_ffn_b2"], dtype=np.float32)

    shared = {
        "w_in_uT": np.ascontiguousarray(W_in[:INNER].T),
        "w_in_zT": np.ascontiguousarray(W_in[INNER:].T),
        "w_dtT": np.ascontiguousarray(W_dt.T),
        "w_outT": np.ascontiguousarray(W_out.T),
        "w1_0": np.ascontiguousarray(w1[0]),
        "w1_1": np.ascontiguousarray(w1[1]),
        "w2_0": np.ascontiguousarray(w2[0]),
        "w2_1": np.ascontiguousarray(w2[1]),
        "sw1": np.ascontiguousarray(np.asarray(inp["L0_ffn_sw1"], dtype=np.float32)),
        "sw2": np.ascontiguousarray(np.asarray(inp["L0_ffn_sw2"], dtype=np.float32)),
        "wn_c": _cols(inp["L0_attn_w_norm"], DT),
        "bdt_c": _cols(inp["L0_attn_b_dt"], IT),
        "b1_0c": _cols(b1[0], HT),
        "b1_1c": _cols(b1[1], HT),
        "sb1_c": _cols(inp["L0_ffn_sb1"], HT),
        "b2_0c": _cols(b2[0], DT),
        "b2_1c": _cols(b2[1], DT),
        "sb2_c": _cols(inp["L0_ffn_sb2"], DT),
        "g0_c": _cols(gate_w[0], DT),
        "g1_c": _cols(gate_w[1], DT),
    }

    nc = build_nc()
    in_maps = []
    for c in range(8):
        m = dict(shared)
        m["xT"] = np.ascontiguousarray(x[4 * c:4 * c + 4].transpose(2, 0, 1))
        in_maps.append(m)
    res = run_bass_kernel_spmd(nc, in_maps, core_ids=list(range(8)))
    out = np.empty((32, 2048, 1024), np.float32)
    for c in range(8):
        out[4 * c:4 * c + 4] = np.asarray(res.results[c]["outT"]).transpose(1, 2, 0)
    return out


# version 35
# speedup vs baseline: 1.1223x; 1.1223x over previous
import sys
if "/opt/trn_rl_repo" not in sys.path:
    sys.path.insert(0, "/opt/trn_rl_repo")
from contextlib import ExitStack

import numpy as np
import concourse.bass as bass
import concourse.mybir as mybir
from concourse.bass_utils import run_bass_kernel_spmd

F32 = mybir.dt.float32
BF16 = mybir.dt.bfloat16
AF = mybir.ActivationFunctionType
ALU = mybir.AluOpType

D = 1024
INNER = 2048
HID = 4096
DT = D // 128
IT = INNER // 128
HT = HID // 128
SUB = 512
EPS = 1e-5

ENG_SEMS = ["tE", "sE", "vE", "pE"]


class Build:
    def __init__(self, nc, prog):
        self.nc = nc
        self.prog = prog
        self.cnt = {}
        self.sems = {}
        for s in ENG_SEMS:
            self._mk(s)

    def _mk(self, name):
        self.sems[name] = self.prog.enter_context(self.nc.semaphore(name))
        self.cnt[name] = 0

    def stream(self, name):
        if name not in self.sems:
            self._mk(name)
        return name


class Phase:
    def __init__(self, B: Build):
        self.B = B
        self.streams = {e: [] for e in ["sync", "gpsimd", "tensor", "scalar", "vector"]}
        self.dma_streams = set()

    def wait(self, eng, sem, cnt):
        if cnt is None or cnt <= 0:
            return
        h = self.B.sems[sem]
        self.streams[eng].append(lambda e, h=h, cnt=cnt: e.wait_ge(h, cnt))

    def op(self, eng, fn, inc=None, inc_n=1):
        if inc is None:
            self.streams[eng].append(fn)
            return None
        self.B.cnt[inc] += inc_n
        h = self.B.sems[inc]
        def wrapped(e, fn=fn, h=h, n=inc_n):
            fn(e).then_inc(h, n)
        self.streams[eng].append(wrapped)
        return self.B.cnt[inc]

    def dma(self, eng, dst, src, stream, gate=True):
        s = self.B.stream(stream)
        self.dma_streams.add(s)
        if gate and self.B.cnt[s] > 0:
            self.wait(eng, s, self.B.cnt[s])
        return self.op(eng, lambda e, d=dst, sr=src: e.dma_start(d, sr), inc=s, inc_n=16)

    def run(self, skip_drain=()):
        for s in sorted(self.dma_streams):
            if s in skip_drain:
                continue
            self.wait("sync", s, self.B.cnt[s])
        with self.B.nc.Block() as blk:
            for eng in ["sync", "gpsimd", "tensor", "scalar", "vector"]:
                ops = self.streams[eng]
                if not ops:
                    continue
                def body(e, ops=ops):
                    for f in ops:
                        f(e)
                getattr(blk, eng)(body)


def build_nc(n_b=4, n_c=4):
    K = n_c * SUB
    NSUB = n_b * n_c
    nc = bass.Bass("TRN2", target_bir_lowering=False)

    d_x = nc.dram_tensor("xT", [D, n_b, K], F32, kind="ExternalInput")
    d_wu = nc.dram_tensor("w_in_uT", [D, INNER], F32, kind="ExternalInput")
    d_wz = nc.dram_tensor("w_in_zT", [D, INNER], F32, kind="ExternalInput")
    d_wdt = nc.dram_tensor("w_dtT", [INNER, INNER], F32, kind="ExternalInput")
    d_wout = nc.dram_tensor("w_outT", [INNER, D], F32, kind="ExternalInput")
    d_w1 = [nc.dram_tensor(f"w1_{e}", [D, HID], F32, kind="ExternalInput") for e in range(2)]
    d_w2 = [nc.dram_tensor(f"w2_{e}", [HID, D], F32, kind="ExternalInput") for e in range(2)]
    d_sw1 = nc.dram_tensor("sw1", [D, HID], F32, kind="ExternalInput")
    d_sw2 = nc.dram_tensor("sw2", [HID, D], F32, kind="ExternalInput")
    d_wn = nc.dram_tensor("wn_c", [128, DT], F32, kind="ExternalInput")
    d_bdt = nc.dram_tensor("bdt_c", [128, IT], F32, kind="ExternalInput")
    d_b1 = [nc.dram_tensor(f"b1_{e}c", [128, HT], F32, kind="ExternalInput") for e in range(2)]
    d_sb1 = nc.dram_tensor("sb1_c", [128, HT], F32, kind="ExternalInput")
    d_b2 = [nc.dram_tensor(f"b2_{e}c", [128, DT], F32, kind="ExternalInput") for e in range(2)]
    d_sb2 = nc.dram_tensor("sb2_c", [128, DT], F32, kind="ExternalInput")
    d_g0 = nc.dram_tensor("g0_c", [128, DT], F32, kind="ExternalInput")
    d_g1 = nc.dram_tensor("g1_c", [128, DT], F32, kind="ExternalInput")
    d_out = nc.dram_tensor("outT", [D, n_b, K], F32, kind="ExternalOutput")

    d_usp = nc.dram_tensor("u_sp", [NSUB, 128, IT * SUB], BF16)
    d_zsp = nc.dram_tensor("z_sp", [NSUB, 128, IT * SUB], BF16)
    d_xlnsp = nc.dram_tensor("xln_sp", [NSUB, 128, DT * SUB], F32)
    d_szsp = nc.dram_tensor("sz_sp", [NSUB, 128, IT * SUB], BF16)
    d_xmsp = nc.dram_tensor("xm_sp", [NSUB, 128, DT * SUB], F32)
    d_xfsp = nc.dram_tensor("xf_sp", [NSUB, 128, DT * SUB], BF16)
    d_msp = nc.dram_tensor("m_sp", [NSUB, 1, SUB], BF16)
    d_acc0 = nc.dram_tensor("acc0_sp", [NSUB, 128, DT * SUB], F32)
    d_acc1 = nc.dram_tensor("acc1_sp", [NSUB, 128, DT * SUB], F32)
    d_wdt_b = nc.dram_tensor("wdt_b", [INNER, INNER], BF16)
    d_wout_b = nc.dram_tensor("wout_b", [INNER, D], BF16)
    d_w1_b = [nc.dram_tensor(f"w1b_{e}", [D, HID], BF16) for e in range(2)]
    d_w2_b = [nc.dram_tensor(f"w2b_{e}", [HID, D], BF16) for e in range(2)]
    d_sw1_b = nc.dram_tensor("sw1_b", [D, HID], BF16)
    d_sw2_b = nc.dram_tensor("sw2_b", [HID, D], BF16)
    cache_ready = {}
    late_cache = []

    prog = ExitStack()
    B = Build(nc, prog)

    c_wn = prog.enter_context(nc.sbuf_tensor("c_wn", [128, DT], F32))
    c_bdt = prog.enter_context(nc.sbuf_tensor("c_bdt", [128, IT], F32))
    c_nbdt = prog.enter_context(nc.sbuf_tensor("c_nbdt", [128, IT], F32))
    c_b1 = [prog.enter_context(nc.sbuf_tensor(f"c_b1_{e}", [128, HT], F32)) for e in range(2)]
    c_sb1 = prog.enter_context(nc.sbuf_tensor("c_sb1", [128, HT], F32))
    c_b2 = [prog.enter_context(nc.sbuf_tensor(f"c_b2_{e}", [128, DT], F32)) for e in range(2)]
    c_sb2 = prog.enter_context(nc.sbuf_tensor("c_sb2", [128, DT], F32))
    c_g0 = prog.enter_context(nc.sbuf_tensor("c_g0", [128, DT], F32))
    c_g1 = prog.enter_context(nc.sbuf_tensor("c_g1", [128, DT], F32))
    c_dwf = prog.enter_context(nc.sbuf_tensor("c_dwf", [128, DT], F32))
    c_dwhi = prog.enter_context(nc.sbuf_tensor("c_dwhi", [128, DT], BF16))
    c_dwhi32 = prog.enter_context(nc.sbuf_tensor("c_dwhi32", [128, DT], F32))
    c_dwlo = prog.enter_context(nc.sbuf_tensor("c_dwlo", [128, DT], BF16))
    c_dwsum = prog.enter_context(nc.sbuf_tensor("c_dwsum", [128, 1], F32))
    c_dwsumb = prog.enter_context(nc.sbuf_tensor("c_dwsumb", [128, 1], BF16))
    r_sdw = prog.enter_context(nc.sbuf_tensor("r_sdw", [1, 1], F32))
    ones_col = prog.enter_context(nc.sbuf_tensor("ones_col", [128, 1], BF16))
    ones_colf = prog.enter_context(nc.sbuf_tensor("ones_colf", [128, 1], F32))
    ones_rowf = prog.enter_context(nc.sbuf_tensor("ones_rowf", [1, 128], F32))
    ones_rowb = prog.enter_context(nc.sbuf_tensor("ones_rowb", [1, 128], BF16))
    c_eps = prog.enter_context(nc.sbuf_tensor("c_eps", [1, 1], F32))
    c_eps2 = prog.enter_context(nc.sbuf_tensor("c_eps2", [1, 1], F32))
    carry = prog.enter_context(nc.sbuf_tensor("carry", [128, n_b * IT], F32))

    st_p1a = ExitStack()
    xs = [st_p1a.enter_context(nc.sbuf_tensor(f"xs{i}", [128, DT * SUB], F32)) for i in range(2)]
    x_loaded, x_loaded2 = {}, {}

    ph = Phase(B)
    for dst, src in [(c_wn, d_wn), (c_bdt, d_bdt), (c_b1[0], d_b1[0]), (c_b1[1], d_b1[1]),
                     (c_sb1, d_sb1), (c_b2[0], d_b2[0]), (c_b2[1], d_b2[1]), (c_sb2, d_sb2),
                     (c_g0, d_g0), (c_g1, d_g1)]:
        ph.dma("sync", dst[:, :], src.ap(), "CL", gate=False)
    nconst = B.cnt["CL"]
    ph.op("gpsimd", lambda e: e.memset(ones_col[:, :], 1.0))
    ph.op("gpsimd", lambda e: e.memset(ones_colf[:, :], 1.0))
    ph.op("gpsimd", lambda e: e.memset(ones_rowf[:, :], 1.0))
    ph.op("gpsimd", lambda e: e.memset(ones_rowb[:, :], 1.0))
    ph.op("gpsimd", lambda e: e.memset(c_eps[:, :], EPS))
    ph.op("gpsimd", lambda e: e.memset(c_eps2[:, :], EPS * EPS))
    ph.op("gpsimd", lambda e: e.memset(carry[:, :], 0.0))
    for i in (0, 1):
        c0, b0 = divmod(i, n_b)
        for h, (heng, hq) in enumerate([("scalar", f"XL{i % 2}"), ("sync", f"XS{i % 2}")]):
            srcx = bass.AP(d_x, b0 * K + c0 * SUB + h * 4 * 128 * n_b * K,
                           [[n_b * K, 128], [128 * n_b * K, 4], [1, SUB]])
            dstx = bass.AP(xs[i % 2], h * 4 * SUB, [[DT * SUB, 128], [SUB, 4], [1, SUB]])
            (x_loaded if h == 0 else x_loaded2)[i] = ph.dma(heng, dstx, srcx, hq)
    ph.wait("vector", "CL", nconst)
    ph.op("vector", lambda e: e.tensor_scalar_mul(c_nbdt[:, :], c_bdt[:, :], -1.0))
    v1 = ph.op("vector", lambda e: e.tensor_sub(c_dwf[:, :], c_g0[:, :], c_g1[:, :]), inc="vE")
    ph.wait("scalar", "vE", v1)
    s0 = ph.op("scalar", lambda e: e.copy(c_dwhi[:, :], c_dwf[:, :]), inc="sE")
    ph.wait("scalar", "sE", s0)
    s1 = ph.op("scalar", lambda e: e.copy(c_dwhi32[:, :], c_dwhi[:, :]), inc="sE")
    ph.wait("vector", "sE", s1)
    v2 = ph.op("vector", lambda e: e.tensor_sub(c_dwlo[:, :], c_dwf[:, :], c_dwhi32[:, :]), inc="vE")
    ph.wait("vector", "vE", v2)
    v3 = ph.op("vector", lambda e: e.reduce_sum(c_dwsum[:, :], c_dwf[:, :], axis=mybir.AxisListType.X), inc="vE")
    ph.wait("scalar", "vE", v3)
    ph.op("scalar", lambda e: e.copy(c_dwsumb[:, :], c_dwsum[:, :]), inc="sE")
    ph.run(skip_drain={"XL0", "XL1", "XS0", "XS1"})

    with nc.psum_tensor("p_sdw", [1, 1], F32) as p_sdw:
        ph = Phase(B)
        t1 = ph.op("tensor", lambda e: e.matmul(p_sdw[:, :], ones_col[:, :], c_dwsumb[:, :], start=True, stop=True), inc="tE")
        ph.wait("scalar", "tE", t1)
        ph.op("scalar", lambda e: e.copy(r_sdw[:, :], p_sdw[:, :]), inc="sE")
        ph.run()

    with st_p1a as st:
        wu = st.enter_context(nc.sbuf_tensor("wu", [128, DT * INNER], BF16))
        wz = st.enter_context(nc.sbuf_tensor("wz", [128, DT * INNER], BF16))
        xbf = st.enter_context(nc.sbuf_tensor("xbf", [128, DT * SUB], BF16))
        x2bf = st.enter_context(nc.sbuf_tensor("x2bf", [128, DT * SUB], BF16))
        ln1 = st.enter_context(nc.sbuf_tensor("ln1", [128, DT * SUB], F32))
        hb = [st.enter_context(nc.sbuf_tensor(f"hb{i}", [128, DT * SUB], BF16)) for i in range(2)]
        ub = st.enter_context(nc.sbuf_tensor("ub", [128, IT * SUB], BF16))
        zb = st.enter_context(nc.sbuf_tensor("zb", [128, 8 * SUB], BF16))
        xlnb = st.enter_context(nc.sbuf_tensor("xlnb", [128, DT * SUB], F32))
        rows = {nm: st.enter_context(nc.sbuf_tensor(f"ra_{nm}", [1, SUB], F32))
                for nm in ["m", "msq", "t0", "v", "sd1", "sd12"]}
        for nm in ["r1", "r12"]:
            rows[nm] = st.enter_context(nc.sbuf_tensor(f"ra_{nm}", [1, SUB], F32))
        for nm in ["r1b", "mr1", "rr"]:
            rows[nm] = st.enter_context(nc.sbuf_tensor(f"ra_{nm}", [1, SUB], BF16))
        pstat = st.enter_context(nc.psum_tensor("pstat", [128, SUB], F32))
        pbc = [st.enter_context(nc.psum_tensor(f"pbc{i}", [128, SUB], F32)) for i in range(3)]
        pA = [st.enter_context(nc.psum_tensor(f"pA{i}", [128, SUB], F32)) for i in range(4)]

        ph = Phase(B)
        qw_base = B.cnt.get("qW", 0)

        cast_done, sum_done, sumsq_done = {}, {}, {}
        rows_done, bc_done, h_done, xln_done, xln_stored = {}, {}, {}, {}, {}
        evac_cnt, grp_cnt, bank_hist = {}, {}, {}
        ustoreA, ustoreB, zstoreA, zstoreB = {}, {}, {}, {}

        def x_src(i):
            c, b = divmod(i, n_b)
            return bass.AP(d_x, b * K + c * SUB, [[n_b * K, 128], [128 * n_b * K, DT], [1, SUB]])

        def slab_dst(buf, nt=DT):
            return bass.AP(buf, 0, [[nt * SUB, 128], [SUB, nt], [1, SUB]])

        wq_cnt = {}

        def em_weights():
            n = 0
            for wt, dr in [(wu, d_wu), (wz, d_wz)]:
                for hh in range(2):
                    srcap = bass.AP(dr, hh * 4 * 128 * INNER, [[INNER, 128], [128 * INNER, 4], [1, INNER]])
                    dstap = bass.AP(wt, hh * 4 * INNER, [[DT * INNER, 128], [INNER, 4], [1, INNER]])
                    wq_cnt[n] = ph.dma("gpsimd", dstap, srcap, "qW0")
                    n += 1

        def cache_chunks(pairs):
            out = []
            for nm, dsrc, ddst, r, cN in pairs:
                nch = max(1, (r * cN * 4) // (2 << 20))
                rch = r // nch
                for hh in range(nch):
                    sap = bass.AP(dsrc, hh * rch * cN, [[cN, rch], [1, cN]])
                    dap = bass.AP(ddst, hh * rch * cN, [[cN, rch], [1, cN]])
                    out.append((nm, dap, sap))
            return out

        chunks = cache_chunks([
            ("wdt", d_wdt, d_wdt_b, INNER, INNER),
            ("wout", d_wout, d_wout_b, INNER, D),
            ("w1_0", d_w1[0], d_w1_b[0], D, HID),
            ("w2_0", d_w2[0], d_w2_b[0], HID, D)])
        late_cache.extend(cache_chunks([
            ("w1_1", d_w1[1], d_w1_b[1], D, HID),
            ("w2_1", d_w2[1], d_w2_b[1], HID, D),
            ("sw1", d_sw1, d_sw1_b, D, HID),
            ("sw2", d_sw2, d_sw2_b, HID, D)]))

        def em_cache_some(k, gate_cnt=None):
            if gate_cnt is not None and chunks:
                ph.wait("gpsimd", "tE", gate_cnt)
            for _ in range(k):
                if not chunks:
                    return
                nm, dap, sap = chunks.pop(0)
                cache_ready[nm] = ph.dma("gpsimd", dap, sap, "qWC")

        def em_xload(i, eng="scalar"):
            if i >= NSUB:
                return
            ph.wait(eng, "vE", xln_done[i - 2])
            x_loaded[i] = ph.dma(eng, slab_dst(xs[i % 2]), x_src(i), f"XL{i % 2}")

        def em_casts(i):
            if i >= NSUB:
                return
            ph.wait("vector", f"XL{i % 2}", x_loaded[i])
            if i < 2:
                ph.wait("vector", f"XS{i % 2}", x_loaded2[i])
            if i >= 1:
                ph.wait("vector", "tE", sumsq_done[i - 1])
                ph.wait("vector", "tE", sum_done[i - 1])
            ph.op("vector", lambda e, x=xs[i % 2]: e.tensor_mul(x2bf[:, :], x[:, :], x[:, :]))
            cast_done[i] = ph.op("vector", lambda e, x=xs[i % 2]: e.tensor_copy(xbf[:, :], x[:, :]), inc="vE")

        def em_stats(i):
            ph.wait("tensor", "vE", cast_done[i])
            if i >= 1:
                ph.wait("tensor", "vE", rows_done[i - 1])
            for t in range(DT):
                sum_done[i] = ph.op("tensor", lambda e, t=t: e.matmul(
                    pstat[0:1, :], ones_col[:, :], xbf[:, t * SUB:(t + 1) * SUB],
                    start=(t == 0), stop=(t == DT - 1)), inc="tE" if t == DT - 1 else None)
            for t in range(DT):
                sumsq_done[i] = ph.op("tensor", lambda e, t=t: e.matmul(
                    pstat[32:33, :], ones_col[:, :], x2bf[:, t * SUB:(t + 1) * SUB],
                    start=(t == 0), stop=(t == DT - 1)), inc="tE" if t == DT - 1 else None)

        def em_rows_a(i):
            ph.wait("vector", "tE", sumsq_done[i])
            ph.op("vector", lambda e: e.tensor_single_scalar(rows["m"][:, :], pstat[0:1, :], 1.0 / D, ALU.mult))
            c1 = ph.op("vector", lambda e: e.tensor_single_scalar(rows["msq"][:, :], pstat[32:33, :], 1.0 / D, ALU.mult), inc="vE")
            ph.wait("vector", "vE", c1)
            c2 = ph.op("vector", lambda e: e.tensor_mul(rows["t0"][:, :], rows["m"][:, :], rows["m"][:, :]), inc="vE")
            ph.wait("vector", "vE", c2)
            rows_done[("v", i)] = ph.op("vector", lambda e: e.tensor_sub(rows["v"][:, :], rows["msq"][:, :], rows["t0"][:, :]), inc="vE")

        def em_rows_b(i):
            ph.wait("scalar", "vE", rows_done[("v", i)])
            ph.op("scalar", lambda e: e.activation(rows["sd1"][:, :], rows["v"][:, :], AF.Ln, bias=c_eps[:, :], scale=1.0))
            l2 = ph.op("scalar", lambda e: e.activation(rows["sd12"][:, :], rows["v"][:, :], AF.Ln, bias=c_eps2[:, :], scale=1.0 + EPS), inc="sE")
            ph.wait("scalar", "sE", l2)
            ph.op("scalar", lambda e: e.activation(rows["r1"][:, :], rows["sd1"][:, :], AF.Exp, scale=-0.5))
            rows_done[("sq", i)] = ph.op("scalar", lambda e: e.activation(rows["r12"][:, :], rows["sd12"][:, :], AF.Exp, scale=-0.5), inc="sE")

        def em_rows_c(i):
            ph.wait("vector", "sE", rows_done[("sq", i)])
            ph.op("vector", lambda e: e.tensor_copy(rows["r1b"][:, :], rows["r1"][:, :]))
            ph.op("vector", lambda e: e.tensor_copy(rows["mr1"][:, :], rows["m"][:, :]))
            rows_done[i] = ph.op("vector", lambda e: e.tensor_copy(rows["rr"][:, :], rows["r12"][:, :]), inc="vE")

        def em_bc(i):
            ph.wait("tensor", "vE", rows_done[i])
            if i >= 1:
                ph.wait("tensor", "vE", h_done[i - 1])
            ph.op("tensor", lambda e: e.matmul(pbc[0][:, :], ones_rowb[:, :], rows["mr1"][:, :], start=True, stop=True))
            ph.op("tensor", lambda e: e.matmul(pbc[1][:, :], ones_rowb[:, :], rows["r1b"][:, :], start=True, stop=True))
            bc_done[i] = ph.op("tensor", lambda e: e.matmul(pbc[2][:, :], ones_rowb[:, :], rows["rr"][:, :], start=True, stop=True), inc="tE")

        def em_lnh_h(i):
            xcur, hcur = xs[i % 2], hb[i % 2]
            ph.wait("vector", "tE", bc_done[i])
            p1 = None
            for t in range(DT):
                sl = slice(t * SUB, (t + 1) * SUB)
                p1 = ph.op("vector", lambda e, x=xcur, sl=sl: e.tensor_sub(ln1[:, sl], x[:, sl], pbc[0][:, :]),
                           inc="vE" if t == DT - 1 else None)
            ph.wait("vector", "vE", p1)
            if i >= 2:
                ph.wait("vector", "tE", grp_cnt[(i - 2, "z", IT - 1)])
            for t in range(DT):
                sl = slice(t * SUB, (t + 1) * SUB)
                h_done[i] = ph.op("vector", lambda e, h=hcur, sl=sl, t=t: e.scalar_tensor_tensor(
                    h[:, sl], ln1[:, sl], c_wn[:, t:t + 1], pbc[2][:, :], ALU.mult, ALU.mult),
                    inc="vE" if t == DT - 1 else None)

        def em_lnh_rest(i):
            xcur = xs[i % 2]
            ph.wait("vector", "vE", h_done[i])
            p2 = None
            for t in range(DT):
                sl = slice(t * SUB, (t + 1) * SUB)
                p2 = ph.op("vector", lambda e, sl=sl: e.tensor_mul(ln1[:, sl], ln1[:, sl], pbc[1][:, :]),
                           inc="vE" if t == DT - 1 else None)
            ph.wait("vector", "vE", p2)
            if i >= 1:
                ph.wait("vector", "XLNS", xln_stored[i - 1])
            for t in range(DT):
                sl = slice(t * SUB, (t + 1) * SUB)
                xln_done[i] = ph.op("vector", lambda e, x=xcur, sl=sl: e.tensor_add(xlnb[:, sl], x[:, sl], ln1[:, sl]),
                                    inc="vE" if t == DT - 1 else None)

        def em_uvz(i, tag, jlist):
            wt = wu if tag == "u" else wz
            dstbuf = ub if tag == "u" else zb
            hcur = hb[i % 2]
            ph.wait("tensor", "vE", h_done[i])
            for j in jlist:
                bank = (0 if tag == "u" else IT) + j
                bank = bank % 4
                pg = pA[bank]
                prev = bank_hist.get(bank)
                if prev is not None:
                    ph.wait("tensor", "sE", evac_cnt[prev])
                g = None
                for k in range(DT):
                    if i == 0:
                        n = (0 if tag == "u" else 2) + k // 4
                        ph.wait("tensor", "qW0", wq_cnt[n])
                    g = ph.op("tensor", lambda e, wt=wt, j=j, k=k, pg=pg, h=hcur: e.matmul(
                        pg[:, :], wt[:, k * INNER + j * 128: k * INNER + j * 128 + 128],
                        h[:, k * SUB:(k + 1) * SUB], start=(k == 0), stop=(k == DT - 1)),
                        inc="tE" if k == DT - 1 else None)
                grp_cnt[(i, tag, j)] = g
                ph.wait("scalar", "tE", g)
                if tag == "u":
                    if j == 0 and i >= 1:
                        ph.wait("scalar", "USA", ustoreA[i - 1])
                    if j == 8 and i >= 1:
                        ph.wait("scalar", "USB", ustoreB[i - 1])
                    col = j * SUB
                else:
                    if j == 0 and i >= 1:
                        ph.wait("scalar", "ZSB", zstoreB[i - 1])
                    if j == 8:
                        ph.wait("scalar", "ZSA", zstoreA[i])
                    col = (j % 8) * SUB
                evac_cnt[(i, tag, j)] = ph.op("scalar", lambda e, d=dstbuf, col=col, pg=pg: e.activation(
                    d[:, col:col + SUB], pg[:, :], AF.Silu), inc="sE")
                bank_hist[bank] = (i, tag, j)
                if tag == "z" and j == 7:
                    ph.wait("scalar", "sE", evac_cnt[(i, "z", 7)])
                    apz = bass.AP(d_zsp, i * 128 * IT * SUB, [[IT * SUB, 128], [1, 8 * SUB]])
                    zstoreA[i] = ph.dma("scalar", apz, zb[:, :], "ZSA")

        def em_store_uz(i, tag, half):
            if tag == "z" and half == "A":
                return
            jend = 7 if half == "A" else 15
            ph.wait("sync", "sE", evac_cnt[(i, tag, jend)])
            cofs = 0 if half == "A" else 8 * SUB
            if tag == "u":
                srcb = ub[:, cofs:cofs + 8 * SUB]
                ap = bass.AP(d_usp, i * 128 * IT * SUB + cofs, [[IT * SUB, 128], [1, 8 * SUB]])
            else:
                srcb = zb[:, :]
                ap = bass.AP(d_zsp, i * 128 * IT * SUB + cofs, [[IT * SUB, 128], [1, 8 * SUB]])
            cntv = ph.dma("sync", ap, srcb, f"{tag.upper()}S{half}")
            {("u", "A"): ustoreA, ("u", "B"): ustoreB,
             ("z", "B"): zstoreB}[(tag, half)][i] = cntv

        em_weights()
        em_casts(0)
        em_stats(0)
        em_rows_a(0); em_rows_b(0); em_rows_c(0)
        em_bc(0)
        em_lnh_h(0)
        em_casts(1)
        em_lnh_rest(0)
        ph.wait("sync", "vE", xln_done[0])
        xln_stored[0] = ph.dma("sync", d_xlnsp.ap()[0, :, :], xlnb[:, :], "XLNS")

        for w in range(NSUB):
            em_xload(w + 2)
            nxt = w + 1 < NSUB
            if nxt and w > 0:
                em_stats(w + 1)
                em_rows_a(w + 1)
            em_uvz(w, "u", range(0, 4))
            if nxt and w == 0:
                em_stats(1)
                em_rows_a(1)
            if nxt:
                em_rows_b(w + 1)
            em_uvz(w, "u", range(4, 8))
            em_store_uz(w, "u", "A")
            if nxt:
                em_rows_c(w + 1)
            em_casts(w + 2)
            em_uvz(w, "u", range(8, 12))
            em_uvz(w, "u", range(12, IT))
            if nxt:
                em_bc(w + 1)
            em_store_uz(w, "u", "B")
            if nxt:
                em_lnh_h(w + 1)
                em_lnh_rest(w + 1)
            em_uvz(w, "z", range(0, 8))
            em_store_uz(w, "z", "A")
            em_uvz(w, "z", range(8, IT))
            em_store_uz(w, "z", "B")
            if nxt:
                ph.wait("sync", "vE", xln_done[w + 1])
                xln_stored[w + 1] = ph.dma("sync", d_xlnsp.ap()[w + 1, :, :], xlnb[:, :], "XLNS")
            if w >= 2:
                em_cache_some(2, gate_cnt=grp_cnt[(w, "u", 7)])

        em_cache_some(len(chunks))
        ph.run()

    wout_ctx = nc.sbuf_tensor("wout", [128, IT * D], BF16)
    wout = wout_ctx.__enter__()
    wout_cnt = {}
    szin0_ctx = nc.sbuf_tensor("szin0", [128, IT * SUB], BF16)
    szin0 = szin0_ctx.__enter__()
    p1c_pre = {}
    with ExitStack() as st:
        wdt = st.enter_context(nc.sbuf_tensor("wdt", [128, IT * INNER], BF16))
        uin = [st.enter_context(nc.sbuf_tensor(f"uin{i}", [128, IT * SUB], BF16)) for i in range(2)]
        zin = [st.enter_context(nc.sbuf_tensor(f"zin{i}", [128, IT * SUB], BF16)) for i in range(2)]
        szb = st.enter_context(nc.sbuf_tensor("szb", [128, 8 * SUB], BF16))
        lamt = [st.enter_context(nc.sbuf_tensor(f"lamt{i}", [128, SUB], F32)) for i in range(2)]
        lampt = [st.enter_context(nc.sbuf_tensor(f"lampt{i}", [128, SUB], F32)) for i in range(2)]
        bbt = [st.enter_context(nc.sbuf_tensor(f"bbt{i}", [128, SUB], F32)) for i in range(2)]
        sst = [st.enter_context(nc.sbuf_tensor(f"sst{i}", [128, SUB], F32)) for i in range(2)]
        pA = [st.enter_context(nc.psum_tensor(f"pBA{i}", [128, SUB], F32)) for i in range(8)]

        ph = Phase(B)
        wq_cnt = {}
        ph.wait("gpsimd", "qWC", cache_ready["wdt"])
        for g in range(8):
            srcw = bass.AP(d_wdt_b, g * 2 * 128 * INNER, [[INNER, 128], [128 * INNER, 2], [1, INNER]])
            dstw = bass.AP(wdt, g * 2 * INNER, [[IT * INNER, 128], [INNER, 2], [1, INNER]])
            wq_cnt[g] = ph.dma("gpsimd", dstw, srcw, f"qW0{'b' * (g % 2)}")
        ph.wait("gpsimd", "qWC", cache_ready["wout"])
        for hh in range(2):
            srcap = bass.AP(d_wout_b, hh * 8 * 128 * D, [[D, 128], [128 * D, 8], [1, D]])
            dstap = bass.AP(wout, hh * 8 * D, [[IT * D, 128], [D, 8], [1, D]])
            wout_cnt[hh] = ph.dma("gpsimd", dstap, srcap, f"qW{4 + hh}")

        u_loaded, z_loaded, grp, evac = {}, {}, {}, {}
        b_done, scan_done, sz_done, szA_stored, szB_stored = {}, {}, {}, {}, {}

        half = IT * SUB // 2
        u_loaded[0] = ph.dma("sync", uin[0][:, 0:half], d_usp.ap()[0, :, 0:half], "UL0")
        u_half2 = ph.dma("scalar", uin[0][:, half:IT * SUB], d_usp.ap()[0, :, half:IT * SUB], "ULS")
        z_loaded[0] = ph.dma("sync", zin[0][:, :], d_zsp.ap()[0, :, :], "ZL0")
        if NSUB > 1:
            u_loaded[1] = ph.dma("sync", uin[1][:, :], d_usp.ap()[1, :, :], "UL1")
            z_loaded[1] = ph.dma("sync", zin[1][:, :], d_zsp.ap()[1, :, :], "ZL1")

        def em_lam_tail(i, j, pg, ucur, zcur, b):
            gj = i * IT + j
            g = grp[(i, j)]

            lj = j % 2
            ph.wait("scalar", "tE", g)
            if gj >= 2:
                ph.wait("scalar", "vE", scan_done[gj - 2])
                ph.wait("scalar", "pE", b_done[gj - 2])
            ph.op("scalar", lambda e, pg=pg, j=j, lj=lj: e.activation(
                lamt[lj][:, :], pg[:, :], AF.Sigmoid, bias=c_bdt[:, j:j + 1], scale=1.0))
            ev = ph.op("scalar", lambda e, pg=pg, j=j, lj=lj: e.activation(
                lampt[lj][:, :], pg[:, :], AF.Sigmoid, bias=c_nbdt[:, j:j + 1], scale=-1.0), inc="sE")
            evac[(i, j)] = ev

            ph.wait("gpsimd", "sE", ev)
            if gj >= 2:
                ph.wait("gpsimd", "vE", scan_done[gj - 2])
            b_done[gj] = ph.op("gpsimd", lambda e, lj=lj, u=ucur, j=j: e.tensor_tensor(
                bbt[lj][:, :], lampt[lj][:, :], u[:, j * SUB:(j + 1) * SUB], ALU.mult), inc="pE")

            ccol = b * IT + j
            ph.wait("vector", "pE", b_done[gj])
            if gj >= 2:
                ph.wait("vector", "vE", sz_done[gj - 2])
            sc = ph.op("vector", lambda e, lj=lj, ccol=ccol: e.tensor_tensor_scan(
                sst[lj][:, :], lamt[lj][:, :], bbt[lj][:, :], carry[:, ccol:ccol + 1],
                ALU.mult, ALU.add), inc="vE")
            scan_done[gj] = sc
            ph.wait("vector", "vE", sc)
            ph.op("vector", lambda e, lj=lj, ccol=ccol: e.tensor_copy(
                carry[:, ccol:ccol + 1], sst[lj][:, SUB - 1:SUB]), inc="vE")
            if j == 0 and i >= 1:
                ph.wait("vector", "SZSB", szB_stored[i - 1])
            if j == 8:
                ph.wait("vector", "SZSA", szA_stored[i])
            sz_done[gj] = ph.op("vector", lambda e, lj=lj, z=zcur, j=j: e.tensor_mul(
                szb[:, (j % 8) * SUB:(j % 8 + 1) * SUB], sst[lj][:, :], z[:, j * SUB:(j + 1) * SUB]), inc="vE")
            if j == 7:
                ph.wait("sync", "vE", sz_done[gj])
                apz = bass.AP(d_szsp, i * 128 * IT * SUB, [[IT * SUB, 128], [1, 8 * SUB]])
                szA_stored[i] = ph.dma("sync", apz, szb[:, :], "SZSA")

        for i in range(NSUB):
            c, b = divmod(i, n_b)
            ucur, zcur = uin[i % 2], zin[i % 2]
            ph.wait("tensor", f"UL{i % 2}", u_loaded[i])
            ph.wait("gpsimd", f"UL{i % 2}", u_loaded[i])
            if i == 0:
                ph.wait("tensor", "ULS", u_half2)
                ph.wait("gpsimd", "ULS", u_half2)
            ph.wait("vector", f"ZL{i % 2}", z_loaded[i])
            if i == 0:
                for g8 in range(8):
                    ph.wait("tensor", f"qW0{'b' * (g8 % 2)}", wq_cnt[g8])
                    for k in (2 * g8, 2 * g8 + 1):
                        for j in range(8):
                            gg = ph.op("tensor", lambda e, j=j, k=k, u=ucur: e.matmul(
                                pA[j][:, :], wdt[:, k * INNER + j * 128: k * INNER + j * 128 + 128],
                                u[:, k * SUB:(k + 1) * SUB], start=(k == 0), stop=(k == IT - 1)),
                                inc="tE" if k == IT - 1 else None)
                            if k == IT - 1:
                                grp[(0, j)] = gg
                for j in range(8):
                    em_lam_tail(0, j, pA[j], ucur, zcur, b)
                jstart = 8
            else:
                jstart = 0
            for j in range(jstart, IT):
                gj = i * IT + j
                pg = pA[j % 2]
                key = (i, j - 2) if j >= 2 else (i - 1, IT - 2 + j)
                if key in evac:
                    ph.wait("tensor", "sE", evac[key])
                g = None
                for k in range(IT):
                    g = ph.op("tensor", lambda e, j=j, k=k, pg=pg, u=ucur: e.matmul(
                        pg[:, :], wdt[:, k * INNER + j * 128: k * INNER + j * 128 + 128],
                        u[:, k * SUB:(k + 1) * SUB], start=(k == 0), stop=(k == IT - 1)),
                        inc="tE" if k == IT - 1 else None)
                grp[(i, j)] = g
                em_lam_tail(i, j, pg, ucur, zcur, b)

            ph.wait("sync", "vE", sz_done[i * IT + IT - 1])
            apz = bass.AP(d_szsp, i * 128 * IT * SUB + 8 * SUB, [[IT * SUB, 128], [1, 8 * SUB]])
            szB_stored[i] = ph.dma("sync", apz, szb[:, :], "SZSB")
            if i + 2 < NSUB:
                ph.wait("sync", "pE", b_done[i * IT + IT - 1])
                u_loaded[i + 2] = ph.dma("sync", uin[i % 2][:, :], d_usp.ap()[i + 2, :, :], f"UL{i % 2}")
                z_loaded[i + 2] = ph.dma("sync", zin[i % 2][:, :], d_zsp.ap()[i + 2, :, :], f"ZL{i % 2}")

        ph.wait("sync", "SZSB", szB_stored[0])
        p1c_pre["sz"] = ph.dma("sync", szin0[:, :], d_szsp.ap()[0, :, :], "SZL0")
        ph.run()

    with ExitStack() as st:
        szin = [szin0, st.enter_context(nc.sbuf_tensor("szin1", [128, IT * SUB], BF16))]
        xlnin = [st.enter_context(nc.sbuf_tensor(f"xlnin{i}", [128, DT * SUB], F32)) for i in range(2)]
        xmb = [st.enter_context(nc.sbuf_tensor(f"xmb{i}", [128, DT * SUB], F32)) for i in range(3)]
        xmbf = st.enter_context(nc.sbuf_tensor("xmbf", [128, DT * SUB], BF16))
        xm2bf = st.enter_context(nc.sbuf_tensor("xm2bf", [128, DT * SUB], BF16))
        xmlo = st.enter_context(nc.sbuf_tensor("xmlo", [128, DT * SUB], BF16))
        xfb = st.enter_context(nc.sbuf_tensor("xfb", [128, DT * SUB], BF16))
        t1b = st.enter_context(nc.sbuf_tensor("t1b", [128, DT * SUB], BF16))
        rows = {nm: st.enter_context(nc.sbuf_tensor(f"rc_{nm}", [1, SUB], F32))
                for nm in ["m", "msq", "t0", "v"]}
        rows["sd1"] = rows["t0"]
        rows2 = []
        for p in range(2):
            d = {nm: st.enter_context(nc.sbuf_tensor(f"rc2_{nm}{p}", [1, SUB], BF16))
                 for nm in ["r2b", "mr2", "mrow"]}
            rows2.append(d)
        for nm in ["pg", "r2", "thr"]:
            rows[nm] = st.enter_context(nc.sbuf_tensor(f"rc_{nm}", [1, SUB], F32))
        pstat = st.enter_context(nc.psum_tensor("pstatc", [128, SUB], F32))
        pgate = st.enter_context(nc.psum_tensor("pgate", [1, SUB], F32))
        pbc = [st.enter_context(nc.psum_tensor(f"pbcc{i}", [128, SUB], F32)) for i in range(4)]
        pA = [st.enter_context(nc.psum_tensor(f"pCA{i}", [128, SUB], F32)) for i in range(2)]

        ph = Phase(B)
        wq_cnt = wout_cnt

        sz_loaded, xln_loaded, grp, ev = {}, {}, {}, {}
        stats_done, gate_done, lo_done, rows_done = {}, {}, {}, {}
        bc_done, xf_done, mrow_done, pg_read, cast_done = {}, {}, {}, {}, {}
        xm_stored, xf_stored, m_stored, cast_tile = {}, {}, {}, {}

        def em_loads(i):
            if i >= NSUB:
                return
            if i >= 2:
                ph.wait("sync", "tE", grp[(i - 2, DT - 1)])
            sz_loaded[i] = ph.dma("sync", szin[i % 2][:, :], d_szsp.ap()[i, :, :], f"SZL{i % 2}")
            if i >= 2:
                ph.wait("sync", "vE", ev[(i - 2, DT - 1)])
            xln_loaded[i] = ph.dma("sync", xlnin[i % 2][:, :], d_xlnsp.ap()[i, :, :], f"XLL{i % 2}")

        def em_y(i, olist):
            szc, xlnc, xmc = szin[i % 2], xlnin[i % 2], xmb[i % 3]
            if olist[0] == 0:
                ph.wait("tensor", f"SZL{i % 2}", sz_loaded[i])
            for o in olist:
                pg = pA[o % 2]
                key = (i, o - 2) if o >= 2 else (i - 1, DT - 2 + o)
                if key in ev:
                    ph.wait("tensor", "vE", ev[key])
                g = None
                for k in range(IT):
                    if i == 0 and o == 0:
                        ph.wait("tensor", f"qW{4 + k // 8}", wq_cnt[k // 8])
                    g = ph.op("tensor", lambda e, o=o, k=k, pg=pg, s=szc: e.matmul(
                        pg[:, :], wout[:, k * D + o * 128: k * D + o * 128 + 128],
                        s[:, k * SUB:(k + 1) * SUB], start=(k == 0), stop=(k == IT - 1)),
                        inc="tE" if k == IT - 1 else None)
                grp[(i, o)] = g
                ph.wait("vector", "tE", g)
                if o == 0:
                    ph.wait("vector", f"XLL{i % 2}", xln_loaded[i])
                    if i >= 3:
                        ph.wait("vector", "XMS", xm_stored[i - 3])
                        ph.wait("vector", "sE", cast_done[i - 3])
                        ph.wait("vector", "vE", xf_done[i - 3])
                sl = slice(o * SUB, (o + 1) * SUB)
                ev[(i, o)] = ph.op("vector", lambda e, pg=pg, sl=sl, xl=xlnc, xm=xmc: e.tensor_add(
                    xm[:, sl], pg[:, :], xl[:, sl]), inc="vE")

        def em_casts(i):
            xmc = xmb[i % 3]
            for o in range(DT):
                sl = slice(o * SUB, (o + 1) * SUB)
                ph.wait("scalar", "vE", ev[(i, o)])
                if o == 0 and i >= 1:
                    ph.wait("scalar", "tE", gate_done[i - 1])
                ph.op("scalar", lambda e, sl=sl, xm=xmc: e.copy(xmbf[:, sl], xm[:, sl]))
                cd = ph.op("scalar", lambda e, sl=sl, xm=xmc: e.activation(xm2bf[:, sl], xm[:, sl], AF.Square), inc="sE")
                cast_tile[(i, o)] = cd
                ph.wait("gpsimd", "sE", cd)
                if o == 0 and i >= 1:
                    ph.wait("gpsimd", "tE", gate_done[i - 1])
                lo_done[i] = ph.op("gpsimd", lambda e, sl=sl, xm=xmc: e.tensor_sub(
                    xmlo[:, sl], xm[:, sl], xmbf[:, sl]), inc="pE")
            cast_done[i] = cast_tile[(i, DT - 1)]

        def em_statsgate(i):
            if i >= 1:
                ph.wait("tensor", "vE", rows_done[i - 1])
            for t in range(DT):
                ph.wait("tensor", "sE", cast_tile[(i, t)])
                ph.op("tensor", lambda e, t=t: e.matmul(
                    pstat[0:1, :], ones_col[:, :], xmbf[:, t * SUB:(t + 1) * SUB],
                    start=(t == 0), stop=(t == DT - 1)))
            for t in range(DT):
                stats_done[i] = ph.op("tensor", lambda e, t=t: e.matmul(
                    pstat[32:33, :], ones_col[:, :], xm2bf[:, t * SUB:(t + 1) * SUB],
                    start=(t == 0), stop=(t == DT - 1)), inc="tE" if t == DT - 1 else None)
            ph.wait("tensor", "pE", lo_done[i])
            if i >= 1:
                ph.wait("tensor", "sE", pg_read[i - 1])
            ngate = 0
            for lhs, rhs in [(c_dwhi, xmbf), (c_dwhi, xmlo), (c_dwlo, xmbf)]:
                for t in range(DT):
                    ngate += 1
                    gate_done[i] = ph.op("tensor", lambda e, lhs=lhs, rhs=rhs, t=t,
                                         stt=(ngate == 1), spp=(ngate == 24): e.matmul(
                        pgate[:, :], lhs[:, t:t + 1], rhs[:, t * SUB:(t + 1) * SUB],
                        start=stt, stop=spp), inc="tE" if ngate == 24 else None)

        def em_rows(i):
            rp = rows2[i % 2]
            ph.wait("vector", "tE", stats_done[i])
            ph.op("vector", lambda e: e.tensor_single_scalar(rows["m"][:, :], pstat[0:1, :], 1.0 / D, ALU.mult))
            c1 = ph.op("vector", lambda e: e.tensor_single_scalar(rows["msq"][:, :], pstat[32:33, :], 1.0 / D, ALU.mult), inc="vE")
            ph.wait("vector", "vE", c1)
            c2 = ph.op("vector", lambda e: e.tensor_mul(rows["t0"][:, :], rows["m"][:, :], rows["m"][:, :]), inc="vE")
            ph.wait("vector", "vE", c2)
            v_v = ph.op("vector", lambda e: e.tensor_sub(rows["v"][:, :], rows["msq"][:, :], rows["t0"][:, :]), inc="vE")
            ph.wait("scalar", "vE", v_v)
            s_l = ph.op("scalar", lambda e: e.activation(rows["sd1"][:, :], rows["v"][:, :], AF.Ln, bias=c_eps[:, :], scale=1.0), inc="sE")
            ph.wait("scalar", "sE", s_l)
            s_sq = ph.op("scalar", lambda e: e.activation(rows["r2"][:, :], rows["sd1"][:, :], AF.Exp, scale=-0.5), inc="sE")
            ph.wait("vector", "sE", s_sq)
            ph.op("vector", lambda e, rp=rp: e.tensor_copy(rp["r2b"][:, :], rows["r2"][:, :]))
            ph.op("vector", lambda e, rp=rp: e.tensor_mul(rp["mr2"][:, :], rows["m"][:, :], rows["r2"][:, :]))
            thr_d = ph.op("vector", lambda e: e.tensor_scalar(rows["thr"][:, :], rows["m"][:, :], r_sdw[:, :], None, ALU.mult), inc="vE")
            rows_done[i] = thr_d
            ph.wait("scalar", "tE", gate_done[i])
            if i >= 1:
                ph.wait("scalar", "vE", mrow_done[i - 1])
            pg_read[i] = ph.op("scalar", lambda e: e.copy(rows["pg"][:, :], pgate[:, :]), inc="sE")
            ph.wait("vector", "sE", pg_read[i])
            ph.wait("vector", "vE", thr_d)
            if i >= 2:
                ph.wait("vector", "MS", m_stored[i - 2])
            mrow_done[i] = ph.op("vector", lambda e, rp=rp: e.tensor_tensor(
                rp["mrow"][:, :], rows["pg"][:, :], rows["thr"][:, :], ALU.is_ge), inc="vE")

        def em_bc(i):
            rp = rows2[i % 2]
            pb = pbc[2 * (i % 2):2 * (i % 2) + 2]
            ph.wait("tensor", "vE", rows_done[i])
            if i >= 2:
                ph.wait("tensor", "vE", xf_done[i - 2])
            ph.op("tensor", lambda e, rp=rp, pb=pb: e.matmul(pb[0][:, :], ones_rowb[:, :], rp["r2b"][:, :], start=True, stop=True))
            bc_done[i] = ph.op("tensor", lambda e, rp=rp, pb=pb: e.matmul(pb[1][:, :], ones_rowb[:, :], rp["mr2"][:, :], start=True, stop=True), inc="tE")

        def em_xf_t1b(i):
            xmc = xmb[i % 3]
            pb = pbc[2 * (i % 2):2 * (i % 2) + 2]
            ph.wait("vector", "tE", bc_done[i])
            if i >= 1:
                ph.wait("vector", "vE", xf_done[i - 1])
            for t in range(DT):
                sl = slice(t * SUB, (t + 1) * SUB)
                xf_done[("t1b", i)] = ph.op("vector", lambda e, sl=sl, xm=xmc, pb=pb: e.tensor_mul(
                    t1b[:, sl], xm[:, sl], pb[0][:, :]), inc="vE" if t == DT - 1 else None)

        def em_xf_sub(i):
            pb = pbc[2 * (i % 2):2 * (i % 2) + 2]
            ph.wait("vector", "vE", xf_done[("t1b", i)])
            if i >= 1:
                ph.wait("vector", "XFS", xf_stored[i - 1])
            for t in range(DT):
                sl = slice(t * SUB, (t + 1) * SUB)
                xf_done[i] = ph.op("vector", lambda e, sl=sl, pb=pb: e.tensor_sub(xfb[:, sl], t1b[:, sl], pb[1][:, :]),
                                   inc="vE" if t == DT - 1 else None)

        def em_stores(i):
            rp = rows2[i % 2]
            xmc = xmb[i % 3]
            ph.wait("sync", "vE", xf_done[i])
            ph.wait("sync", "vE", mrow_done[i])
            xm_stored[i] = ph.dma("sync", d_xmsp.ap()[i, :, :], xmc[:, :], "XMS")
            xf_stored[i] = ph.dma("sync", d_xfsp.ap()[i, :, :], xfb[:, :], "XFS")
            m_stored[i] = ph.dma("sync", d_msp.ap()[i, :, :], rp["mrow"][:, :], "MS")

        sz_loaded[0] = p1c_pre["sz"]
        xln_loaded[0] = ph.dma("sync", xlnin[0][:, :], d_xlnsp.ap()[0, :, :], "XLL0")
        em_loads(1)
        for w in range(NSUB + 2):
            a, b, c = w, w - 1, w - 2
            if 0 <= c < NSUB:
                em_bc(c)
            if 0 <= b < NSUB:
                em_statsgate(b)
                em_rows(b)
            if a < NSUB:
                em_y(a, list(range(0, 4)))
            if 0 <= c < NSUB:
                em_xf_t1b(c)
            if a < NSUB:
                em_y(a, list(range(4, DT)))
            if 0 <= c < NSUB:
                em_xf_sub(c)
            if a < NSUB:
                em_casts(a)
            if 0 <= c < NSUB:
                em_stores(c)
            if a < NSUB:
                em_loads(a + 2)

        ph.run()

    szin0_ctx.__exit__(None, None, None)
    wout_ctx.__exit__(None, None, None)

    p2w = {}

    def ffn_branch(which, next_key=None, next_dW1=None):
        with ExitStack() as st:
            if "wA" not in p2w:
                p2w["wA"] = prog.enter_context(nc.sbuf_tensor("wA_p2", [128, DT * HID], BF16))
                p2w["wB"] = prog.enter_context(nc.sbuf_tensor("wB_p2", [128, HT * D], BF16))
                p2w["xf0"] = prog.enter_context(nc.sbuf_tensor("xf0_p2", [128, DT * SUB], BF16))
                p2w["mr0"] = prog.enter_context(nc.sbuf_tensor("mr0_p2", [1, SUB], BF16))
            wA, wB = p2w["wA"], p2w["wB"]
            hd = st.enter_context(nc.sbuf_tensor(f"hd_{which}", [128, HT * SUB], BF16))
            xfin = [p2w["xf0"],
                    st.enter_context(nc.sbuf_tensor(f"xfin_{which}1", [128, DT * SUB], BF16))]
            xr = st.enter_context(nc.sbuf_tensor(f"xr_{which}", [128, DT * SUB], BF16)) if which != "sh" else None
            outst = [st.enter_context(nc.sbuf_tensor(f"outst_{which}{i}", [128, SUB], F32)) for i in range(2)]
            accin = [st.enter_context(nc.sbuf_tensor(f"acc_{which}{i}", [128, SUB], F32)) for i in range(2)] if which != "e0" else None
            xmin = [st.enter_context(nc.sbuf_tensor(f"xmin_{which}{i}", [128, SUB], F32)) for i in range(2)] if which == "sh" else None
            axm = [st.enter_context(nc.sbuf_tensor(f"axm_{which}{i}", [128, SUB], F32)) for i in range(2)] if which == "sh" else None
            mr = [p2w["mr0"],
                  st.enter_context(nc.sbuf_tensor(f"mr_{which}1", [1, SUB], BF16))] if which != "sh" else None
            thrr = st.enter_context(nc.sbuf_tensor(f"thr_{which}", [1, SUB], BF16)) if which == "e1" else None
            pA = [st.enter_context(nc.psum_tensor(f"pm1_{which}{i}", [128, SUB], F32)) for i in range(2)]
            pB = [st.enter_context(nc.psum_tensor(f"pm2_{which}{i}", [128, SUB], F32)) for i in range(2)]
            pM = st.enter_context(nc.psum_tensor(f"pM_{which}", [128, SUB], F32)) if which != "sh" else None

            if which == "e0":
                dW1, dW2, bias1, bias2 = d_w1_b[0], d_w2_b[0], c_b1[0], c_b2[0]
                ck1, ck2 = "w1_0", "w2_0"
            elif which == "e1":
                dW1, dW2, bias1, bias2 = d_w1_b[1], d_w2_b[1], c_b1[1], c_b2[1]
                ck1, ck2 = "w1_1", "w2_1"
            else:
                dW1, dW2, bias1, bias2 = d_sw1_b, d_sw2_b, c_sb1, c_sb2
                ck1, ck2 = "sw1", "sw2"

            ph = Phase(B)
            wqb = {}
            while late_cache and late_cache[0][0] in (ck1, ck2):
                nm, dap, sap = late_cache.pop(0)
                cache_ready[nm] = ph.dma("gpsimd", dap, sap, "qWC")
            if "wa_next" in p2w:
                wqa = p2w.pop("wa_next")
                wqa_jmajor = False
            else:
                wqa_jmajor = True
                wqa = {}
                ph.wait("gpsimd", "qWC", cache_ready[ck1])
                for jg in range(8):
                    srcap = bass.AP(dW1, jg * 512, [[HID, 128], [128 * HID, DT], [1, 512]])
                    dstap = bass.AP(wA, jg * 512, [[DT * HID, 128], [HID, DT], [1, 512]])
                    wqa[jg] = ph.dma("gpsimd", dstap, srcap, "qWA")
            xf_loaded, xf_loaded2, m_loaded = {}, {}, {}
            if wqa_jmajor:
                ph.wait("scalar", "qWC", cache_ready[ck2])
                hh = DT * SUB // 2
                xf_loaded2[0] = ph.dma("scalar", xfin[0][:, hh:DT * SUB],
                                       d_xfsp.ap()[0, :, hh:DT * SUB], "PXFS")
            else:
                ph.wait("gpsimd", "qWC", cache_ready[ck2])
            for q in range(4):
                srcap = bass.AP(dW2, q * 8 * 128 * D, [[D, 128], [128 * D, 8], [1, D]])
                dstap = bass.AP(wB, q * 8 * D, [[HT * D, 128], [D, 8], [1, D]])
                if wqa_jmajor:
                    wqb[q] = ph.dma("scalar", dstap, srcap, f"qWB{q}", gate=False)
                else:
                    wqb[q] = ph.dma("gpsimd", dstap, srcap, f"qW{q}")

            mb_done, thr_done, xr_done, hd_ev, ax_done = {}, {}, {}, {}, {}
            m1_grp, m2_grp, out_done, out_stored = {}, {}, {}, {}

            def em_xfload(i):
                if i >= NSUB:
                    return
                if i == 0 and "xf0_cnt" in p2w:
                    xf_loaded[0] = p2w.pop("xf0_cnt")
                    if which != "sh":
                        m_loaded[0] = p2w.pop("m0_cnt")
                    return
                if i == 0 and wqa_jmajor:
                    m_loaded[0] = ph.dma("sync", mr[0][:, :], d_msp.ap()[0, :, :], "PM0")
                    hh = DT * SUB // 2
                    xf_loaded[0] = ph.dma("sync", xfin[0][:, 0:hh], d_xfsp.ap()[0, :, 0:hh], "PXF0")
                    return
                if i >= 2:
                    if which == "sh":
                        ph.wait("sync", "tE", m1_grp[(i - 2, HT - 1)])
                    else:
                        ph.wait("sync", "vE", xr_done[i - 2])
                xf_loaded[i] = ph.dma("sync", xfin[i % 2][:, :], d_xfsp.ap()[i, :, :], f"PXF{i % 2}")
                if which != "sh":
                    m_loaded[i] = ph.dma("sync", mr[i % 2][:, :], d_msp.ap()[i, :, :], f"PM{i % 2}")

            def em_mask(i):
                mrow = mr[i % 2]
                if which == "e1":
                    ph.wait("vector", f"PM{i % 2}", m_loaded[i])
                    if i >= 1:
                        ph.wait("vector", "tE", mb_done[i - 1])
                    thr_done[i] = ph.op("vector", lambda e, r=mrow: e.tensor_scalar(
                        thrr[:, :], r[:, :], -1.0, 1.0, ALU.mult, ALU.add), inc="vE")
                    brow = thrr
                    ph.wait("tensor", "vE", thr_done[i])
                else:
                    brow = mrow
                    ph.wait("tensor", f"PM{i % 2}", m_loaded[i])
                if i >= 1:
                    ph.wait("tensor", "vE", xr_done[i - 1])
                mb_done[i] = ph.op("tensor", lambda e, r=brow: e.matmul(
                    pM[:, :], ones_rowb[:, :], r[:, :], start=True, stop=True), inc="tE")
                ph.wait("vector", "tE", mb_done[i])
                ph.wait("vector", f"PXF{i % 2}", xf_loaded[i])
                if i in xf_loaded2:
                    ph.wait("vector", "PXFS", xf_loaded2[i])
                if i >= 1:
                    ph.wait("vector", "tE", m1_grp[(i - 1, HT - 1)])
                xfc = xfin[i % 2]
                for t in range(DT):
                    sl = slice(t * SUB, (t + 1) * SUB)
                    xr_done[i] = ph.op("vector", lambda e, x=xfc, sl=sl: e.tensor_mul(
                        xr[:, sl], x[:, sl], pM[:, :]), inc="vE" if t == DT - 1 else None)

            def em_m1(i):
                rhs = xfin[i % 2] if which == "sh" else xr
                if which == "sh":
                    ph.wait("tensor", f"PXF{i % 2}", xf_loaded[i])
                else:
                    ph.wait("tensor", "vE", xr_done[i])
                for j in range(HT):
                    pg = pA[j % 2]
                    if i == 0 and wqa_jmajor and j % 4 == 0:
                        ph.wait("tensor", "qWA", wqa[j // 4])
                    key = (i, j - 2) if j >= 2 else (i - 1, HT - 2 + j)
                    if key in hd_ev:
                        ph.wait("tensor", "sE", hd_ev[key])
                    g = None
                    for k in range(DT):
                        if i == 0 and j == 0 and not wqa_jmajor:
                            ph.wait("tensor", f"qW{k // 2}", wqa[k // 2])
                        g = ph.op("tensor", lambda e, j=j, k=k, pg=pg, r=rhs: e.matmul(
                            pg[:, :], wA[:, k * HID + j * 128: k * HID + j * 128 + 128],
                            r[:, k * SUB:(k + 1) * SUB], start=(k == 0), stop=(k == DT - 1)),
                            inc="tE" if k == DT - 1 else None)
                    m1_grp[(i, j)] = g
                    ph.wait("scalar", "tE", g)
                    if i >= 1 and j < 2:
                        ph.wait("scalar", "tE", m2_grp[(i - 1, DT - 1)])
                    hd_ev[(i, j)] = ph.op("scalar", lambda e, j=j, pg=pg, b1=bias1: e.activation(
                        hd[:, j * SUB:(j + 1) * SUB], pg[:, :], AF.Relu, bias=b1[:, j:j + 1]), inc="sE")

            def em_m2out(i):
                c, b = divmod(i, n_b)
                for o in range(DT):
                    pg = pB[o % 2]
                    oc = o % 2
                    go = i * DT + o
                    key = (i, o - 2) if o >= 2 else (i - 1, DT - 2 + o)
                    if key in out_done:
                        ph.wait("tensor", "vE", out_done[key])
                    g = None
                    for k in range(HT):
                        if i == 0 and o == 0:
                            ph.wait("tensor", f"qWB{k // 8}" if wqa_jmajor else f"qW{k // 8}", wqb[k // 8])
                        if o == 0 and k == 0:
                            ph.wait("tensor", "sE", hd_ev[(i, HT - 2)])
                        if o == 0 and k == HT - 1:
                            ph.wait("tensor", "sE", hd_ev[(i, HT - 1)])
                        g = ph.op("tensor", lambda e, o=o, k=k, pg=pg: e.matmul(
                            pg[:, :], wB[:, k * D + o * 128: k * D + o * 128 + 128],
                            hd[:, k * SUB:(k + 1) * SUB], start=(k == 0), stop=(k == HT - 1)),
                            inc="tE" if k == HT - 1 else None)
                    m2_grp[(i, o)] = g

                    la = lx = None
                    if which != "e0":
                        d_prev = d_acc0 if which == "e1" else d_acc1
                        la = ph.dma("sync", accin[oc][:, :], bass.AP(
                            d_prev, i * 128 * DT * SUB + o * SUB, [[DT * SUB, 128], [1, SUB]]), f"PAC{oc}")
                    if which == "sh":
                        if key in ax_done:
                            ph.wait("scalar", "pE", ax_done[key])
                        lx = ph.dma("scalar", xmin[oc][:, :], bass.AP(
                            d_xmsp, i * 128 * DT * SUB + o * SUB, [[DT * SUB, 128], [1, SUB]]), f"PXM{oc}")

                    ph.wait("vector", "tE", g)
                    if go >= 2:
                        ph.wait("vector", f"POS{oc}", out_stored[go - 2])
                    if which == "e0":
                        out_done[(i, o)] = ph.op("vector", lambda e, pg=pg, o=o, oc=oc, b2=bias2: e.tensor_scalar(
                            outst[oc][:, :], pg[:, :], b2[:, o:o + 1], None, ALU.add), inc="vE")
                    elif which == "e1":
                        ph.wait("vector", f"PAC{oc}", la)
                        out_done[(i, o)] = ph.op("vector", lambda e, pg=pg, o=o, oc=oc, b2=bias2: e.scalar_tensor_tensor(
                            outst[oc][:, :], pg[:, :], b2[:, o:o + 1], accin[oc][:, :], ALU.add, ALU.add), inc="vE")
                    else:
                        ph.wait("gpsimd", f"PAC{oc}", la)
                        ph.wait("gpsimd", f"PXM{oc}", lx)
                        if key in out_done:
                            ph.wait("gpsimd", "vE", out_done[key])
                        ax_done[(i, o)] = ph.op("gpsimd", lambda e, oc=oc: e.tensor_add(
                            axm[oc][:, :], accin[oc][:, :], xmin[oc][:, :]), inc="pE")
                        ph.wait("vector", "pE", ax_done[(i, o)])
                        out_done[(i, o)] = ph.op("vector", lambda e, pg=pg, o=o, oc=oc, b2=bias2: e.scalar_tensor_tensor(
                            outst[oc][:, :], pg[:, :], b2[:, o:o + 1], axm[oc][:, :], ALU.add, ALU.add), inc="vE")

                    ph.wait("sync", "vE", out_done[(i, o)])
                    if which == "e0":
                        dstap = bass.AP(d_acc0, i * 128 * DT * SUB + o * SUB, [[DT * SUB, 128], [1, SUB]])
                    elif which == "e1":
                        dstap = bass.AP(d_acc1, i * 128 * DT * SUB + o * SUB, [[DT * SUB, 128], [1, SUB]])
                    else:
                        dstap = bass.AP(d_out, (o * 128) * n_b * K + b * K + c * SUB, [[n_b * K, 128], [1, SUB]])
                    out_stored[go] = ph.dma("sync", dstap, outst[oc][:, :], f"POS{oc}")

            em_xfload(0)
            em_xfload(1)
            if which != "sh":
                em_mask(0)
            for w in range(NSUB):
                em_m1(w)
                if which != "sh" and late_cache:
                    ph.wait("gpsimd", "tE", m1_grp[(w, 0)])
                    for _ in range(2):
                        if late_cache:
                            nm, dap, sap = late_cache.pop(0)
                            cache_ready[nm] = ph.dma("gpsimd", dap, sap, "qWC")
                if which != "sh" and w + 1 < NSUB:
                    em_mask(w + 1)
                em_m2out(w)
                em_xfload(w + 2)

            if which != "sh":
                ph.wait("sync", "vE", xr_done[NSUB - 2])
                p2w["xf0_cnt"] = ph.dma("sync", xfin[0][:, :], d_xfsp.ap()[0, :, :], "PXF0")
                if which == "e0":
                    ph.wait("sync", "tE", mb_done[NSUB - 2])
                    p2w["m0_cnt"] = ph.dma("sync", mr[0][:, :], d_msp.ap()[0, :, :], "PM0")

            if next_dW1 is not None:
                ph.wait("gpsimd", "tE", m1_grp[(NSUB - 1, HT - 1)])
                ph.wait("gpsimd", "qWC", cache_ready[next_key])
                nxt = {}
                for q in range(4):
                    srcap = bass.AP(next_dW1, q * 2 * 128 * HID, [[HID, 128], [128 * HID, 2], [1, HID]])
                    dstap = bass.AP(wA, q * 2 * HID, [[DT * HID, 128], [HID, 2], [1, HID]])
                    nxt[q] = ph.dma("gpsimd", dstap, srcap, f"qW{q}")
                p2w["wa_next"] = nxt
            for s in ["POS0", "POS1"]:
                if s in B.cnt:
                    ph.wait("sync", s, B.cnt[s])
            ph.run()

    ffn_branch("e0", next_key="w1_1", next_dW1=d_w1_b[1])
    ffn_branch("e1", next_key="sw1", next_dW1=d_sw1_b)
    ffn_branch("sh")

    ph = Phase(B)
    for s in ["POS0", "POS1", "qW"]:
        if s in B.cnt:
            ph.wait("sync", s, B.cnt[s])
    ph.run()

    prog.close()
    return nc


def _cols(v, nt):
    return np.ascontiguousarray(np.asarray(v, dtype=np.float32).reshape(nt, 128).T)


def kernel(**inp):
    x = np.asarray(inp["L0_attn_x"], dtype=np.float32)
    W_in = np.asarray(inp["L0_attn_W_in"], dtype=np.float32)
    W_dt = np.asarray(inp["L0_attn_W_dt"], dtype=np.float32)
    W_out = np.asarray(inp["L0_attn_W_out"], dtype=np.float32)
    gate_w = np.asarray(inp["L0_ffn_gate_w"], dtype=np.float32)
    w1 = np.asarray(inp["L0_ffn_w1"], dtype=np.float32)
    w2 = np.asarray(inp["L0_ffn_w2"], dtype=np.float32)
    b1 = np.asarray(inp["L0_ffn_b1"], dtype=np.float32)
    b2 = np.asarray(inp["L0_ffn_b2"], dtype=np.float32)

    shared = {
        "w_in_uT": np.ascontiguousarray(W_in[:INNER].T),
        "w_in_zT": np.ascontiguousarray(W_in[INNER:].T),
        "w_dtT": np.ascontiguousarray(W_dt.T),
        "w_outT": np.ascontiguousarray(W_out.T),
        "w1_0": np.ascontiguousarray(w1[0]),
        "w1_1": np.ascontiguousarray(w1[1]),
        "w2_0": np.ascontiguousarray(w2[0]),
        "w2_1": np.ascontiguousarray(w2[1]),
        "sw1": np.ascontiguousarray(np.asarray(inp["L0_ffn_sw1"], dtype=np.float32)),
        "sw2": np.ascontiguousarray(np.asarray(inp["L0_ffn_sw2"], dtype=np.float32)),
        "wn_c": _cols(inp["L0_attn_w_norm"], DT),
        "bdt_c": _cols(inp["L0_attn_b_dt"], IT),
        "b1_0c": _cols(b1[0], HT),
        "b1_1c": _cols(b1[1], HT),
        "sb1_c": _cols(inp["L0_ffn_sb1"], HT),
        "b2_0c": _cols(b2[0], DT),
        "b2_1c": _cols(b2[1], DT),
        "sb2_c": _cols(inp["L0_ffn_sb2"], DT),
        "g0_c": _cols(gate_w[0], DT),
        "g1_c": _cols(gate_w[1], DT),
    }

    nc = build_nc()
    in_maps = []
    for c in range(8):
        m = dict(shared)
        m["xT"] = np.ascontiguousarray(x[4 * c:4 * c + 4].transpose(2, 0, 1))
        in_maps.append(m)
    res = run_bass_kernel_spmd(nc, in_maps, core_ids=list(range(8)))
    out = np.empty((32, 2048, 1024), np.float32)
    for c in range(8):
        out[4 * c:4 * c + 4] = np.asarray(res.results[c]["outT"]).transpose(1, 2, 0)
    return out


# version 36 (final)
# speedup vs baseline: 1.1306x; 1.0074x over previous
import sys
if "/opt/trn_rl_repo" not in sys.path:
    sys.path.insert(0, "/opt/trn_rl_repo")
from contextlib import ExitStack

import numpy as np
import concourse.bass as bass
import concourse.mybir as mybir
from concourse.bass_utils import run_bass_kernel_spmd

F32 = mybir.dt.float32
BF16 = mybir.dt.bfloat16
AF = mybir.ActivationFunctionType
ALU = mybir.AluOpType

D = 1024
INNER = 2048
HID = 4096
DT = D // 128
IT = INNER // 128
HT = HID // 128
SUB = 512
EPS = 1e-5

ENG_SEMS = ["tE", "sE", "vE", "pE"]


class Build:
    def __init__(self, nc, prog):
        self.nc = nc
        self.prog = prog
        self.cnt = {}
        self.sems = {}
        for s in ENG_SEMS:
            self._mk(s)

    def _mk(self, name):
        self.sems[name] = self.prog.enter_context(self.nc.semaphore(name))
        self.cnt[name] = 0

    def stream(self, name):
        if name not in self.sems:
            self._mk(name)
        return name


class Phase:
    def __init__(self, B: Build):
        self.B = B
        self.streams = {e: [] for e in ["sync", "gpsimd", "tensor", "scalar", "vector"]}
        self.dma_streams = set()

    def wait(self, eng, sem, cnt):
        if cnt is None or cnt <= 0:
            return
        h = self.B.sems[sem]
        self.streams[eng].append(lambda e, h=h, cnt=cnt: e.wait_ge(h, cnt))

    def op(self, eng, fn, inc=None, inc_n=1):
        if inc is None:
            self.streams[eng].append(fn)
            return None
        self.B.cnt[inc] += inc_n
        h = self.B.sems[inc]
        def wrapped(e, fn=fn, h=h, n=inc_n):
            fn(e).then_inc(h, n)
        self.streams[eng].append(wrapped)
        return self.B.cnt[inc]

    def dma(self, eng, dst, src, stream, gate=True):
        s = self.B.stream(stream)
        self.dma_streams.add(s)
        if gate and self.B.cnt[s] > 0:
            self.wait(eng, s, self.B.cnt[s])
        return self.op(eng, lambda e, d=dst, sr=src: e.dma_start(d, sr), inc=s, inc_n=16)

    def run(self, skip_drain=()):
        for s in sorted(self.dma_streams):
            if s in skip_drain:
                continue
            self.wait("sync", s, self.B.cnt[s])
        with self.B.nc.Block() as blk:
            for eng in ["sync", "gpsimd", "tensor", "scalar", "vector"]:
                ops = self.streams[eng]
                if not ops:
                    continue
                def body(e, ops=ops):
                    for f in ops:
                        f(e)
                getattr(blk, eng)(body)


def build_nc(n_b=4, n_c=4):
    K = n_c * SUB
    NSUB = n_b * n_c
    nc = bass.Bass("TRN2", target_bir_lowering=False)

    d_x = nc.dram_tensor("xT", [D, n_b, K], F32, kind="ExternalInput")
    d_wu = nc.dram_tensor("w_in_uT", [D, INNER], F32, kind="ExternalInput")
    d_wz = nc.dram_tensor("w_in_zT", [D, INNER], F32, kind="ExternalInput")
    d_wdt = nc.dram_tensor("w_dtT", [INNER, INNER], F32, kind="ExternalInput")
    d_wout = nc.dram_tensor("w_outT", [INNER, D], F32, kind="ExternalInput")
    d_w1 = [nc.dram_tensor(f"w1_{e}", [D, HID], F32, kind="ExternalInput") for e in range(2)]
    d_w2 = [nc.dram_tensor(f"w2_{e}", [HID, D], F32, kind="ExternalInput") for e in range(2)]
    d_sw1 = nc.dram_tensor("sw1", [D, HID], F32, kind="ExternalInput")
    d_sw2 = nc.dram_tensor("sw2", [HID, D], F32, kind="ExternalInput")
    d_wn = nc.dram_tensor("wn_c", [128, DT], F32, kind="ExternalInput")
    d_bdt = nc.dram_tensor("bdt_c", [128, IT], F32, kind="ExternalInput")
    d_b1 = [nc.dram_tensor(f"b1_{e}c", [128, HT], F32, kind="ExternalInput") for e in range(2)]
    d_sb1 = nc.dram_tensor("sb1_c", [128, HT], F32, kind="ExternalInput")
    d_b2 = [nc.dram_tensor(f"b2_{e}c", [128, DT], F32, kind="ExternalInput") for e in range(2)]
    d_sb2 = nc.dram_tensor("sb2_c", [128, DT], F32, kind="ExternalInput")
    d_g0 = nc.dram_tensor("g0_c", [128, DT], F32, kind="ExternalInput")
    d_g1 = nc.dram_tensor("g1_c", [128, DT], F32, kind="ExternalInput")
    d_out = nc.dram_tensor("outT", [D, n_b, K], F32, kind="ExternalOutput")

    d_usp = nc.dram_tensor("u_sp", [NSUB, 128, IT * SUB], BF16)
    d_zsp = nc.dram_tensor("z_sp", [NSUB, 128, IT * SUB], BF16)
    d_xlnsp = nc.dram_tensor("xln_sp", [NSUB, 128, DT * SUB], F32)
    d_szsp = nc.dram_tensor("sz_sp", [NSUB, 128, IT * SUB], BF16)
    d_xmsp = nc.dram_tensor("xm_sp", [NSUB, 128, DT * SUB], F32)
    d_xfsp = nc.dram_tensor("xf_sp", [NSUB, 128, DT * SUB], BF16)
    d_msp = nc.dram_tensor("m_sp", [NSUB, 1, SUB], BF16)
    d_acc0 = nc.dram_tensor("acc0_sp", [NSUB, 128, DT * SUB], F32)
    d_acc1 = nc.dram_tensor("acc1_sp", [NSUB, 128, DT * SUB], F32)
    d_wdt_b = nc.dram_tensor("wdt_b", [INNER, INNER], BF16)
    d_wout_b = nc.dram_tensor("wout_b", [INNER, D], BF16)
    d_w1_b = [nc.dram_tensor(f"w1b_{e}", [D, HID], BF16) for e in range(2)]
    d_w2_b = [nc.dram_tensor(f"w2b_{e}", [HID, D], BF16) for e in range(2)]
    d_sw1_b = nc.dram_tensor("sw1_b", [D, HID], BF16)
    d_sw2_b = nc.dram_tensor("sw2_b", [HID, D], BF16)
    cache_ready = {}
    late_cache = []

    prog = ExitStack()
    B = Build(nc, prog)

    c_wn = prog.enter_context(nc.sbuf_tensor("c_wn", [128, DT], F32))
    c_bdt = prog.enter_context(nc.sbuf_tensor("c_bdt", [128, IT], F32))
    c_nbdt = prog.enter_context(nc.sbuf_tensor("c_nbdt", [128, IT], F32))
    c_b1 = [prog.enter_context(nc.sbuf_tensor(f"c_b1_{e}", [128, HT], F32)) for e in range(2)]
    c_sb1 = prog.enter_context(nc.sbuf_tensor("c_sb1", [128, HT], F32))
    c_b2 = [prog.enter_context(nc.sbuf_tensor(f"c_b2_{e}", [128, DT], F32)) for e in range(2)]
    c_sb2 = prog.enter_context(nc.sbuf_tensor("c_sb2", [128, DT], F32))
    c_g0 = prog.enter_context(nc.sbuf_tensor("c_g0", [128, DT], F32))
    c_g1 = prog.enter_context(nc.sbuf_tensor("c_g1", [128, DT], F32))
    c_dwf = prog.enter_context(nc.sbuf_tensor("c_dwf", [128, DT], F32))
    c_dwhi = prog.enter_context(nc.sbuf_tensor("c_dwhi", [128, DT], BF16))
    c_dwhi32 = prog.enter_context(nc.sbuf_tensor("c_dwhi32", [128, DT], F32))
    c_dwlo = prog.enter_context(nc.sbuf_tensor("c_dwlo", [128, DT], BF16))
    c_dwsum = prog.enter_context(nc.sbuf_tensor("c_dwsum", [128, 1], F32))
    c_dwsumb = prog.enter_context(nc.sbuf_tensor("c_dwsumb", [128, 1], BF16))
    r_sdw = prog.enter_context(nc.sbuf_tensor("r_sdw", [1, 1], F32))
    ones_col = prog.enter_context(nc.sbuf_tensor("ones_col", [128, 1], BF16))
    ones_colf = prog.enter_context(nc.sbuf_tensor("ones_colf", [128, 1], F32))
    ones_rowf = prog.enter_context(nc.sbuf_tensor("ones_rowf", [1, 128], F32))
    ones_rowb = prog.enter_context(nc.sbuf_tensor("ones_rowb", [1, 128], BF16))
    c_eps = prog.enter_context(nc.sbuf_tensor("c_eps", [1, 1], F32))
    c_eps2 = prog.enter_context(nc.sbuf_tensor("c_eps2", [1, 1], F32))
    carry = prog.enter_context(nc.sbuf_tensor("carry", [128, n_b * IT], F32))

    st_p1a = ExitStack()
    xs = [st_p1a.enter_context(nc.sbuf_tensor(f"xs{i}", [128, DT * SUB], F32)) for i in range(2)]
    x_loaded, x_loaded2 = {}, {}

    ph = Phase(B)
    for dst, src in [(c_wn, d_wn), (c_bdt, d_bdt), (c_b1[0], d_b1[0]), (c_b1[1], d_b1[1]),
                     (c_sb1, d_sb1), (c_b2[0], d_b2[0]), (c_b2[1], d_b2[1]), (c_sb2, d_sb2),
                     (c_g0, d_g0), (c_g1, d_g1)]:
        ph.dma("sync", dst[:, :], src.ap(), "CL", gate=False)
    nconst = B.cnt["CL"]
    ph.op("gpsimd", lambda e: e.memset(ones_col[:, :], 1.0))
    ph.op("gpsimd", lambda e: e.memset(ones_colf[:, :], 1.0))
    ph.op("gpsimd", lambda e: e.memset(ones_rowf[:, :], 1.0))
    ph.op("gpsimd", lambda e: e.memset(ones_rowb[:, :], 1.0))
    ph.op("gpsimd", lambda e: e.memset(c_eps[:, :], EPS))
    ph.op("gpsimd", lambda e: e.memset(c_eps2[:, :], EPS * EPS))
    ph.op("gpsimd", lambda e: e.memset(carry[:, :], 0.0))
    for i in (0, 1):
        c0, b0 = divmod(i, n_b)
        for h, (heng, hq) in enumerate([("scalar", f"XL{i % 2}"), ("sync", f"XS{i % 2}")]):
            srcx = bass.AP(d_x, b0 * K + c0 * SUB + h * 4 * 128 * n_b * K,
                           [[n_b * K, 128], [128 * n_b * K, 4], [1, SUB]])
            dstx = bass.AP(xs[i % 2], h * 4 * SUB, [[DT * SUB, 128], [SUB, 4], [1, SUB]])
            (x_loaded if h == 0 else x_loaded2)[i] = ph.dma(heng, dstx, srcx, hq)
    ph.wait("vector", "CL", nconst)
    ph.op("vector", lambda e: e.tensor_scalar_mul(c_nbdt[:, :], c_bdt[:, :], -1.0))
    v1 = ph.op("vector", lambda e: e.tensor_sub(c_dwf[:, :], c_g0[:, :], c_g1[:, :]), inc="vE")
    ph.wait("scalar", "vE", v1)
    s0 = ph.op("scalar", lambda e: e.copy(c_dwhi[:, :], c_dwf[:, :]), inc="sE")
    ph.wait("scalar", "sE", s0)
    s1 = ph.op("scalar", lambda e: e.copy(c_dwhi32[:, :], c_dwhi[:, :]), inc="sE")
    ph.wait("vector", "sE", s1)
    v2 = ph.op("vector", lambda e: e.tensor_sub(c_dwlo[:, :], c_dwf[:, :], c_dwhi32[:, :]), inc="vE")
    ph.wait("vector", "vE", v2)
    v3 = ph.op("vector", lambda e: e.reduce_sum(c_dwsum[:, :], c_dwf[:, :], axis=mybir.AxisListType.X), inc="vE")
    ph.wait("scalar", "vE", v3)
    ph.op("scalar", lambda e: e.copy(c_dwsumb[:, :], c_dwsum[:, :]), inc="sE")
    ph.run(skip_drain={"XL0", "XL1", "XS0", "XS1"})

    with nc.psum_tensor("p_sdw", [1, 1], F32) as p_sdw:
        ph = Phase(B)
        t1 = ph.op("tensor", lambda e: e.matmul(p_sdw[:, :], ones_col[:, :], c_dwsumb[:, :], start=True, stop=True), inc="tE")
        ph.wait("scalar", "tE", t1)
        ph.op("scalar", lambda e: e.copy(r_sdw[:, :], p_sdw[:, :]), inc="sE")
        ph.run()

    with st_p1a as st:
        wu = st.enter_context(nc.sbuf_tensor("wu", [128, DT * INNER], BF16))
        wz = st.enter_context(nc.sbuf_tensor("wz", [128, DT * INNER], BF16))
        xbf = st.enter_context(nc.sbuf_tensor("xbf", [128, DT * SUB], BF16))
        x2bf = st.enter_context(nc.sbuf_tensor("x2bf", [128, DT * SUB], BF16))
        ln1 = st.enter_context(nc.sbuf_tensor("ln1", [128, DT * SUB], F32))
        hb = [st.enter_context(nc.sbuf_tensor(f"hb{i}", [128, DT * SUB], BF16)) for i in range(2)]
        ub = st.enter_context(nc.sbuf_tensor("ub", [128, IT * SUB], BF16))
        zb = st.enter_context(nc.sbuf_tensor("zb", [128, 8 * SUB], BF16))
        xlnb = st.enter_context(nc.sbuf_tensor("xlnb", [128, DT * SUB], F32))
        rows = {nm: st.enter_context(nc.sbuf_tensor(f"ra_{nm}", [1, SUB], F32))
                for nm in ["m", "msq", "t0", "v", "sd1", "sd12"]}
        for nm in ["r1", "r12"]:
            rows[nm] = st.enter_context(nc.sbuf_tensor(f"ra_{nm}", [1, SUB], F32))
        for nm in ["r1b", "mr1", "rr"]:
            rows[nm] = st.enter_context(nc.sbuf_tensor(f"ra_{nm}", [1, SUB], BF16))
        pstat = st.enter_context(nc.psum_tensor("pstat", [128, SUB], F32))
        pbc = [st.enter_context(nc.psum_tensor(f"pbc{i}", [128, SUB], F32)) for i in range(3)]
        pA = [st.enter_context(nc.psum_tensor(f"pA{i}", [128, SUB], F32)) for i in range(4)]

        ph = Phase(B)
        qw_base = B.cnt.get("qW", 0)

        cast_done, sum_done, sumsq_done = {}, {}, {}
        rows_done, bc_done, h_done, xln_done, xln_stored = {}, {}, {}, {}, {}
        evac_cnt, grp_cnt, bank_hist = {}, {}, {}
        ustoreA, ustoreB, zstoreA, zstoreB = {}, {}, {}, {}

        def x_src(i):
            c, b = divmod(i, n_b)
            return bass.AP(d_x, b * K + c * SUB, [[n_b * K, 128], [128 * n_b * K, DT], [1, SUB]])

        def slab_dst(buf, nt=DT):
            return bass.AP(buf, 0, [[nt * SUB, 128], [SUB, nt], [1, SUB]])

        wq_cnt = {}

        def em_weights():
            n = 0
            for wt, dr in [(wu, d_wu), (wz, d_wz)]:
                for hh in range(2):
                    srcap = bass.AP(dr, hh * 4 * 128 * INNER, [[INNER, 128], [128 * INNER, 4], [1, INNER]])
                    dstap = bass.AP(wt, hh * 4 * INNER, [[DT * INNER, 128], [INNER, 4], [1, INNER]])
                    wq_cnt[n] = ph.dma("gpsimd", dstap, srcap, "qW0")
                    n += 1

        def cache_chunks(pairs):
            out = []
            for nm, dsrc, ddst, r, cN in pairs:
                nch = max(1, (r * cN * 4) // (2 << 20))
                rch = r // nch
                for hh in range(nch):
                    sap = bass.AP(dsrc, hh * rch * cN, [[cN, rch], [1, cN]])
                    dap = bass.AP(ddst, hh * rch * cN, [[cN, rch], [1, cN]])
                    out.append((nm, dap, sap))
            return out

        chunks = cache_chunks([
            ("wdt", d_wdt, d_wdt_b, INNER, INNER),
            ("wout", d_wout, d_wout_b, INNER, D),
            ("w1_0", d_w1[0], d_w1_b[0], D, HID),
            ("w2_0", d_w2[0], d_w2_b[0], HID, D)])
        late_cache.extend(cache_chunks([
            ("w1_1", d_w1[1], d_w1_b[1], D, HID),
            ("w2_1", d_w2[1], d_w2_b[1], HID, D),
            ("sw1", d_sw1, d_sw1_b, D, HID),
            ("sw2", d_sw2, d_sw2_b, HID, D)]))

        def em_cache_some(k, gate_cnt=None):
            if gate_cnt is not None and chunks:
                ph.wait("gpsimd", "tE", gate_cnt)
            for _ in range(k):
                if not chunks:
                    return
                nm, dap, sap = chunks.pop(0)
                cache_ready[nm] = ph.dma("gpsimd", dap, sap, "qWC")

        def em_xload(i, eng="scalar"):
            if i >= NSUB:
                return
            ph.wait(eng, "vE", xln_done[i - 2])
            x_loaded[i] = ph.dma(eng, slab_dst(xs[i % 2]), x_src(i), f"XL{i % 2}")

        def em_casts(i):
            if i >= NSUB:
                return
            ph.wait("vector", f"XL{i % 2}", x_loaded[i])
            if i < 2:
                ph.wait("vector", f"XS{i % 2}", x_loaded2[i])
            if i >= 1:
                ph.wait("vector", "tE", sumsq_done[i - 1])
                ph.wait("vector", "tE", sum_done[i - 1])
            ph.op("vector", lambda e, x=xs[i % 2]: e.tensor_mul(x2bf[:, :], x[:, :], x[:, :]))
            cast_done[i] = ph.op("vector", lambda e, x=xs[i % 2]: e.tensor_copy(xbf[:, :], x[:, :]), inc="vE")

        def em_stats(i):
            ph.wait("tensor", "vE", cast_done[i])
            if i >= 1:
                ph.wait("tensor", "vE", rows_done[i - 1])
            for t in range(DT):
                sum_done[i] = ph.op("tensor", lambda e, t=t: e.matmul(
                    pstat[0:1, :], ones_col[:, :], xbf[:, t * SUB:(t + 1) * SUB],
                    start=(t == 0), stop=(t == DT - 1)), inc="tE" if t == DT - 1 else None)
            for t in range(DT):
                sumsq_done[i] = ph.op("tensor", lambda e, t=t: e.matmul(
                    pstat[32:33, :], ones_col[:, :], x2bf[:, t * SUB:(t + 1) * SUB],
                    start=(t == 0), stop=(t == DT - 1)), inc="tE" if t == DT - 1 else None)

        def em_rows_a(i):
            ph.wait("vector", "tE", sumsq_done[i])
            ph.op("vector", lambda e: e.tensor_single_scalar(rows["m"][:, :], pstat[0:1, :], 1.0 / D, ALU.mult))
            c1 = ph.op("vector", lambda e: e.tensor_single_scalar(rows["msq"][:, :], pstat[32:33, :], 1.0 / D, ALU.mult), inc="vE")
            ph.wait("vector", "vE", c1)
            c2 = ph.op("vector", lambda e: e.tensor_mul(rows["t0"][:, :], rows["m"][:, :], rows["m"][:, :]), inc="vE")
            ph.wait("vector", "vE", c2)
            rows_done[("v", i)] = ph.op("vector", lambda e: e.tensor_sub(rows["v"][:, :], rows["msq"][:, :], rows["t0"][:, :]), inc="vE")

        def em_rows_b(i):
            ph.wait("scalar", "vE", rows_done[("v", i)])
            ph.op("scalar", lambda e: e.activation(rows["sd1"][:, :], rows["v"][:, :], AF.Ln, bias=c_eps[:, :], scale=1.0))
            l2 = ph.op("scalar", lambda e: e.activation(rows["sd12"][:, :], rows["v"][:, :], AF.Ln, bias=c_eps2[:, :], scale=1.0 + EPS), inc="sE")
            ph.wait("scalar", "sE", l2)
            ph.op("scalar", lambda e: e.activation(rows["r1"][:, :], rows["sd1"][:, :], AF.Exp, scale=-0.5))
            rows_done[("sq", i)] = ph.op("scalar", lambda e: e.activation(rows["r12"][:, :], rows["sd12"][:, :], AF.Exp, scale=-0.5), inc="sE")

        def em_rows_c(i):
            ph.wait("vector", "sE", rows_done[("sq", i)])
            ph.op("vector", lambda e: e.tensor_copy(rows["r1b"][:, :], rows["r1"][:, :]))
            ph.op("vector", lambda e: e.tensor_copy(rows["mr1"][:, :], rows["m"][:, :]))
            rows_done[i] = ph.op("vector", lambda e: e.tensor_copy(rows["rr"][:, :], rows["r12"][:, :]), inc="vE")

        def em_bc(i):
            ph.wait("tensor", "vE", rows_done[i])
            if i >= 1:
                ph.wait("tensor", "vE", h_done[i - 1])
            ph.op("tensor", lambda e: e.matmul(pbc[0][:, :], ones_rowb[:, :], rows["mr1"][:, :], start=True, stop=True))
            ph.op("tensor", lambda e: e.matmul(pbc[1][:, :], ones_rowb[:, :], rows["r1b"][:, :], start=True, stop=True))
            bc_done[i] = ph.op("tensor", lambda e: e.matmul(pbc[2][:, :], ones_rowb[:, :], rows["rr"][:, :], start=True, stop=True), inc="tE")

        def em_lnh_h(i):
            xcur, hcur = xs[i % 2], hb[i % 2]
            ph.wait("vector", "tE", bc_done[i])
            p1 = None
            for t in range(DT):
                sl = slice(t * SUB, (t + 1) * SUB)
                p1 = ph.op("vector", lambda e, x=xcur, sl=sl: e.tensor_sub(ln1[:, sl], x[:, sl], pbc[0][:, :]),
                           inc="vE" if t == DT - 1 else None)
            ph.wait("vector", "vE", p1)
            if i >= 2:
                ph.wait("vector", "tE", grp_cnt[(i - 2, "z", IT - 1)])
            for t in range(DT):
                sl = slice(t * SUB, (t + 1) * SUB)
                h_done[i] = ph.op("vector", lambda e, h=hcur, sl=sl, t=t: e.scalar_tensor_tensor(
                    h[:, sl], ln1[:, sl], c_wn[:, t:t + 1], pbc[2][:, :], ALU.mult, ALU.mult),
                    inc="vE" if t == DT - 1 else None)

        def em_lnh_rest(i):
            xcur = xs[i % 2]
            ph.wait("vector", "vE", h_done[i])
            p2 = None
            for t in range(DT):
                sl = slice(t * SUB, (t + 1) * SUB)
                p2 = ph.op("vector", lambda e, sl=sl: e.tensor_mul(ln1[:, sl], ln1[:, sl], pbc[1][:, :]),
                           inc="vE" if t == DT - 1 else None)
            ph.wait("vector", "vE", p2)
            if i >= 1:
                ph.wait("vector", "XLNS", xln_stored[i - 1])
            for t in range(DT):
                sl = slice(t * SUB, (t + 1) * SUB)
                xln_done[i] = ph.op("vector", lambda e, x=xcur, sl=sl: e.tensor_add(xlnb[:, sl], x[:, sl], ln1[:, sl]),
                                    inc="vE" if t == DT - 1 else None)

        def em_uvz(i, tag, jlist):
            wt = wu if tag == "u" else wz
            dstbuf = ub if tag == "u" else zb
            hcur = hb[i % 2]
            ph.wait("tensor", "vE", h_done[i])
            for j in jlist:
                bank = (0 if tag == "u" else IT) + j
                bank = bank % 4
                pg = pA[bank]
                prev = bank_hist.get(bank)
                if prev is not None:
                    ph.wait("tensor", "sE", evac_cnt[prev])
                g = None
                for k in range(DT):
                    if i == 0:
                        n = (0 if tag == "u" else 2) + k // 4
                        ph.wait("tensor", "qW0", wq_cnt[n])
                    g = ph.op("tensor", lambda e, wt=wt, j=j, k=k, pg=pg, h=hcur: e.matmul(
                        pg[:, :], wt[:, k * INNER + j * 128: k * INNER + j * 128 + 128],
                        h[:, k * SUB:(k + 1) * SUB], start=(k == 0), stop=(k == DT - 1)),
                        inc="tE" if k == DT - 1 else None)
                grp_cnt[(i, tag, j)] = g
                ph.wait("scalar", "tE", g)
                if tag == "u":
                    if j == 0 and i >= 1:
                        ph.wait("scalar", "USA", ustoreA[i - 1])
                    if j == 8 and i >= 1:
                        ph.wait("scalar", "USB", ustoreB[i - 1])
                    col = j * SUB
                else:
                    if j == 0 and i >= 1:
                        ph.wait("scalar", "ZSB", zstoreB[i - 1])
                    if j == 8:
                        ph.wait("scalar", "ZSA", zstoreA[i])
                    col = (j % 8) * SUB
                evac_cnt[(i, tag, j)] = ph.op("scalar", lambda e, d=dstbuf, col=col, pg=pg: e.activation(
                    d[:, col:col + SUB], pg[:, :], AF.Silu), inc="sE")
                bank_hist[bank] = (i, tag, j)
                if tag == "z" and j == 7:
                    ph.wait("scalar", "sE", evac_cnt[(i, "z", 7)])
                    apz = bass.AP(d_zsp, i * 128 * IT * SUB, [[IT * SUB, 128], [1, 8 * SUB]])
                    zstoreA[i] = ph.dma("scalar", apz, zb[:, :], "ZSA")

        def em_store_uz(i, tag, half):
            if tag == "z" and half == "A":
                return
            jend = 7 if half == "A" else 15
            ph.wait("sync", "sE", evac_cnt[(i, tag, jend)])
            cofs = 0 if half == "A" else 8 * SUB
            if tag == "u":
                srcb = ub[:, cofs:cofs + 8 * SUB]
                ap = bass.AP(d_usp, i * 128 * IT * SUB + cofs, [[IT * SUB, 128], [1, 8 * SUB]])
            else:
                srcb = zb[:, :]
                ap = bass.AP(d_zsp, i * 128 * IT * SUB + cofs, [[IT * SUB, 128], [1, 8 * SUB]])
            cntv = ph.dma("sync", ap, srcb, f"{tag.upper()}S{half}")
            {("u", "A"): ustoreA, ("u", "B"): ustoreB,
             ("z", "B"): zstoreB}[(tag, half)][i] = cntv

        em_weights()
        em_casts(0)
        em_stats(0)
        em_rows_a(0); em_rows_b(0); em_rows_c(0)
        em_bc(0)
        em_lnh_h(0)
        em_casts(1)
        em_lnh_rest(0)
        ph.wait("sync", "vE", xln_done[0])
        xln_stored[0] = ph.dma("sync", d_xlnsp.ap()[0, :, :], xlnb[:, :], "XLNS")

        for w in range(NSUB):
            em_xload(w + 2)
            nxt = w + 1 < NSUB
            if nxt and w > 0:
                em_stats(w + 1)
                em_rows_a(w + 1)
            em_uvz(w, "u", range(0, 4))
            if nxt and w == 0:
                em_stats(1)
                em_rows_a(1)
            if nxt:
                em_rows_b(w + 1)
            em_uvz(w, "u", range(4, 8))
            em_store_uz(w, "u", "A")
            if nxt:
                em_rows_c(w + 1)
            em_casts(w + 2)
            em_uvz(w, "u", range(8, 12))
            em_uvz(w, "u", range(12, IT))
            if nxt:
                em_bc(w + 1)
            em_store_uz(w, "u", "B")
            if nxt:
                em_lnh_h(w + 1)
                em_lnh_rest(w + 1)
            em_uvz(w, "z", range(0, 8))
            em_store_uz(w, "z", "A")
            em_uvz(w, "z", range(8, IT))
            em_store_uz(w, "z", "B")
            if nxt:
                ph.wait("sync", "vE", xln_done[w + 1])
                xln_stored[w + 1] = ph.dma("sync", d_xlnsp.ap()[w + 1, :, :], xlnb[:, :], "XLNS")
            if w >= 2:
                em_cache_some(2, gate_cnt=grp_cnt[(w, "u", 7)])

        em_cache_some(len(chunks))
        ph.run()

    wout_ctx = nc.sbuf_tensor("wout", [128, IT * D], BF16)
    wout = wout_ctx.__enter__()
    wout_cnt = {}
    szin0_ctx = nc.sbuf_tensor("szin0", [128, IT * SUB], BF16)
    szin0 = szin0_ctx.__enter__()
    p1c_pre = {}
    with ExitStack() as st:
        wdt = st.enter_context(nc.sbuf_tensor("wdt", [128, IT * INNER], BF16))
        uin = [st.enter_context(nc.sbuf_tensor(f"uin{i}", [128, IT * SUB], BF16)) for i in range(2)]
        zin = [st.enter_context(nc.sbuf_tensor(f"zin{i}", [128, IT * SUB], BF16)) for i in range(2)]
        szb = st.enter_context(nc.sbuf_tensor("szb", [128, 8 * SUB], BF16))
        lamt = [st.enter_context(nc.sbuf_tensor(f"lamt{i}", [128, SUB], F32)) for i in range(2)]
        lampt = [st.enter_context(nc.sbuf_tensor(f"lampt{i}", [128, SUB], F32)) for i in range(2)]
        bbt = [st.enter_context(nc.sbuf_tensor(f"bbt{i}", [128, SUB], F32)) for i in range(2)]
        sst = [st.enter_context(nc.sbuf_tensor(f"sst{i}", [128, SUB], F32)) for i in range(2)]
        pA = [st.enter_context(nc.psum_tensor(f"pBA{i}", [128, SUB], F32)) for i in range(8)]

        ph = Phase(B)
        wq_cnt = {}
        ph.wait("gpsimd", "qWC", cache_ready["wdt"])
        for g in range(8):
            srcw = bass.AP(d_wdt_b, g * 2 * 128 * INNER, [[INNER, 128], [128 * INNER, 2], [1, INNER]])
            dstw = bass.AP(wdt, g * 2 * INNER, [[IT * INNER, 128], [INNER, 2], [1, INNER]])
            wq_cnt[g] = ph.dma("gpsimd", dstw, srcw, f"qW0{'b' * (g % 2)}")
        ph.wait("gpsimd", "qWC", cache_ready["wout"])
        for hh in range(2):
            srcap = bass.AP(d_wout_b, hh * 8 * 128 * D, [[D, 128], [128 * D, 8], [1, D]])
            dstap = bass.AP(wout, hh * 8 * D, [[IT * D, 128], [D, 8], [1, D]])
            wout_cnt[hh] = ph.dma("gpsimd", dstap, srcap, f"qW{4 + hh}")

        u_loaded, z_loaded, grp, evac = {}, {}, {}, {}
        b_done, scan_done, sz_done, szA_stored, szB_stored = {}, {}, {}, {}, {}

        half = IT * SUB // 2
        u_loaded[0] = ph.dma("sync", uin[0][:, 0:half], d_usp.ap()[0, :, 0:half], "UL0")
        u_half2 = ph.dma("scalar", uin[0][:, half:IT * SUB], d_usp.ap()[0, :, half:IT * SUB], "ULS")
        z_loaded[0] = ph.dma("sync", zin[0][:, :], d_zsp.ap()[0, :, :], "ZL0")
        if NSUB > 1:
            u_loaded[1] = ph.dma("sync", uin[1][:, :], d_usp.ap()[1, :, :], "UL1")
            z_loaded[1] = ph.dma("sync", zin[1][:, :], d_zsp.ap()[1, :, :], "ZL1")

        def em_lam_tail(i, j, pg, ucur, zcur, b):
            gj = i * IT + j
            g = grp[(i, j)]

            lj = j % 2
            ph.wait("scalar", "tE", g)
            if gj >= 2:
                ph.wait("scalar", "vE", scan_done[gj - 2])
                ph.wait("scalar", "pE", b_done[gj - 2])
            ph.op("scalar", lambda e, pg=pg, j=j, lj=lj: e.activation(
                lamt[lj][:, :], pg[:, :], AF.Sigmoid, bias=c_bdt[:, j:j + 1], scale=1.0))
            ev = ph.op("scalar", lambda e, pg=pg, j=j, lj=lj: e.activation(
                lampt[lj][:, :], pg[:, :], AF.Sigmoid, bias=c_nbdt[:, j:j + 1], scale=-1.0), inc="sE")
            evac[(i, j)] = ev

            ph.wait("gpsimd", "sE", ev)
            if gj >= 2:
                ph.wait("gpsimd", "vE", scan_done[gj - 2])
            b_done[gj] = ph.op("gpsimd", lambda e, lj=lj, u=ucur, j=j: e.tensor_tensor(
                bbt[lj][:, :], lampt[lj][:, :], u[:, j * SUB:(j + 1) * SUB], ALU.mult), inc="pE")

            ccol = b * IT + j
            ph.wait("vector", "pE", b_done[gj])
            if gj >= 2:
                ph.wait("vector", "vE", sz_done[gj - 2])
            sc = ph.op("vector", lambda e, lj=lj, ccol=ccol: e.tensor_tensor_scan(
                sst[lj][:, :], lamt[lj][:, :], bbt[lj][:, :], carry[:, ccol:ccol + 1],
                ALU.mult, ALU.add), inc="vE")
            scan_done[gj] = sc
            ph.wait("vector", "vE", sc)
            ph.op("vector", lambda e, lj=lj, ccol=ccol: e.tensor_copy(
                carry[:, ccol:ccol + 1], sst[lj][:, SUB - 1:SUB]), inc="vE")
            if j == 0 and i >= 1:
                ph.wait("vector", "SZSB", szB_stored[i - 1])
            if j == 8:
                ph.wait("vector", "SZSA", szA_stored[i])
            sz_done[gj] = ph.op("vector", lambda e, lj=lj, z=zcur, j=j: e.tensor_mul(
                szb[:, (j % 8) * SUB:(j % 8 + 1) * SUB], sst[lj][:, :], z[:, j * SUB:(j + 1) * SUB]), inc="vE")
            if j == 7:
                ph.wait("sync", "vE", sz_done[gj])
                apz = bass.AP(d_szsp, i * 128 * IT * SUB, [[IT * SUB, 128], [1, 8 * SUB]])
                szA_stored[i] = ph.dma("sync", apz, szb[:, :], "SZSA")

        for i in range(NSUB):
            c, b = divmod(i, n_b)
            ucur, zcur = uin[i % 2], zin[i % 2]
            ph.wait("tensor", f"UL{i % 2}", u_loaded[i])
            ph.wait("gpsimd", f"UL{i % 2}", u_loaded[i])
            if i == 0:
                ph.wait("tensor", "ULS", u_half2)
                ph.wait("gpsimd", "ULS", u_half2)
            ph.wait("vector", f"ZL{i % 2}", z_loaded[i])
            if i == 0:
                for g8 in range(8):
                    ph.wait("tensor", f"qW0{'b' * (g8 % 2)}", wq_cnt[g8])
                    for k in (2 * g8, 2 * g8 + 1):
                        for j in range(8):
                            gg = ph.op("tensor", lambda e, j=j, k=k, u=ucur: e.matmul(
                                pA[j][:, :], wdt[:, k * INNER + j * 128: k * INNER + j * 128 + 128],
                                u[:, k * SUB:(k + 1) * SUB], start=(k == 0), stop=(k == IT - 1)),
                                inc="tE" if k == IT - 1 else None)
                            if k == IT - 1:
                                grp[(0, j)] = gg
                for j in range(8):
                    em_lam_tail(0, j, pA[j], ucur, zcur, b)
                jstart = 8
            else:
                jstart = 0
            for j in range(jstart, IT):
                gj = i * IT + j
                if i == 0:
                    pg = pA[j % 8]
                    key = (0, j - 8)
                else:
                    pg = pA[j % 2]
                    key = (i, j - 2) if j >= 2 else (i - 1, IT - 2 + j)
                if key in evac:
                    ph.wait("tensor", "sE", evac[key])
                g = None
                for k in range(IT):
                    g = ph.op("tensor", lambda e, j=j, k=k, pg=pg, u=ucur: e.matmul(
                        pg[:, :], wdt[:, k * INNER + j * 128: k * INNER + j * 128 + 128],
                        u[:, k * SUB:(k + 1) * SUB], start=(k == 0), stop=(k == IT - 1)),
                        inc="tE" if k == IT - 1 else None)
                grp[(i, j)] = g
                em_lam_tail(i, j, pg, ucur, zcur, b)

            ph.wait("sync", "vE", sz_done[i * IT + IT - 1])
            apz = bass.AP(d_szsp, i * 128 * IT * SUB + 8 * SUB, [[IT * SUB, 128], [1, 8 * SUB]])
            szB_stored[i] = ph.dma("sync", apz, szb[:, :], "SZSB")
            if i + 2 < NSUB:
                ph.wait("sync", "pE", b_done[i * IT + IT - 1])
                u_loaded[i + 2] = ph.dma("sync", uin[i % 2][:, :], d_usp.ap()[i + 2, :, :], f"UL{i % 2}")
                z_loaded[i + 2] = ph.dma("sync", zin[i % 2][:, :], d_zsp.ap()[i + 2, :, :], f"ZL{i % 2}")

        ph.wait("sync", "SZSB", szB_stored[0])
        p1c_pre["sz"] = ph.dma("sync", szin0[:, :], d_szsp.ap()[0, :, :], "SZL0")
        ph.run()

    with ExitStack() as st:
        szin = [szin0, st.enter_context(nc.sbuf_tensor("szin1", [128, IT * SUB], BF16))]
        xlnin = [st.enter_context(nc.sbuf_tensor(f"xlnin{i}", [128, DT * SUB], F32)) for i in range(2)]
        xmb = [st.enter_context(nc.sbuf_tensor(f"xmb{i}", [128, DT * SUB], F32)) for i in range(3)]
        xmbf = st.enter_context(nc.sbuf_tensor("xmbf", [128, DT * SUB], BF16))
        xm2bf = st.enter_context(nc.sbuf_tensor("xm2bf", [128, DT * SUB], BF16))
        xmlo = st.enter_context(nc.sbuf_tensor("xmlo", [128, DT * SUB], BF16))
        xfb = st.enter_context(nc.sbuf_tensor("xfb", [128, DT * SUB], BF16))
        t1b = st.enter_context(nc.sbuf_tensor("t1b", [128, DT * SUB], BF16))
        rows = {nm: st.enter_context(nc.sbuf_tensor(f"rc_{nm}", [1, SUB], F32))
                for nm in ["m", "msq", "t0", "v"]}
        rows["sd1"] = rows["t0"]
        rows2 = []
        for p in range(2):
            d = {nm: st.enter_context(nc.sbuf_tensor(f"rc2_{nm}{p}", [1, SUB], BF16))
                 for nm in ["r2b", "mr2", "mrow"]}
            rows2.append(d)
        for nm in ["pg", "r2", "thr"]:
            rows[nm] = st.enter_context(nc.sbuf_tensor(f"rc_{nm}", [1, SUB], F32))
        pstat = st.enter_context(nc.psum_tensor("pstatc", [128, SUB], F32))
        pgate = st.enter_context(nc.psum_tensor("pgate", [1, SUB], F32))
        pbc = [st.enter_context(nc.psum_tensor(f"pbcc{i}", [128, SUB], F32)) for i in range(4)]
        pA = [st.enter_context(nc.psum_tensor(f"pCA{i}", [128, SUB], F32)) for i in range(2)]

        ph = Phase(B)
        wq_cnt = wout_cnt

        sz_loaded, xln_loaded, grp, ev = {}, {}, {}, {}
        stats_done, gate_done, lo_done, rows_done = {}, {}, {}, {}
        bc_done, xf_done, mrow_done, pg_read, cast_done = {}, {}, {}, {}, {}
        xm_stored, xf_stored, m_stored, cast_tile = {}, {}, {}, {}

        def em_loads(i):
            if i >= NSUB:
                return
            if i >= 2:
                ph.wait("sync", "tE", grp[(i - 2, DT - 1)])
            sz_loaded[i] = ph.dma("sync", szin[i % 2][:, :], d_szsp.ap()[i, :, :], f"SZL{i % 2}")
            if i >= 2:
                ph.wait("sync", "vE", ev[(i - 2, DT - 1)])
            xln_loaded[i] = ph.dma("sync", xlnin[i % 2][:, :], d_xlnsp.ap()[i, :, :], f"XLL{i % 2}")

        def em_y(i, olist):
            szc, xlnc, xmc = szin[i % 2], xlnin[i % 2], xmb[i % 3]
            if olist[0] == 0:
                ph.wait("tensor", f"SZL{i % 2}", sz_loaded[i])
            for o in olist:
                pg = pA[o % 2]
                key = (i, o - 2) if o >= 2 else (i - 1, DT - 2 + o)
                if key in ev:
                    ph.wait("tensor", "vE", ev[key])
                g = None
                for k in range(IT):
                    if i == 0 and o == 0:
                        ph.wait("tensor", f"qW{4 + k // 8}", wq_cnt[k // 8])
                    g = ph.op("tensor", lambda e, o=o, k=k, pg=pg, s=szc: e.matmul(
                        pg[:, :], wout[:, k * D + o * 128: k * D + o * 128 + 128],
                        s[:, k * SUB:(k + 1) * SUB], start=(k == 0), stop=(k == IT - 1)),
                        inc="tE" if k == IT - 1 else None)
                grp[(i, o)] = g
                ph.wait("vector", "tE", g)
                if o == 0:
                    ph.wait("vector", f"XLL{i % 2}", xln_loaded[i])
                    if i >= 3:
                        ph.wait("vector", "XMS", xm_stored[i - 3])
                        ph.wait("vector", "sE", cast_done[i - 3])
                        ph.wait("vector", "vE", xf_done[i - 3])
                sl = slice(o * SUB, (o + 1) * SUB)
                ev[(i, o)] = ph.op("vector", lambda e, pg=pg, sl=sl, xl=xlnc, xm=xmc: e.tensor_add(
                    xm[:, sl], pg[:, :], xl[:, sl]), inc="vE")

        def em_casts(i):
            xmc = xmb[i % 3]
            for o in range(DT):
                sl = slice(o * SUB, (o + 1) * SUB)
                ph.wait("scalar", "vE", ev[(i, o)])
                if o == 0 and i >= 1:
                    ph.wait("scalar", "tE", gate_done[i - 1])
                ph.op("scalar", lambda e, sl=sl, xm=xmc: e.copy(xmbf[:, sl], xm[:, sl]))
                cd = ph.op("scalar", lambda e, sl=sl, xm=xmc: e.activation(xm2bf[:, sl], xm[:, sl], AF.Square), inc="sE")
                cast_tile[(i, o)] = cd
                ph.wait("gpsimd", "sE", cd)
                if o == 0 and i >= 1:
                    ph.wait("gpsimd", "tE", gate_done[i - 1])
                lo_done[i] = ph.op("gpsimd", lambda e, sl=sl, xm=xmc: e.tensor_sub(
                    xmlo[:, sl], xm[:, sl], xmbf[:, sl]), inc="pE")
            cast_done[i] = cast_tile[(i, DT - 1)]

        def em_statsgate(i):
            if i >= 1:
                ph.wait("tensor", "vE", rows_done[i - 1])
            for t in range(DT):
                ph.wait("tensor", "sE", cast_tile[(i, t)])
                ph.op("tensor", lambda e, t=t: e.matmul(
                    pstat[0:1, :], ones_col[:, :], xmbf[:, t * SUB:(t + 1) * SUB],
                    start=(t == 0), stop=(t == DT - 1)))
            for t in range(DT):
                stats_done[i] = ph.op("tensor", lambda e, t=t: e.matmul(
                    pstat[32:33, :], ones_col[:, :], xm2bf[:, t * SUB:(t + 1) * SUB],
                    start=(t == 0), stop=(t == DT - 1)), inc="tE" if t == DT - 1 else None)
            ph.wait("tensor", "pE", lo_done[i])
            if i >= 1:
                ph.wait("tensor", "sE", pg_read[i - 1])
            ngate = 0
            for lhs, rhs in [(c_dwhi, xmbf), (c_dwhi, xmlo), (c_dwlo, xmbf)]:
                for t in range(DT):
                    ngate += 1
                    gate_done[i] = ph.op("tensor", lambda e, lhs=lhs, rhs=rhs, t=t,
                                         stt=(ngate == 1), spp=(ngate == 24): e.matmul(
                        pgate[:, :], lhs[:, t:t + 1], rhs[:, t * SUB:(t + 1) * SUB],
                        start=stt, stop=spp), inc="tE" if ngate == 24 else None)

        def em_rows(i):
            rp = rows2[i % 2]
            ph.wait("vector", "tE", stats_done[i])
            ph.op("vector", lambda e: e.tensor_single_scalar(rows["m"][:, :], pstat[0:1, :], 1.0 / D, ALU.mult))
            c1 = ph.op("vector", lambda e: e.tensor_single_scalar(rows["msq"][:, :], pstat[32:33, :], 1.0 / D, ALU.mult), inc="vE")
            ph.wait("vector", "vE", c1)
            c2 = ph.op("vector", lambda e: e.tensor_mul(rows["t0"][:, :], rows["m"][:, :], rows["m"][:, :]), inc="vE")
            ph.wait("vector", "vE", c2)
            v_v = ph.op("vector", lambda e: e.tensor_sub(rows["v"][:, :], rows["msq"][:, :], rows["t0"][:, :]), inc="vE")
            ph.wait("scalar", "vE", v_v)
            s_l = ph.op("scalar", lambda e: e.activation(rows["sd1"][:, :], rows["v"][:, :], AF.Ln, bias=c_eps[:, :], scale=1.0), inc="sE")
            ph.wait("scalar", "sE", s_l)
            s_sq = ph.op("scalar", lambda e: e.activation(rows["r2"][:, :], rows["sd1"][:, :], AF.Exp, scale=-0.5), inc="sE")
            ph.wait("vector", "sE", s_sq)
            ph.op("vector", lambda e, rp=rp: e.tensor_copy(rp["r2b"][:, :], rows["r2"][:, :]))
            ph.op("vector", lambda e, rp=rp: e.tensor_mul(rp["mr2"][:, :], rows["m"][:, :], rows["r2"][:, :]))
            thr_d = ph.op("vector", lambda e: e.tensor_scalar(rows["thr"][:, :], rows["m"][:, :], r_sdw[:, :], None, ALU.mult), inc="vE")
            rows_done[i] = thr_d
            ph.wait("scalar", "tE", gate_done[i])
            if i >= 1:
                ph.wait("scalar", "vE", mrow_done[i - 1])
            pg_read[i] = ph.op("scalar", lambda e: e.copy(rows["pg"][:, :], pgate[:, :]), inc="sE")
            ph.wait("vector", "sE", pg_read[i])
            ph.wait("vector", "vE", thr_d)
            if i >= 2:
                ph.wait("vector", "MS", m_stored[i - 2])
            mrow_done[i] = ph.op("vector", lambda e, rp=rp: e.tensor_tensor(
                rp["mrow"][:, :], rows["pg"][:, :], rows["thr"][:, :], ALU.is_ge), inc="vE")

        def em_bc(i):
            rp = rows2[i % 2]
            pb = pbc[2 * (i % 2):2 * (i % 2) + 2]
            ph.wait("tensor", "vE", rows_done[i])
            if i >= 2:
                ph.wait("tensor", "vE", xf_done[i - 2])
            ph.op("tensor", lambda e, rp=rp, pb=pb: e.matmul(pb[0][:, :], ones_rowb[:, :], rp["r2b"][:, :], start=True, stop=True))
            bc_done[i] = ph.op("tensor", lambda e, rp=rp, pb=pb: e.matmul(pb[1][:, :], ones_rowb[:, :], rp["mr2"][:, :], start=True, stop=True), inc="tE")

        def em_xf_t1b(i):
            xmc = xmb[i % 3]
            pb = pbc[2 * (i % 2):2 * (i % 2) + 2]
            ph.wait("vector", "tE", bc_done[i])
            if i >= 1:
                ph.wait("vector", "vE", xf_done[i - 1])
            for t in range(DT):
                sl = slice(t * SUB, (t + 1) * SUB)
                xf_done[("t1b", i)] = ph.op("vector", lambda e, sl=sl, xm=xmc, pb=pb: e.tensor_mul(
                    t1b[:, sl], xm[:, sl], pb[0][:, :]), inc="vE" if t == DT - 1 else None)

        def em_xf_sub(i):
            pb = pbc[2 * (i % 2):2 * (i % 2) + 2]
            ph.wait("vector", "vE", xf_done[("t1b", i)])
            if i >= 1:
                ph.wait("vector", "XFS", xf_stored[i - 1])
            for t in range(DT):
                sl = slice(t * SUB, (t + 1) * SUB)
                xf_done[i] = ph.op("vector", lambda e, sl=sl, pb=pb: e.tensor_sub(xfb[:, sl], t1b[:, sl], pb[1][:, :]),
                                   inc="vE" if t == DT - 1 else None)

        def em_stores(i):
            rp = rows2[i % 2]
            xmc = xmb[i % 3]
            ph.wait("sync", "vE", xf_done[i])
            ph.wait("sync", "vE", mrow_done[i])
            xm_stored[i] = ph.dma("sync", d_xmsp.ap()[i, :, :], xmc[:, :], "XMS")
            xf_stored[i] = ph.dma("sync", d_xfsp.ap()[i, :, :], xfb[:, :], "XFS")
            m_stored[i] = ph.dma("sync", d_msp.ap()[i, :, :], rp["mrow"][:, :], "MS")

        sz_loaded[0] = p1c_pre["sz"]
        xln_loaded[0] = ph.dma("sync", xlnin[0][:, :], d_xlnsp.ap()[0, :, :], "XLL0")
        em_loads(1)
        for w in range(NSUB + 2):
            a, b, c = w, w - 1, w - 2
            if 0 <= c < NSUB:
                em_bc(c)
            if 0 <= b < NSUB:
                em_statsgate(b)
                em_rows(b)
            if a < NSUB:
                em_y(a, list(range(0, 4)))
            if 0 <= c < NSUB:
                em_xf_t1b(c)
            if a < NSUB:
                em_y(a, list(range(4, DT)))
            if 0 <= c < NSUB:
                em_xf_sub(c)
            if a < NSUB:
                em_casts(a)
            if 0 <= c < NSUB:
                em_stores(c)
            if a < NSUB:
                em_loads(a + 2)

        ph.run()

    szin0_ctx.__exit__(None, None, None)
    wout_ctx.__exit__(None, None, None)

    p2w = {}

    def ffn_branch(which, next_key=None, next_dW1=None):
        with ExitStack() as st:
            if "wA" not in p2w:
                p2w["wA"] = prog.enter_context(nc.sbuf_tensor("wA_p2", [128, DT * HID], BF16))
                p2w["wB"] = prog.enter_context(nc.sbuf_tensor("wB_p2", [128, HT * D], BF16))
                p2w["xf0"] = prog.enter_context(nc.sbuf_tensor("xf0_p2", [128, DT * SUB], BF16))
                p2w["mr0"] = prog.enter_context(nc.sbuf_tensor("mr0_p2", [1, SUB], BF16))
            wA, wB = p2w["wA"], p2w["wB"]
            hd = st.enter_context(nc.sbuf_tensor(f"hd_{which}", [128, HT * SUB], BF16))
            xfin = [p2w["xf0"],
                    st.enter_context(nc.sbuf_tensor(f"xfin_{which}1", [128, DT * SUB], BF16))]
            xr = st.enter_context(nc.sbuf_tensor(f"xr_{which}", [128, DT * SUB], BF16)) if which != "sh" else None
            outst = [st.enter_context(nc.sbuf_tensor(f"outst_{which}{i}", [128, SUB], F32)) for i in range(2)]
            accin = [st.enter_context(nc.sbuf_tensor(f"acc_{which}{i}", [128, SUB], F32)) for i in range(2)] if which != "e0" else None
            xmin = [st.enter_context(nc.sbuf_tensor(f"xmin_{which}{i}", [128, SUB], F32)) for i in range(2)] if which == "sh" else None
            axm = [st.enter_context(nc.sbuf_tensor(f"axm_{which}{i}", [128, SUB], F32)) for i in range(2)] if which == "sh" else None
            mr = [p2w["mr0"],
                  st.enter_context(nc.sbuf_tensor(f"mr_{which}1", [1, SUB], BF16))] if which != "sh" else None
            thrr = st.enter_context(nc.sbuf_tensor(f"thr_{which}", [1, SUB], BF16)) if which == "e1" else None
            pA = [st.enter_context(nc.psum_tensor(f"pm1_{which}{i}", [128, SUB], F32)) for i in range(2)]
            pB = [st.enter_context(nc.psum_tensor(f"pm2_{which}{i}", [128, SUB], F32)) for i in range(2)]
            pM = st.enter_context(nc.psum_tensor(f"pM_{which}", [128, SUB], F32)) if which != "sh" else None

            if which == "e0":
                dW1, dW2, bias1, bias2 = d_w1_b[0], d_w2_b[0], c_b1[0], c_b2[0]
                ck1, ck2 = "w1_0", "w2_0"
            elif which == "e1":
                dW1, dW2, bias1, bias2 = d_w1_b[1], d_w2_b[1], c_b1[1], c_b2[1]
                ck1, ck2 = "w1_1", "w2_1"
            else:
                dW1, dW2, bias1, bias2 = d_sw1_b, d_sw2_b, c_sb1, c_sb2
                ck1, ck2 = "sw1", "sw2"

            ph = Phase(B)
            wqb = {}
            while late_cache and late_cache[0][0] in (ck1, ck2):
                nm, dap, sap = late_cache.pop(0)
                cache_ready[nm] = ph.dma("gpsimd", dap, sap, "qWC")
            if "wa_next" in p2w:
                wqa = p2w.pop("wa_next")
                wqa_jmajor = False
            else:
                wqa_jmajor = True
                wqa = {}
                ph.wait("gpsimd", "qWC", cache_ready[ck1])
                for jg in range(8):
                    srcap = bass.AP(dW1, jg * 512, [[HID, 128], [128 * HID, DT], [1, 512]])
                    dstap = bass.AP(wA, jg * 512, [[DT * HID, 128], [HID, DT], [1, 512]])
                    wqa[jg] = ph.dma("gpsimd", dstap, srcap, "qWA")
            xf_loaded, xf_loaded2, m_loaded = {}, {}, {}
            if wqa_jmajor:
                ph.wait("scalar", "qWC", cache_ready[ck2])
                hh = DT * SUB // 2
                xf_loaded2[0] = ph.dma("scalar", xfin[0][:, hh:DT * SUB],
                                       d_xfsp.ap()[0, :, hh:DT * SUB], "PXFS")
            else:
                ph.wait("gpsimd", "qWC", cache_ready[ck2])
            for q in range(4):
                srcap = bass.AP(dW2, q * 8 * 128 * D, [[D, 128], [128 * D, 8], [1, D]])
                dstap = bass.AP(wB, q * 8 * D, [[HT * D, 128], [D, 8], [1, D]])
                if wqa_jmajor:
                    wqb[q] = ph.dma("scalar", dstap, srcap, f"qWB{q}", gate=False)
                else:
                    wqb[q] = ph.dma("gpsimd", dstap, srcap, f"qW{q}")

            mb_done, thr_done, xr_done, hd_ev, ax_done = {}, {}, {}, {}, {}
            m1_grp, m2_grp, out_done, out_stored = {}, {}, {}, {}

            def em_xfload(i):
                if i >= NSUB:
                    return
                if i == 0 and "xf0_cnt" in p2w:
                    xf_loaded[0] = p2w.pop("xf0_cnt")
                    if which != "sh":
                        m_loaded[0] = p2w.pop("m0_cnt")
                    return
                if i == 0 and wqa_jmajor:
                    m_loaded[0] = ph.dma("sync", mr[0][:, :], d_msp.ap()[0, :, :], "PM0")
                    hh = DT * SUB // 2
                    xf_loaded[0] = ph.dma("sync", xfin[0][:, 0:hh], d_xfsp.ap()[0, :, 0:hh], "PXF0")
                    return
                if i >= 2:
                    if which == "sh":
                        ph.wait("sync", "tE", m1_grp[(i - 2, HT - 1)])
                    else:
                        ph.wait("sync", "vE", xr_done[i - 2])
                xf_loaded[i] = ph.dma("sync", xfin[i % 2][:, :], d_xfsp.ap()[i, :, :], f"PXF{i % 2}")
                if which != "sh":
                    m_loaded[i] = ph.dma("sync", mr[i % 2][:, :], d_msp.ap()[i, :, :], f"PM{i % 2}")

            def em_mask(i):
                mrow = mr[i % 2]
                if which == "e1":
                    ph.wait("vector", f"PM{i % 2}", m_loaded[i])
                    if i >= 1:
                        ph.wait("vector", "tE", mb_done[i - 1])
                    thr_done[i] = ph.op("vector", lambda e, r=mrow: e.tensor_scalar(
                        thrr[:, :], r[:, :], -1.0, 1.0, ALU.mult, ALU.add), inc="vE")
                    brow = thrr
                    ph.wait("tensor", "vE", thr_done[i])
                else:
                    brow = mrow
                    ph.wait("tensor", f"PM{i % 2}", m_loaded[i])
                if i >= 1:
                    ph.wait("tensor", "vE", xr_done[i - 1])
                mb_done[i] = ph.op("tensor", lambda e, r=brow: e.matmul(
                    pM[:, :], ones_rowb[:, :], r[:, :], start=True, stop=True), inc="tE")
                ph.wait("vector", "tE", mb_done[i])
                ph.wait("vector", f"PXF{i % 2}", xf_loaded[i])
                if i in xf_loaded2:
                    ph.wait("vector", "PXFS", xf_loaded2[i])
                if i >= 1:
                    ph.wait("vector", "tE", m1_grp[(i - 1, HT - 1)])
                xfc = xfin[i % 2]
                for t in range(DT):
                    sl = slice(t * SUB, (t + 1) * SUB)
                    xr_done[i] = ph.op("vector", lambda e, x=xfc, sl=sl: e.tensor_mul(
                        xr[:, sl], x[:, sl], pM[:, :]), inc="vE" if t == DT - 1 else None)

            def em_m1(i):
                rhs = xfin[i % 2] if which == "sh" else xr
                if which == "sh":
                    ph.wait("tensor", f"PXF{i % 2}", xf_loaded[i])
                else:
                    ph.wait("tensor", "vE", xr_done[i])
                for j in range(HT):
                    pg = pA[j % 2]
                    if i == 0 and wqa_jmajor and j % 4 == 0:
                        ph.wait("tensor", "qWA", wqa[j // 4])
                    key = (i, j - 2) if j >= 2 else (i - 1, HT - 2 + j)
                    if key in hd_ev:
                        ph.wait("tensor", "sE", hd_ev[key])
                    g = None
                    for k in range(DT):
                        if i == 0 and j == 0 and not wqa_jmajor:
                            ph.wait("tensor", f"qW{k // 2}", wqa[k // 2])
                        g = ph.op("tensor", lambda e, j=j, k=k, pg=pg, r=rhs: e.matmul(
                            pg[:, :], wA[:, k * HID + j * 128: k * HID + j * 128 + 128],
                            r[:, k * SUB:(k + 1) * SUB], start=(k == 0), stop=(k == DT - 1)),
                            inc="tE" if k == DT - 1 else None)
                    m1_grp[(i, j)] = g
                    ph.wait("scalar", "tE", g)
                    if i >= 1 and j < 2:
                        ph.wait("scalar", "tE", m2_grp[(i - 1, DT - 1)])
                    hd_ev[(i, j)] = ph.op("scalar", lambda e, j=j, pg=pg, b1=bias1: e.activation(
                        hd[:, j * SUB:(j + 1) * SUB], pg[:, :], AF.Relu, bias=b1[:, j:j + 1]), inc="sE")

            def em_m2out(i):
                c, b = divmod(i, n_b)
                for o in range(DT):
                    pg = pB[o % 2]
                    oc = o % 2
                    go = i * DT + o
                    key = (i, o - 2) if o >= 2 else (i - 1, DT - 2 + o)
                    if key in out_done:
                        ph.wait("tensor", "vE", out_done[key])
                    g = None
                    for k in range(HT):
                        if i == 0 and o == 0:
                            ph.wait("tensor", f"qWB{k // 8}" if wqa_jmajor else f"qW{k // 8}", wqb[k // 8])
                        if o == 0 and k == 0:
                            ph.wait("tensor", "sE", hd_ev[(i, HT - 2)])
                        if o == 0 and k == HT - 1:
                            ph.wait("tensor", "sE", hd_ev[(i, HT - 1)])
                        g = ph.op("tensor", lambda e, o=o, k=k, pg=pg: e.matmul(
                            pg[:, :], wB[:, k * D + o * 128: k * D + o * 128 + 128],
                            hd[:, k * SUB:(k + 1) * SUB], start=(k == 0), stop=(k == HT - 1)),
                            inc="tE" if k == HT - 1 else None)
                    m2_grp[(i, o)] = g

                    la = lx = None
                    if which != "e0":
                        d_prev = d_acc0 if which == "e1" else d_acc1
                        la = ph.dma("sync", accin[oc][:, :], bass.AP(
                            d_prev, i * 128 * DT * SUB + o * SUB, [[DT * SUB, 128], [1, SUB]]), f"PAC{oc}")
                    if which == "sh":
                        if key in ax_done:
                            ph.wait("scalar", "pE", ax_done[key])
                        lx = ph.dma("scalar", xmin[oc][:, :], bass.AP(
                            d_xmsp, i * 128 * DT * SUB + o * SUB, [[DT * SUB, 128], [1, SUB]]), f"PXM{oc}")

                    ph.wait("vector", "tE", g)
                    if go >= 2:
                        ph.wait("vector", f"POS{oc}", out_stored[go - 2])
                    if which == "e0":
                        out_done[(i, o)] = ph.op("vector", lambda e, pg=pg, o=o, oc=oc, b2=bias2: e.tensor_scalar(
                            outst[oc][:, :], pg[:, :], b2[:, o:o + 1], None, ALU.add), inc="vE")
                    elif which == "e1":
                        ph.wait("vector", f"PAC{oc}", la)
                        out_done[(i, o)] = ph.op("vector", lambda e, pg=pg, o=o, oc=oc, b2=bias2: e.scalar_tensor_tensor(
                            outst[oc][:, :], pg[:, :], b2[:, o:o + 1], accin[oc][:, :], ALU.add, ALU.add), inc="vE")
                    else:
                        ph.wait("gpsimd", f"PAC{oc}", la)
                        ph.wait("gpsimd", f"PXM{oc}", lx)
                        if key in out_done:
                            ph.wait("gpsimd", "vE", out_done[key])
                        ax_done[(i, o)] = ph.op("gpsimd", lambda e, oc=oc: e.tensor_add(
                            axm[oc][:, :], accin[oc][:, :], xmin[oc][:, :]), inc="pE")
                        ph.wait("vector", "pE", ax_done[(i, o)])
                        out_done[(i, o)] = ph.op("vector", lambda e, pg=pg, o=o, oc=oc, b2=bias2: e.scalar_tensor_tensor(
                            outst[oc][:, :], pg[:, :], b2[:, o:o + 1], axm[oc][:, :], ALU.add, ALU.add), inc="vE")

                    ph.wait("sync", "vE", out_done[(i, o)])
                    if which == "e0":
                        dstap = bass.AP(d_acc0, i * 128 * DT * SUB + o * SUB, [[DT * SUB, 128], [1, SUB]])
                    elif which == "e1":
                        dstap = bass.AP(d_acc1, i * 128 * DT * SUB + o * SUB, [[DT * SUB, 128], [1, SUB]])
                    else:
                        dstap = bass.AP(d_out, (o * 128) * n_b * K + b * K + c * SUB, [[n_b * K, 128], [1, SUB]])
                    out_stored[go] = ph.dma("sync", dstap, outst[oc][:, :], f"POS{oc}")

            em_xfload(0)
            em_xfload(1)
            if which != "sh":
                em_mask(0)
            for w in range(NSUB):
                em_m1(w)
                if which != "sh" and late_cache:
                    ph.wait("gpsimd", "tE", m1_grp[(w, 0)])
                    for _ in range(2):
                        if late_cache:
                            nm, dap, sap = late_cache.pop(0)
                            cache_ready[nm] = ph.dma("gpsimd", dap, sap, "qWC")
                if which != "sh" and w + 1 < NSUB:
                    em_mask(w + 1)
                em_m2out(w)
                em_xfload(w + 2)

            if which != "sh":
                ph.wait("sync", "vE", xr_done[NSUB - 2])
                p2w["xf0_cnt"] = ph.dma("sync", xfin[0][:, :], d_xfsp.ap()[0, :, :], "PXF0")
                if which == "e0":
                    ph.wait("sync", "tE", mb_done[NSUB - 2])
                    p2w["m0_cnt"] = ph.dma("sync", mr[0][:, :], d_msp.ap()[0, :, :], "PM0")

            if next_dW1 is not None:
                ph.wait("gpsimd", "tE", m1_grp[(NSUB - 1, HT - 1)])
                ph.wait("gpsimd", "qWC", cache_ready[next_key])
                nxt = {}
                for q in range(4):
                    srcap = bass.AP(next_dW1, q * 2 * 128 * HID, [[HID, 128], [128 * HID, 2], [1, HID]])
                    dstap = bass.AP(wA, q * 2 * HID, [[DT * HID, 128], [HID, 2], [1, HID]])
                    nxt[q] = ph.dma("gpsimd", dstap, srcap, f"qW{q}")
                p2w["wa_next"] = nxt
            for s in ["POS0", "POS1"]:
                if s in B.cnt:
                    ph.wait("sync", s, B.cnt[s])
            ph.run()

    ffn_branch("e0", next_key="w1_1", next_dW1=d_w1_b[1])
    ffn_branch("e1", next_key="sw1", next_dW1=d_sw1_b)
    ffn_branch("sh")

    ph = Phase(B)
    for s in ["POS0", "POS1", "qW"]:
        if s in B.cnt:
            ph.wait("sync", s, B.cnt[s])
    ph.run()

    prog.close()
    return nc


def _cols(v, nt):
    return np.ascontiguousarray(np.asarray(v, dtype=np.float32).reshape(nt, 128).T)


def kernel(**inp):
    x = np.asarray(inp["L0_attn_x"], dtype=np.float32)
    W_in = np.asarray(inp["L0_attn_W_in"], dtype=np.float32)
    W_dt = np.asarray(inp["L0_attn_W_dt"], dtype=np.float32)
    W_out = np.asarray(inp["L0_attn_W_out"], dtype=np.float32)
    gate_w = np.asarray(inp["L0_ffn_gate_w"], dtype=np.float32)
    w1 = np.asarray(inp["L0_ffn_w1"], dtype=np.float32)
    w2 = np.asarray(inp["L0_ffn_w2"], dtype=np.float32)
    b1 = np.asarray(inp["L0_ffn_b1"], dtype=np.float32)
    b2 = np.asarray(inp["L0_ffn_b2"], dtype=np.float32)

    shared = {
        "w_in_uT": np.ascontiguousarray(W_in[:INNER].T),
        "w_in_zT": np.ascontiguousarray(W_in[INNER:].T),
        "w_dtT": np.ascontiguousarray(W_dt.T),
        "w_outT": np.ascontiguousarray(W_out.T),
        "w1_0": np.ascontiguousarray(w1[0]),
        "w1_1": np.ascontiguousarray(w1[1]),
        "w2_0": np.ascontiguousarray(w2[0]),
        "w2_1": np.ascontiguousarray(w2[1]),
        "sw1": np.ascontiguousarray(np.asarray(inp["L0_ffn_sw1"], dtype=np.float32)),
        "sw2": np.ascontiguousarray(np.asarray(inp["L0_ffn_sw2"], dtype=np.float32)),
        "wn_c": _cols(inp["L0_attn_w_norm"], DT),
        "bdt_c": _cols(inp["L0_attn_b_dt"], IT),
        "b1_0c": _cols(b1[0], HT),
        "b1_1c": _cols(b1[1], HT),
        "sb1_c": _cols(inp["L0_ffn_sb1"], HT),
        "b2_0c": _cols(b2[0], DT),
        "b2_1c": _cols(b2[1], DT),
        "sb2_c": _cols(inp["L0_ffn_sb2"], DT),
        "g0_c": _cols(gate_w[0], DT),
        "g1_c": _cols(gate_w[1], DT),
    }

    nc = build_nc()
    in_maps = []
    for c in range(8):
        m = dict(shared)
        m["xT"] = np.ascontiguousarray(x[4 * c:4 * c + 4].transpose(2, 0, 1))
        in_maps.append(m)
    res = run_bass_kernel_spmd(nc, in_maps, core_ids=list(range(8)))
    out = np.empty((32, 2048, 1024), np.float32)
    for c in range(8):
        out[4 * c:4 * c + 4] = np.asarray(res.results[c]["outT"]).transpose(1, 2, 0)
    return out
